# version 01
# baseline (speedup 1.0000x reference)
import numpy as np
from contextlib import ExitStack

import concourse.bass as bass
import concourse.mybir as mybir
from concourse.bass_utils import run_bass_kernel_spmd

F32 = mybir.dt.float32
BF16 = mybir.dt.bfloat16
I32 = mybir.dt.int32
OP = mybir.AluOpType
AF = mybir.ActivationFunctionType
AX = mybir.AxisListType.X

NCORES = 8
B_PER = 2
L = 2048
DM = 1024
T = B_PER * L
TT = 512
NTT = T // TT
KC = DM // 128
H = 16
D = 64
C = 64
CS = 32
NCH = L // C
I3 = 3072
MC3 = I3 // 128
QMAX = 32767.0
NEG = -1.0e30
PHASES = 8
P2_STEPS = None
P2_SKIP = set()
CC_GROUP = None


class P:
    pass


def build(phases=PHASES, debug_taps=()):
    nc = bass.Bass("TRN2", target_bir_lowering=False)
    p = P()
    p.nc = nc
    p.phases = phases
    dt = nc.dram_tensor

    p.x = dt("x", [T, DM], F32, kind="ExternalInput").ap()
    p.Wq = dt("Wq", [DM, DM], F32, kind="ExternalInput").ap()
    p.Wk = dt("Wk", [DM, DM], F32, kind="ExternalInput").ap()
    p.Wv = dt("Wv", [DM, DM], F32, kind="ExternalInput").ap()
    p.bq = dt("bq", [DM], F32, kind="ExternalInput").ap()
    p.bk = dt("bk", [DM], F32, kind="ExternalInput").ap()
    p.bv = dt("bv", [DM], F32, kind="ExternalInput").ap()
    p.Wg1 = dt("Wg1", [DM, 16], F32, kind="ExternalInput").ap()
    p.bg1 = dt("bg1", [16], F32, kind="ExternalInput").ap()
    p.Wg2 = dt("Wg2", [16, DM], F32, kind="ExternalInput").ap()
    p.bg2 = dt("bg2", [DM], F32, kind="ExternalInput").ap()
    p.wgate = dt("wgate", [I3, DM], F32, kind="ExternalInput").ap()
    p.wup = dt("wup", [I3, DM], F32, kind="ExternalInput").ap()
    p.wdown = dt("wdown", [DM, I3], F32, kind="ExternalInput").ap()
    p.ident = dt("ident", [128, 128], F32, kind="ExternalInput").ap()
    p.ones = dt("ones", [128, 128], F32, kind="ExternalInput").ap()
    p.cmask = dt("cmask", [128, 64], F32, kind="ExternalInput").ap()
    p.tmask = dt("tmask", [128, 64], F32, kind="ExternalInput").ap()

    p.out = dt("out", [T, DM], F32, kind="ExternalOutput").ap()

    p.xf = dt("xf_scr", [KC, 128, T], F32).ap()
    p.qf = dt("qf_scr", [KC, 128, T], F32).ap()
    p.kf = dt("kf_scr", [KC, 128, T], F32).ap()
    p.af = dt("af_scr", [KC, 128, T], F32).ap()
    p.vt = dt("vt_scr", [T, DM], F32).ap()
    p.x1 = dt("x1_scr", [KC, 128, T], F32).ap()
    p.upraw = dt("upraw_scr", [MC3, 128, T], F32).ap()
    p.gateraw = dt("gateraw_scr", [MC3, 128, T], F32).ap()
    p.upq = dt("upq_scr", [MC3, 128, T], F32).ap()
    p.g2raw = dt("g2raw_scr", [MC3, 128, T], F32).ap()
    p.oraw = dt("oraw_scr", [MC3, 128, T], F32).ap()
    p.wupT_hi = dt("wupT_hi", [KC, 128, I3], BF16).ap()
    p.wupT_lo = dt("wupT_lo", [KC, 128, I3], BF16).ap()
    p.wgT_hi = dt("wgT_hi", [KC, 128, I3], BF16).ap()
    p.wgT_lo = dt("wgT_lo", [KC, 128, I3], BF16).ap()
    p.wdT_hi = dt("wdT_hi", [MC3, 128, DM], BF16).ap()
    p.wdT_lo = dt("wdT_lo", [MC3, 128, DM], BF16).ap()
    p.cc_in = dt("cc_in", [2, 128], F32)
    p.cc_out = dt("cc_out", [2, 128], F32)

    p.debug_taps = {}
    for name, shape in debug_taps:
        p.debug_taps[name] = dt("dbg_" + name, list(shape), F32,
                                kind="ExternalOutput").ap()

    with ExitStack() as ctx:
        p.semctx = ctx
        sbt = lambda nm, sh, d=F32: ctx.enter_context(nc.sbuf_tensor(nm, sh, d))
        p.ident_sb = sbt("ident_sb", [128, 128])
        p.ones_sb = sbt("ones_sb", [128, 128])
        p.cmask_sb = sbt("cmask_sb", [128, 64])
        p.tmask_sb = sbt("tmask_sb", [128, 64])
        p.bq_sb = sbt("bq_sb", [128, KC])
        p.bk_sb = sbt("bk_sb", [128, KC])
        p.bg1_sb = sbt("bg1_sb", [16, 1])
        p.bg2_sb = sbt("bg2_sb", [128, KC])
        p.bvrow = sbt("bvrow", [128, DM])
        p.maxacc = sbt("maxacc", [128, 16])
        p.sig_sb = sbt("sig_sb", [128, 16])
        p.row_sb = sbt("row_sb", [128, 8])
        p.ccrow = sbt("ccrow", [2, 128])
        p.eps_sb = sbt("eps_sb", [128, 1])

        emit_setup(p)
        if phases >= 1:
            emit_p0(p)
        if phases >= 2:
            emit_p1(p)
        if phases >= 3:
            emit_p2(p)
        if phases >= 4:
            emit_p3(p)
        if phases >= 5:
            emit_p4(p)
        if phases >= 6:
            emit_p5(p)
        if phases >= 7:
            emit_p6(p)
        if phases >= 8:
            emit_p7(p)
        emit_debug(p)
    return nc, p


def emit_setup(p):
    nc = p.nc
    with nc.semaphore("su_ld") as ld, nc.semaphore("su_ms") as ms, nc.Block() as blk:
        @blk.sync
        def _(sync):
            n = 0
            def d(out, in_):
                nonlocal n
                with nc.allow_non_contiguous_dma(reason="tiny one-time bias loads"):
                    sync.dma_start(out=out, in_=in_).then_inc(ld, 16)
                n += 1
            d(p.ident_sb[:], p.ident)
            d(p.ones_sb[:], p.ones)
            d(p.cmask_sb[:], p.cmask)
            d(p.tmask_sb[:], p.tmask)
            d(p.bq_sb[:], p.bq.rearrange("(c q) -> q c", q=128))
            d(p.bk_sb[:], p.bk.rearrange("(c q) -> q c", q=128))
            d(p.bg2_sb[:], p.bg2.rearrange("(c q) -> q c", q=128))
            d(p.bg1_sb[:], p.bg1.rearrange("g -> g ()"))
            sync.wait_ge(ms, 1)
            d(p.bvrow[0:1, :], p.bv.rearrange("f -> () f"))
            sync.wait_ge(ld, n * 16)

        @blk.vector
        def _(vector):
            vector.memset(p.bvrow[:], 0.0)
            vector.memset(p.maxacc[:], 0.0)
            vector.memset(p.row_sb[:], 0.0)
            vector.memset(p.ccrow[:], 0.0)
            vector.memset(p.eps_sb[:], 1e-5)
            vector.drain()
            vector.nop().then_inc(ms, 1)


def emit_p0(p):
    nc = p.nc
    with ExitStack() as pc:
        sb = lambda nm, sh, d=F32: pc.enter_context(nc.sbuf_tensor(nm, sh, d))
        xin = [sb(f"p0_xin{i}", [128, 4, DM]) for i in range(2)]
        xfo = [sb(f"p0_xfo{i}", [128, KC, TT]) for i in range(2)]
        ps = [pc.enter_context(nc.psum_tensor(f"p0_ps{i}", [128, 128], F32))
              for i in range(4)]
        slist = []
        def sem(nm):
            h = nc.alloc_semaphore(name=nm)
            slist.append(h)
            return h
        ld = [sem(f"p0_ld{i}") for i in range(2)]
        tr = sem("p0_tr")
        ev = sem("p0_ev")
        st = [sem(f"p0_st{i}") for i in range(2)]
        NTR = 32
        with nc.Block() as blk:
            @blk.sync
            def _(sync):
                def store(j):
                    sync.wait_ge(ev, (j + 1) * NTR)
                    sync.dma_start(
                        out=p.xf[:, :, j * TT:(j + 1) * TT].rearrange("c q t -> q c t"),
                        in_=xfo[j % 2][:]).then_inc(st[j % 2], 16)
                for i in range(NTT):
                    if i >= 2:
                        sync.wait_ge(tr, (i - 1) * NTR)
                    src = p.x[i * TT:(i + 1) * TT, :].rearrange(
                        "(s q) f -> q s f", q=128)
                    sync.dma_start(out=xin[i % 2][:], in_=src).then_inc(ld[i % 2], 16)
                    if i >= 1:
                        store(i - 1)
                store(NTT - 1)
                sync.wait_ge(st[0], NTT // 2 * 16)
                sync.wait_ge(st[1], NTT // 2 * 16)

            @blk.tensor
            def _(tensor):
                n = 0
                for i in range(NTT):
                    tensor.wait_ge(ld[i % 2], (i // 2 + 1) * 16)
                    for s in range(4):
                        for c in range(KC):
                            if n >= 4:
                                tensor.wait_ge(ev, n - 3)
                            tensor.transpose(
                                ps[n % 4][:], xin[i % 2][:, s, c * 128:(c + 1) * 128],
                                p.ident_sb[:]).then_inc(tr, 1)
                            n += 1

            @blk.vector
            def _(vector):
                n = 0
                for i in range(NTT):
                    if i >= 2:
                        vector.wait_ge(st[i % 2], (i // 2) * 16)
                    for s in range(4):
                        for c in range(KC):
                            vector.wait_ge(tr, n + 1)
                            vector.tensor_copy(
                                xfo[i % 2][:, c, s * 128:(s + 1) * 128],
                                ps[n % 4][:]).then_inc(ev, 1)
                            n += 1
        nc.clear_and_free_semaphores(slist)

def emit_p1(p):
    nc = p.nc
    with ExitStack() as pc:
        sb = lambda nm, sh, d=F32: pc.enter_context(nc.sbuf_tensor(nm, sh, d))
        wq_hi = sb("wq_hi", [128, KC, DM], BF16); wq_lo = sb("wq_lo", [128, KC, DM], BF16)
        wk_hi = sb("wk_hi", [128, KC, DM], BF16); wk_lo = sb("wk_lo", [128, KC, DM], BF16)
        wv_hi = sb("wv_hi", [128, KC, DM], BF16); wv_lo = sb("wv_lo", [128, KC, DM], BF16)
        wg1_hi = sb("wg1_hi", [128, KC, 16], BF16); wg1_lo = sb("wg1_lo", [128, KC, 16], BF16)
        wg2_sb = sb("wg2_sb", [16, DM])
        wtmp = [sb(f"p1_wtmp{i}", [128, DM]) for i in range(2)]
        xf_t = [sb(f"p1_xf{i}", [128, KC, TT]) for i in range(2)]
        xh = [sb(f"p1_xh{i}", [128, KC, TT], BF16) for i in range(2)]
        xl = [sb(f"p1_xl{i}", [128, KC, TT], BF16) for i in range(2)]
        qst = [sb(f"p1_qst{i}", [128, TT]) for i in range(2)]
        kst = [sb(f"p1_kst{i}", [128, TT]) for i in range(2)]
        vst = [sb(f"p1_vst{i}", [128, TT]) for i in range(2)]
        ast = [sb(f"p1_ast{i}", [128, TT]) for i in range(2)]
        g1b = sb("p1_g1b", [16, TT]); sg1 = sb("p1_sg1", [16, TT])
        g1s = [sb(f"p1_g1s{i}", [16, TT]) for i in range(2)]
        asig = sb("p1_asig", [128, TT])
        psq = pc.enter_context(nc.psum_tensor("p1_psq", [128, TT], F32))
        psk = pc.enter_context(nc.psum_tensor("p1_psk", [128, TT], F32))
        psv = pc.enter_context(nc.psum_tensor("p1_psv", [128, TT], F32))
        psg1 = pc.enter_context(nc.psum_tensor("p1_psg1", [16, TT], F32))
        psg2 = pc.enter_context(nc.psum_tensor("p1_psg2", [128, TT], F32))
        slist = []
        def sem(nm):
            h = nc.alloc_semaphore(name=nm)
            slist.append(h)
            return h
        wld = [sem(f"p1_wld{i}") for i in range(2)]
        wsp = sem("p1_wsp")
        xld = [sem(f"p1_xld{i}") for i in range(2)]
        xsp = sem("p1_xsp")
        qmm = sem("p1_qmm"); kmm = sem("p1_kmm"); vmm = sem("p1_vmm")
        g1mm = sem("p1_g1mm"); g2mm = sem("p1_g2mm")
        qev = sem("p1_qev"); kev = sem("p1_kev"); vev = sem("p1_vev")
        aev = sem("p1_aev")
        sc1 = sem("p1_sc1")
        g1sv = sem("p1_g1sv")
        scs = sem("p1_scs")
        qsd = [sem(f"p1_qsd{i}") for i in range(2)]
        ksd = [sem(f"p1_ksd{i}") for i in range(2)]
        vsd = [sem(f"p1_vsd{i}") for i in range(2)]
        asd = [sem(f"p1_asd{i}") for i in range(2)]

        NW = 3 * KC + 1
        with nc.Block() as blk:
            @blk.sync
            def _(sync):
                n = 0
                for (wsrc, _, _) in ((p.Wq, wq_hi, wq_lo), (p.Wk, wk_hi, wk_lo),
                                     (p.Wv, wv_hi, wv_lo)):
                    for kc in range(KC):
                        if n >= 2:
                            sync.wait_ge(wsp, n - 1)
                        sync.dma_start(out=wtmp[n % 2][:],
                                       in_=wsrc[kc * 128:(kc + 1) * 128, :]
                                       ).then_inc(wld[n % 2], 16)
                        n += 1
                if n >= 2:
                    sync.wait_ge(wsp, n - 1)
                with nc.allow_non_contiguous_dma(reason="tiny Wg1 one-time"):
                    sync.dma_start(out=wtmp[n % 2][:, 0:16 * KC].rearrange(
                                       "q (c g) -> q c g", g=16),
                                   in_=p.Wg1.rearrange("(c q) g -> q c g", q=128)
                                   ).then_inc(wld[n % 2], 16)
                n += 1
                sync.wait_ge(wsp, n - 1)
                sync.dma_start(out=wg2_sb[:], in_=p.Wg2).then_inc(wld[n % 2], 16)
                n += 1
                nq = nk = nv = na = 0
                def stores_for(j):
                    nonlocal nq, nk, nv, na
                    for mc in range(KC):
                        sync.wait_ge(qev, nq + 1)
                        sync.dma_start(out=p.qf[mc, :, j * TT:(j + 1) * TT],
                                       in_=qst[nq % 2][:]).then_inc(qsd[nq % 2], 16)
                        nq += 1
                        sync.wait_ge(kev, nk + 1)
                        sync.dma_start(out=p.kf[mc, :, j * TT:(j + 1) * TT],
                                       in_=kst[nk % 2][:]).then_inc(ksd[nk % 2], 16)
                        nk += 1
                        sync.wait_ge(vev, nv + 1)
                        tsub, half = mc % 4, mc // 4
                        r0 = j * TT + tsub * 128
                        sync.dma_start(out=p.vt[r0:r0 + 128,
                                                half * TT:(half + 1) * TT],
                                       in_=vst[nv % 2][:]).then_inc(vsd[nv % 2], 16)
                        nv += 1
                        sync.wait_ge(aev, na + 1)
                        sync.dma_start(out=p.af[mc, :, j * TT:(j + 1) * TT],
                                       in_=ast[na % 2][:]).then_inc(asd[na % 2], 16)
                        na += 1
                for i in range(NTT):
                    if i >= 1:
                        sync.wait_ge(xsp, i)
                    sync.dma_start(
                        out=xf_t[i % 2][:],
                        in_=p.xf[:, :, i * TT:(i + 1) * TT].rearrange("c q t -> q c t")
                        ).then_inc(xld[i % 2], 16)
                    if i >= 1:
                        stores_for(i - 1)
                stores_for(NTT - 1)
                for s2 in (qsd, ksd, vsd, asd):
                    sync.wait_ge(s2[0], NTT * KC // 2 * 16)
                    sync.wait_ge(s2[1], NTT * KC // 2 * 16)

            @blk.vector
            def _(vector):
                n = 0
                for (_, whi, wlo) in ((p.Wq, wq_hi, wq_lo), (p.Wk, wk_hi, wk_lo),
                                      (p.Wv, wv_hi, wv_lo)):
                    for kc in range(KC):
                        vector.wait_ge(wld[n % 2], (n // 2 + 1) * 16)
                        vector.tensor_copy(whi[:, kc, :], wtmp[n % 2][:])
                        vector.drain()
                        vector.tensor_sub(wlo[:, kc, :], wtmp[n % 2][:], whi[:, kc, :])
                        vector.drain()
                        vector.nop().then_inc(wsp, 1)
                        n += 1
                vector.wait_ge(wld[n % 2], (n // 2 + 1) * 16)
                wg1view = wtmp[n % 2][:, 0:16 * KC].rearrange("q (c g) -> q c g", g=16)
                vector.tensor_copy(wg1_hi[:], wg1view)
                vector.drain()
                vector.tensor_sub(wg1_lo[:], wg1view, wg1_hi[:])
                vector.drain()
                vector.nop().then_inc(wsp, 1)
                n += 1
                nq = nk = nv = na = 0
                for i in range(NTT):
                    vector.wait_ge(xld[i % 2], (i // 2 + 1) * 16)
                    if i >= 2:
                        vector.wait_ge(g2mm, (i - 1) * KC)
                    vector.tensor_copy(xh[i % 2][:], xf_t[i % 2][:])
                    vector.drain()
                    vector.tensor_sub(xl[i % 2][:], xf_t[i % 2][:], xh[i % 2][:])
                    vector.drain()
                    vector.nop().then_inc(xsp, 1)
                    vector.wait_ge(sc1, 2 * (i + 1))
                    vector.tensor_mul(g1s[i % 2][:], g1b[:], sg1[:])
                    vector.drain()
                    vector.nop().then_inc(g1sv, 1)
                    for mc in range(KC):
                        vector.wait_ge(qmm, nq + 1)
                        if nq >= 2:
                            vector.wait_ge(qsd[nq % 2], (nq // 2) * 16)
                        vector.tensor_scalar_add(qst[nq % 2][:], psq[:],
                                                 p.bq_sb[:, mc:mc + 1])
                        vector.drain()
                        vector.nop().then_inc(qev, 1)
                        nq += 1
                        vector.wait_ge(kmm, nk + 1)
                        if nk >= 2:
                            vector.wait_ge(ksd[nk % 2], (nk // 2) * 16)
                        vector.tensor_scalar_add(kst[nk % 2][:], psk[:],
                                                 p.bk_sb[:, mc:mc + 1])
                        vector.drain()
                        vector.nop().then_inc(kev, 1)
                        nk += 1
                        vector.wait_ge(vmm, nv + 1)
                        if nv >= 2:
                            vector.wait_ge(vsd[nv % 2], (nv // 2) * 16)
                        vector.tensor_copy(vst[nv % 2][:], psv[:])
                        vector.drain()
                        vector.nop().then_inc(vev, 1)
                        nv += 1
                        vector.wait_ge(scs, na + 1)
                        if na >= 2:
                            vector.wait_ge(asd[na % 2], (na // 2) * 16)
                        vector.tensor_scalar(ast[na % 2][:], asig[:], 0.9, 0.1,
                                             op0=OP.mult, op1=OP.add)
                        vector.drain()
                        vector.nop().then_inc(aev, 1)
                        na += 1

            @blk.scalar
            def _(scalar):
                for i in range(NTT):
                    scalar.wait_ge(g1mm, i + 1)
                    if i >= 1:
                        scalar.wait_ge(g1sv, i)
                    scalar.activation(g1b[:], psg1[:], AF.Identity,
                                      bias=p.bg1_sb[:, 0:1], scale=1.0).then_inc(sc1, 1)
                    scalar.drain()
                    scalar.activation(sg1[:], psg1[:], AF.Sigmoid,
                                      bias=p.bg1_sb[:, 0:1], scale=1.0).then_inc(sc1, 1)
                    scalar.drain()
                    for mc in range(KC):
                        nag = i * KC + mc
                        scalar.wait_ge(g2mm, nag + 1)
                        if nag >= 1:
                            scalar.wait_ge(aev, nag)
                        scalar.activation(asig[:], psg2[:], AF.Sigmoid,
                                          bias=p.bg2_sb[:, mc:mc + 1], scale=1.0).then_inc(scs, 1)
                        scalar.drain()

            @blk.tensor
            def _(tensor):
                tensor.wait_ge(wsp, NW)
                tensor.wait_ge(wld[1], 13 * 16)
                nq = nk = nv = na = 0
                for i in range(NTT):
                    bi = i % 2
                    tensor.wait_ge(xsp, i + 1)
                    if i >= 1:
                        tensor.wait_ge(sc1, 2 * i)
                    for n3, (xs, wk_) in enumerate(
                            ((xh, 0), (xl, 0), (xh, 1))):
                        w = (wg1_hi, wg1_lo)[wk_]
                        for kc in range(KC):
                            ins = tensor.matmul(psg1[:], w[:, kc, :],
                                                xs[bi][:, kc, :],
                                                start=(n3 == 0 and kc == 0),
                                                stop=(n3 == 2 and kc == KC - 1))
                    ins.then_inc(g1mm, 1)
                    for mc in range(KC):
                        tensor.wait_ge(qev, nq)
                        k3 = 0
                        for (xs, wpc) in ((xh, wq_hi), (xl, wq_hi), (xh, wq_lo)):
                            for kc in range(KC):
                                ins = tensor.matmul(
                                    psq[:], wpc[:, kc, mc * 128:(mc + 1) * 128],
                                    xs[bi][:, kc, :],
                                    start=(k3 == 0), stop=(k3 == 3 * KC - 1))
                                k3 += 1
                        ins.then_inc(qmm, 1)
                        nq += 1
                        tensor.wait_ge(kev, nk)
                        k3 = 0
                        for (xs, wpc) in ((xh, wk_hi), (xl, wk_hi), (xh, wk_lo)):
                            for kc in range(KC):
                                ins = tensor.matmul(
                                    psk[:], wpc[:, kc, mc * 128:(mc + 1) * 128],
                                    xs[bi][:, kc, :],
                                    start=(k3 == 0), stop=(k3 == 3 * KC - 1))
                                k3 += 1
                        ins.then_inc(kmm, 1)
                        nk += 1
                        tensor.wait_ge(vev, nv)
                        tsub, half = mc % 4, mc // 4
                        tensor.matmul(psv[:], p.ones_sb[:],
                                      p.bvrow[:, half * TT:(half + 1) * TT],
                                      start=True, stop=False)
                        k3 = 0
                        for (xs, wpc) in ((xh, wv_hi), (xl, wv_hi), (xh, wv_lo)):
                            for kc in range(KC):
                                ins = tensor.matmul(
                                    psv[:], xs[bi][:, kc, tsub * 128:(tsub + 1) * 128],
                                    wpc[:, kc, half * TT:(half + 1) * TT],
                                    start=False, stop=(k3 == 3 * KC - 1))
                                k3 += 1
                        ins.then_inc(vmm, 1)
                        nv += 1
                        tensor.wait_ge(g1sv, i + 1)
                        tensor.wait_ge(scs, na)
                        ins = tensor.matmul(psg2[:], wg2_sb[:, mc * 128:(mc + 1) * 128],
                                            g1s[bi][:], start=True, stop=True)
                        ins.then_inc(g2mm, 1)
                        na += 1
        nc.clear_and_free_semaphores(slist)

def emit_p2(p):
    nc = p.nc
    NST = 8
    NG = NTT * NST
    if P2_STEPS is not None:
        NG = P2_STEPS
    NTRS = 24
    with ExitStack() as pc:
        sb = lambda nm, sh, d=F32: pc.enter_context(nc.sbuf_tensor(nm, sh, d))
        qt = [sb(f"p2_qt{i}", [128, KC, TT]) for i in range(2)]
        kt = [sb(f"p2_kt{i}", [128, KC, TT]) for i in range(2)]
        at0 = sb("p2_at", [128, KC, TT])
        at = [at0, at0]
        xft = sb("p2_xft", [128, KC, TT])
        xr = sb("p2_xr", [128, KC, TT])
        x1t = sb("p2_x1t", [128, KC, TT])
        sq = x1t
        vch = [sb(f"p2_vch{i}", [128, DM]) for i in range(2)]
        vs1 = [sb(f"p2_vs1{i}", [32, DM]) for i in range(2)]
        S_all = sb("p2_S", [128, KC, 64])
        LY = sb("p2_LY", [128, KC, 2, 33])
        rc = sb("p2_rc", [128, KC, 2, 32])
        Qt_ = sb("p2_Qt", [128, KC, 2, 32])
        QtS = sb("p2_QtS", [128, KC, 2, 32])
        KtBD = sb("p2_KtBD", [128, KC, 128])
        K2BD = sb("p2_K2BD", [128, KC, 128])
        ToffSB = sb("p2_ToffSB", [128, KC, 32])
        K2_ = sb("p2_K2", [128, KC, 2, 32])
        ct01 = sb("p2_ct01", [128, KC])
        scb = sb("p2_scb", [128, KC, 64])
        mx = sb("p2_mx", [128, KC])
        dsb = sb("p2_dsb", [128, KC, 64])
        psb = sb("p2_psb", [128, KC, 64])
        sm = sb("p2_sm", [128, KC])
        rs = sb("p2_rs", [128, KC])
        pn = sb("p2_pn", [128, KC, 64])
        pT = sb("p2_pT", [64, KC, 128])
        K2T = sb("p2_K2T", [32, KC, 2, 128])
        TtSB = sb("p2_TtSB", [128, KC, 2, 32])
        att = sb("p2_att", [128, KC, 64])
        m_sb = pn[:].rearrange("q c e -> q (c e)")
        msq = dsb[:].rearrange("q c e -> q (c e)")
        v_sb = scb[:].rearrange("q c e -> q (c e)")
        sd_sb = att[:].rearrange("q c e -> q (c e)")
        rstd = sd_sb
        B = [pc.enter_context(nc.psum_tensor(f"p2_B{i}", [128, TT], F32))
             for i in range(8)]
        B0v = B[0][:].rearrange("q (c e) -> q c e", e=64)
        Ttv = B[0][:].rearrange("q (c s2 t) -> q c s2 t", s2=2, t=32)
        B1v = B[1][:].rearrange("q (c e) -> q c e", e=64)
        B2v = B[2][:].rearrange("q (c e) -> q c e", e=64)
        B3v = B[3][:].rearrange("q (c e) -> q c e", e=64)
        Dv = B[4][:].rearrange("q (c e) -> q c e", e=64)
        B5tv = B[5][:, 0:KC * 32].rearrange("q (c t) -> q c t", t=32)
        TR = [B[6], B[7]]

        slist = []
        def sem(nm):
            h = nc.alloc_semaphore(name=nm)
            slist.append(h)
            return h
        qld = [sem(f"p2_qld{i}") for i in range(2)]
        xfl = sem("p2_xfl")
        vld = [sem(f"p2_vld{i}") for i in range(2)]
        x1sd = [sem(f"p2_x1sd{i}") for i in range(2)]
        V1 = sem("p2_V1"); V2 = sem("p2_V2"); V2b = sem("p2_V2b")
        V3 = sem("p2_V3"); V4 = sem("p2_V4"); V5 = sem("p2_V5")
        V6 = sem("p2_V6"); V7 = sem("p2_V7")
        S1e = sem("p2_S1e")
        T1 = sem("p2_T1"); T3 = sem("p2_T3"); T4 = sem("p2_T4"); T5 = sem("p2_T5")
        ATr = sem("p2_ATr")
        trsem = sem("p2_trsem"); trev = sem("p2_trev")
        LSq = sem("p2_LSq"); LT1 = sem("p2_LT1"); LT2 = sem("p2_LT2")
        LNx = sem("p2_LNx"); LNe = sem("p2_LNe"); LNv = sem("p2_LNv")
        LNs = sem("p2_LNs")

        def v6req(G):
            return G + 1 + (1 if G >= NG // 2 else 0)

        with nc.Block() as blk:
            @blk.sync
            def _(sync):
                for G in range(NG):
                    g, j = G // NST, G % NST
                    if j == 0:
                        if g >= 2:
                            sync.wait_ge(T3, (g - 1) * NST)
                            sync.wait_ge(V1, (g - 1) * NST)
                        base = g * TT
                        for (dst, src) in ((qt, p.qf), (kt, p.kf)):
                            sync.dma_start(
                                out=dst[g % 2][:],
                                in_=src[:, :, base:base + TT].rearrange(
                                    "c q t -> q c t")).then_inc(qld[g % 2], 16)
                        if g >= 1:
                            sync.wait_ge(V1, g * NST)
                        sync.dma_start(
                            out=at[g % 2][:],
                            in_=p.af[:, :, base:base + TT].rearrange(
                                "c q t -> q c t")).then_inc(qld[g % 2], 16)
                        sync.wait_ge(V7, g * NST)
                        sync.dma_start(
                            out=xft[:],
                            in_=p.xf[:, :, base:base + TT].rearrange("c q t -> q c t")
                            ).then_inc(xfl, 16)
                    if G >= 2:
                        sync.wait_ge(T4, G - 1)
                    r0 = g * TT + j * 64
                    sync.dma_start(out=vch[G % 2][0:64, :],
                                   in_=p.vt[r0:r0 + 64, :]).then_inc(vld[G % 2], 16)
                    sync.dma_start(out=vch[G % 2][64:128, :],
                                   in_=p.vt[r0:r0 + 64, :]).then_inc(vld[G % 2], 16)
                    sync.dma_start(out=vs1[G % 2][0:32, :],
                                   in_=p.vt[r0 + 32:r0 + 64, :]).then_inc(vld[G % 2], 16)
                    if j == NST - 1 and g >= 1:
                        gs = g - 1
                        sync.wait_ge(LNx, gs + 1)
                        sync.dma_start(
                            out=p.x1[:, :, gs * TT:(gs + 1) * TT].rearrange(
                                "c q t -> q c t"),
                            in_=x1t[:]).then_inc(x1sd[gs % 2], 16)
                NT2 = NG // NST
                gs = NT2 - 1
                sync.wait_ge(LNx, NT2)
                sync.dma_start(
                    out=p.x1[:, :, gs * TT:(gs + 1) * TT].rearrange("c q t -> q c t"),
                    in_=x1t[:]).then_inc(x1sd[gs % 2], 16)
                sync.wait_ge(x1sd[0], (NT2 + 1) // 2 * 16)
                if NT2 >= 2:
                    sync.wait_ge(x1sd[1], NT2 // 2 * 16)

            @blk.tensor
            def _(tensor):
                ntr = 0

                def rot_transpose(in_ap):
                    nonlocal ntr
                    if ntr >= 2:
                        tensor.wait_ge(trev, ntr - 1)
                    nrows = in_ap.shape[-1]
                    tensor.transpose(TR[ntr % 2][0:nrows, 0:128], in_ap,
                                     p.ident_sb[:]).then_inc(trsem, 1)
                    ntr += 1

                for G in range(NG):
                    g, j = G // NST, G % NST
                    jo = j * 64
                    tensor.wait_ge(qld[g % 2], (g // 2 + 1) * 48)
                    tensor.wait_ge(V4, G)
                    tensor.wait_ge(LNe, g)
                    for c in range(KC):
                        for u in range(2):
                            ins = tensor.matmul(
                                B0v[64 * u:64 * u + 64, c, :],
                                qt[g % 2][64 * u:64 * u + 64, c, jo:jo + 64],
                                kt[g % 2][64 * u:64 * u + 64, c, jo:jo + 64],
                                start=True, stop=True)
                    ins.then_inc(T1, 1)
                    tensor.wait_ge(V1, G + 1)
                    for c in range(KC):
                        for sub in range(2):
                            rot_transpose(K2_[:, c, sub, :])
                    tensor.wait_ge(V2, G + 1)
                    tensor.wait_ge(V4, G)
                    for c in range(KC):
                        tensor.matmul(
                            Ttv[:, c, :, :], KtBD[:, c, :],
                            Qt_[:, c, :, :], start=True, stop=True)
                    for c in range(KC):
                        ins = tensor.matmul(
                            B5tv[:, c, :], K2BD[:, c, :],
                            Qt_[:, c, 1, :], start=True, stop=True)
                    ins.then_inc(T3, 1)
                    tensor.wait_ge(V6, v6req(G))
                    tensor.wait_ge(V7, G)
                    tensor.wait_ge(V4, G + 1)
                    tensor.wait_ge(vld[G % 2], (G // 2 + 1) * 48)
                    for c in range(KC):
                        for u in range(2):
                            tensor.matmul(
                                B3v[64 * u:64 * u + 64, c, :],
                                S_all[64 * u:64 * u + 64, c, :],
                                QtS[64 * u:64 * u + 64, c, :, :],
                                start=True, stop=False)
                            for st2 in range(2):
                                tensor.matmul(
                                    B3v[64 * u:64 * u + 64, c,
                                        32 * st2:32 * st2 + 32],
                                    vch[G % 2][64 * u:64 * u + 64,
                                               c * 128 + u * 64:c * 128 + u * 64 + 64],
                                    TtSB[64 * u:64 * u + 64, c, st2, :],
                                    start=False, stop=False)
                            tensor.matmul(
                                B3v[64 * u:64 * u + 64, c, 32:64],
                                vch[G % 2][64 * u:64 * u + 32,
                                           c * 128 + u * 64:c * 128 + u * 64 + 64],
                                ToffSB[64 * u:64 * u + 32, c, :],
                                start=False, stop=True)
                    tensor.wait_ge(trev, NTRS * G + 16)
                    tensor.wait_ge(V6, v6req(G))
                    for c in range(KC):
                        for u in range(2):
                            for sub in range(2):
                                vsrc = vch[G % 2] if sub == 0 else vs1[G % 2]
                                ins = tensor.matmul(
                                    Dv[64 * u:64 * u + 64, c, :],
                                    K2T[0:32, c, sub, u * 64:u * 64 + 64],
                                    vsrc[0:32,
                                         c * 128 + u * 64:c * 128 + u * 64 + 64],
                                    start=(sub == 0), stop=(sub == 1))
                    ins.then_inc(T5, 1)
                    tensor.wait_ge(V3, G + 1)
                    for c in range(KC):
                        rot_transpose(pn[:, c, :])
                    tensor.wait_ge(trev, NTRS * G + NTRS)
                    tensor.wait_ge(V5, G)
                    tensor.wait_ge(LNe, g)
                    for c in range(KC):
                        for u in range(2):
                            ins = tensor.matmul(
                                B1v[64 * u:64 * u + 64, c, :],
                                pT[0:64, c, u * 64:u * 64 + 64],
                                vch[G % 2][0:64,
                                           c * 128 + u * 64:c * 128 + u * 64 + 64],
                                start=True, stop=True)
                    ins.then_inc(T4, 1)
                    tensor.wait_ge(V5, G + 1)
                    tensor.wait_ge(V7, G)
                    for c in range(KC):
                        for u in range(2):
                            idq = p.ident_sb[64 * u:64 * u + 64, 64 * u:64 * u + 64]
                            ins = tensor.matmul(
                                B2v[64 * u:64 * u + 64, c, :],
                                att[64 * u:64 * u + 64, c, :], idq,
                                start=True, stop=True)
                    ins.then_inc(ATr, 1)
                    if j == NST - 1:
                        tensor.wait_ge(V7, (g + 1) * NST)
                        tensor.wait_ge(V4, (g + 1) * NST)
                        for kc in range(KC):
                            ins = tensor.matmul(B[0][:], p.ones_sb[:],
                                                xr[:, kc, :],
                                                start=(kc == 0), stop=(kc == KC - 1))
                        ins.then_inc(LT1, 1)
                        tensor.wait_ge(LSq, g + 1)
                        tensor.wait_ge(V5, (g + 1) * NST)
                        for kc in range(KC):
                            ins = tensor.matmul(B[1][:], p.ones_sb[:],
                                                sq[:, kc, :],
                                                start=(kc == 0), stop=(kc == KC - 1))
                        ins.then_inc(LT2, 1)

            @blk.vector
            def _(vector):
                vector.memset(LY[:, :, :, 0:1], 1.0)
                vector.memset(KtBD[:], 0.0)
                vector.memset(K2BD[:], 0.0)
                vector.drain()
                for G in range(NG):
                    g, j = G // NST, G % NST
                    jo = j * 64
                    b = G // (NG // 2)
                    if G % (NG // 2) == 0 and NG > NST // 2:
                        if b == 1:
                            vector.wait_ge(T4, NG // 2)
                        vector.memset(S_all[:], 0.0)
                        vector.drain()
                        vector.nop().then_inc(V6, 1)
                    vector.wait_ge(qld[g % 2], (g // 2 + 1) * 48)
                    vector.wait_ge(T4, G)
                    atj = at[g % 2][:, :, jo:jo + 64].rearrange(
                        "q c (s2 t) -> q c s2 t", t=32)
                    qtj = qt[g % 2][:, :, jo:jo + 64].rearrange(
                        "q c (s2 t) -> q c s2 t", t=32)
                    ktj = kt[g % 2][:, :, jo:jo + 64].rearrange(
                        "q c (s2 t) -> q c s2 t", t=32)
                    for c in range(KC):
                        for sub in range(2):
                            vector.tensor_tensor_scan(
                                LY[:, c, sub, 1:33], atj[:, c, sub, :],
                                atj[:, c, sub, :], 1.0, op0=OP.mult, op1=OP.bypass)
                    vector.drain()
                    vector.reciprocal(rc[:], LY[:, :, :, 1:33])
                    vector.drain()
                    vector.tensor_mul(Qt_[:], qtj, LY[:, :, :, 0:32])
                    vector.tensor_mul(
                        KtBD[0:64, :, 0:64].rearrange("q c (s2 t) -> q c s2 t", t=32),
                        ktj[0:64, :, :, :], rc[0:64, :, :, :])
                    vector.tensor_mul(
                        KtBD[64:128, :, 64:128].rearrange("q c (s2 t) -> q c s2 t", t=32),
                        ktj[64:128, :, :, :], rc[64:128, :, :, :])
                    vector.drain()
                    vector.tensor_tensor(
                        rc[:], rc[:],
                        LY[:, :, :, 32:33].broadcast_to([128, KC, 2, 32]), op=OP.mult)
                    vector.drain()
                    vector.tensor_mul(
                        K2BD[0:64, :, 0:32], ktj[0:64, :, 0, :], rc[0:64, :, 0, :])
                    vector.tensor_mul(
                        K2BD[64:128, :, 64:96], ktj[64:128, :, 0, :],
                        rc[64:128, :, 0, :])
                    vector.drain()
                    vector.tensor_tensor(
                        rc[:, :, 0, :], rc[:, :, 0, :],
                        LY[:, :, 1, 32:33].broadcast_to([128, KC, 32]), op=OP.mult)
                    vector.drain()
                    vector.tensor_mul(K2_[:], ktj, rc[:])
                    vector.tensor_mul(ct01[:], LY[:, :, 0, 32], LY[:, :, 1, 32])
                    vector.tensor_copy(QtS[:, :, 0, :], Qt_[:, :, 0, :])
                    vector.drain()
                    vector.tensor_tensor(
                        QtS[:, :, 1, :], Qt_[:, :, 1, :],
                        LY[:, :, 0, 32:33].broadcast_to([128, KC, 32]), op=OP.mult)
                    vector.drain()
                    vector.nop().then_inc(V1, 1)
                    vector.wait_ge(T1, G + 1)
                    vector.tensor_tensor(
                        scb[:], B0v[:],
                        p.cmask_sb[:].rearrange("q s -> q () s"
                                                ).broadcast_to([128, KC, 64]),
                        op=OP.add)
                    vector.drain()
                    vector.nop().then_inc(V2, 1)
                    vector.reduce_max(
                        mx[:].rearrange("q c -> q c ()"), scb[:], axis=AX)
                    vector.drain()
                    vector.tensor_tensor(
                        dsb[:], scb[:],
                        mx[:].rearrange("q c -> q c ()").broadcast_to([128, KC, 64]),
                        op=OP.subtract)
                    vector.drain()
                    vector.nop().then_inc(V2b, 1)
                    vector.wait_ge(S1e, G + 1)
                    vector.reduce_sum(
                        sm[:].rearrange("q c -> q c ()"), psb[:], axis=AX)
                    vector.drain()
                    vector.reciprocal(rs[:], sm[:])
                    vector.drain()
                    if G >= 1:
                        vector.wait_ge(trsem, NTRS * (G - 1) + NTRS)
                    vector.tensor_tensor(
                        pn[:], psb[:],
                        rs[:].rearrange("q c -> q c ()").broadcast_to([128, KC, 64]),
                        op=OP.mult)
                    vector.drain()
                    vector.nop().then_inc(V3, 1)
                    base_tr = NTRS * G
                    if G >= 1:
                        vector.wait_ge(T5, G)
                    nn = 0
                    for c in range(KC):
                        for sub in range(2):
                            vector.wait_ge(trsem, base_tr + nn + 1)
                            vector.tensor_copy(
                                K2T[0:32, c, sub, :],
                                TR[(base_tr + nn) % 2][0:32, 0:128])
                            vector.drain()
                            vector.nop().then_inc(trev, 1)
                            nn += 1
                    vector.wait_ge(T3, G + 1)
                    if G >= 1:
                        vector.wait_ge(T4, G)
                    vector.tensor_tensor(
                        TtSB[:], Ttv[:, :, :, :],
                        p.tmask_sb[:].rearrange("q (s2 t) -> q () s2 t", t=32
                                                ).broadcast_to([128, KC, 2, 32]),
                        op=OP.mult)
                    vector.tensor_copy(ToffSB[:], B5tv[:])
                    vector.drain()
                    vector.nop().then_inc(V4, 1)
                    for c in range(KC):
                        vector.wait_ge(trsem, base_tr + nn + 1)
                        vector.tensor_copy(
                            pT[0:64, c, :], TR[(base_tr + nn) % 2][0:64, 0:128])
                        vector.drain()
                        vector.nop().then_inc(trev, 1)
                        nn += 1
                    vector.wait_ge(T4, G + 1)
                    vector.tensor_copy(att[:], B1v[:])
                    vector.drain()
                    vector.nop().then_inc(V5, 1)
                    vector.wait_ge(T5, G + 1)
                    vector.tensor_tensor(
                        S_all[:], S_all[:],
                        ct01[:].rearrange("q c -> q c ()").broadcast_to([128, KC, 64]),
                        op=OP.mult)
                    vector.drain()
                    vector.tensor_add(S_all[:], S_all[:], Dv[:])
                    vector.drain()
                    vector.nop().then_inc(V6, 1)
                    vector.wait_ge(ATr, G + 1)
                    if j == 0:
                        vector.wait_ge(xfl, (g + 1) * 16)
                    vector.tensor_add(xr[:, :, jo:jo + 64], B2v[:],
                                      xft[:, :, jo:jo + 64])
                    vector.drain()
                    vector.tensor_add(xr[:, :, jo:jo + 64],
                                      xr[:, :, jo:jo + 64], B3v[:])
                    vector.drain()
                    vector.nop().then_inc(V7, 1)
                    if j == NST - 1:
                        vector.wait_ge(LT1, g + 1)
                        vector.tensor_scalar_mul(m_sb, B[0][:], 1.0 / DM)
                        vector.drain()
                        vector.tensor_mul(msq, m_sb, m_sb)
                        vector.drain()
                        vector.wait_ge(LT2, g + 1)
                        vector.scalar_tensor_tensor(
                            v_sb, B[1][:], 1.0 / DM, msq,
                            op0=OP.mult, op1=OP.subtract)
                        vector.drain()
                        vector.nop().then_inc(LNe, 1)
                        vector.nop().then_inc(LNv, 1)
                        vector.wait_ge(LNs, g + 1)
                        vector.reciprocal(rstd, sd_sb)
                        vector.drain()
                        if g >= 1:
                            vector.wait_ge(x1sd[(g - 1) % 2],
                                           ((g - 1) // 2 + 1) * 16)
                        vector.tensor_tensor(
                            x1t[:], xr[:],
                            m_sb.rearrange("q t -> q () t"
                                           ).broadcast_to([128, KC, TT]),
                            op=OP.subtract)
                        vector.drain()
                        vector.tensor_tensor(
                            x1t[:], x1t[:],
                            rstd.rearrange("q t -> q () t"
                                           ).broadcast_to([128, KC, TT]),
                            op=OP.mult)
                        vector.drain()
                        vector.reduce_max(
                            p.maxacc[:, g:g + 1],
                            x1t[:].rearrange("q c t -> q (c t)"),
                            axis=AX, apply_absolute_value=True)
                        vector.drain()
                        vector.nop().then_inc(LNx, 1)
                vector.reduce_max(p.sig_sb[:, 0:1], p.maxacc[:, 0:NTT], axis=AX)
                vector.drain()

            @blk.scalar
            def _(scalar):
                for G in range(NG):
                    g, j = G // NST, G % NST
                    scalar.wait_ge(V2b, G + 1)
                    if G >= 1:
                        scalar.wait_ge(V3, G)
                    scalar.activation(psb[:], dsb[:], AF.Exp, scale=0.125).then_inc(S1e, 1)
                    scalar.drain()
                    if j == NST - 1:
                        scalar.wait_ge(V7, (g + 1) * NST)
                        if g >= 1:
                            scalar.wait_ge(LT2, g)
                            scalar.wait_ge(x1sd[(g - 1) % 2],
                                           ((g - 1) // 2 + 1) * 16)
                        scalar.activation(sq[:], xr[:], AF.Square).then_inc(LSq, 1)
                        scalar.drain()
                        scalar.wait_ge(LNv, g + 1)
                        scalar.activation(sd_sb, v_sb, AF.Sqrt, bias=p.eps_sb[:]).then_inc(LNs, 1)
                        scalar.drain()
        nc.clear_and_free_semaphores(slist)

def emit_p3(p):
    nc = p.nc
    jobs = [(p.wup, p.wupT_hi, p.wupT_lo, MC3, DM),
            (p.wgate, p.wgT_hi, p.wgT_lo, MC3, DM),
            (p.wdown, p.wdT_hi, p.wdT_lo, KC, I3)]
    chunks = []
    for ji, (w, whi, wlo, nmc, kk) in enumerate(jobs):
        for mc in range(nmc):
            chunks.append((ji, mc, kk))
    cumkc = [0]
    for (ji, mc, kk) in chunks:
        cumkc.append(cumkc[-1] + kk // 128)
    with ExitStack() as pc:
        sb = lambda nm, sh, d=F32: pc.enter_context(nc.sbuf_tensor(nm, sh, d))
        wt = [sb(f"p3_wt{i}", [128, I3]) for i in range(2)]
        z = sb("p3_z", [128, I3]); zi = sb("p3_zi", [128, I3], I32)
        zc = sb("p3_zc", [128, I3]); A = sb("p3_A", [128, I3])
        Bb = sb("p3_B", [128, I3]); wq = sb("p3_wq", [128, I3])
        am = sb("p3_am", [128, I3 // 32]); sc_ = sb("p3_s", [128, I3 // 32])
        hi = [sb(f"p3_hi{i}", [128, 24, 128], BF16) for i in range(2)]
        lo = [sb(f"p3_lo{i}", [128, 24, 128], BF16) for i in range(2)]
        ps = [pc.enter_context(nc.psum_tensor(f"p3_ps{i}", [128, 128], F32))
              for i in range(2)]
        slist = []
        def sem(nm):
            h = nc.alloc_semaphore(name=nm)
            slist.append(h)
            return h
        ld = [sem(f"p3_ld{i}") for i in range(2)]
        vq = sem("p3_vq")
        tr = sem("p3_tr")
        pe = sem("p3_pe")
        sd = [sem(f"p3_sd{i}") for i in range(2)]
        with nc.Block() as blk:
            @blk.sync
            def _(sync):
                for n, (ji, mc, kk) in enumerate(chunks):
                    w = jobs[ji][0]
                    if n >= 2:
                        sync.wait_ge(vq, n - 1)
                    sync.dma_start(out=wt[n % 2][:, 0:kk],
                                   in_=w[mc * 128:(mc + 1) * 128, :]
                                   ).then_inc(ld[n % 2], 16)
                    if n >= 1:
                        jj, mm, kk2 = chunks[n - 1]
                        whi, wlo = jobs[jj][1], jobs[jj][2]
                        nkc = kk2 // 128
                        sync.wait_ge(pe, cumkc[n])
                        for (stg, dst) in ((hi, whi), (lo, wlo)):
                            sync.dma_start(
                                out=dst[:, :, mm * 128:(mm + 1) * 128].rearrange(
                                    "k q m -> q k m"),
                                in_=stg[(n - 1) % 2][:, 0:nkc, :]
                                ).then_inc(sd[(n - 1) % 2], 16)
                n = len(chunks)
                jj, mm, kk2 = chunks[n - 1]
                whi, wlo = jobs[jj][1], jobs[jj][2]
                nkc = kk2 // 128
                sync.wait_ge(pe, cumkc[n])
                for (stg, dst) in ((hi, whi), (lo, wlo)):
                    sync.dma_start(
                        out=dst[:, :, mm * 128:(mm + 1) * 128].rearrange(
                            "k q m -> q k m"),
                        in_=stg[(n - 1) % 2][:, 0:nkc, :]
                        ).then_inc(sd[(n - 1) % 2], 16)
                h0 = sum(32 for i in range(0, len(chunks), 2))
                h1 = sum(32 for i in range(1, len(chunks), 2))
                sync.wait_ge(sd[0], h0)
                sync.wait_ge(sd[1], h1)

            @blk.tensor
            def _(tensor):
                ntr = 0
                for n, (ji, mc, kk) in enumerate(chunks):
                    tensor.wait_ge(vq, n + 1)
                    for kc in range(kk // 128):
                        if ntr >= 2:
                            tensor.wait_ge(pe, ntr - 1)
                        tensor.transpose(ps[ntr % 2][:],
                                         wq[:, kc * 128:(kc + 1) * 128],
                                         p.ident_sb[:]).then_inc(tr, 1)
                        ntr += 1

            @blk.vector
            def _(vector):
                ntr = 0
                for n, (ji, mc, kk) in enumerate(chunks):
                    vector.wait_ge(ld[n % 2], (n // 2 + 1) * 16)
                    nb = kk // 32
                    w3 = wt[n % 2][:, 0:kk].rearrange("q (g c) -> q g c", c=32)
                    vector.reduce_max(
                        am[:, 0:nb].rearrange("q g -> q g ()"), w3, axis=AX,
                        apply_absolute_value=True)
                    vector.drain()
                    vector.tensor_scalar(sc_[:, 0:nb], am[:, 0:nb], 1.0 / 7.0,
                                         1e-12, op0=OP.mult, op1=OP.max)
                    vector.drain()
                    vector.reciprocal(am[:, 0:nb], sc_[:, 0:nb])
                    vector.drain()
                    vector.tensor_tensor(
                        z[:, 0:kk].rearrange("q (g c) -> q g c", c=32), w3,
                        am[:, 0:nb].rearrange("q g -> q g ()"
                                              ).broadcast_to([128, nb, 32]),
                        op=OP.mult)
                    vector.drain()
                    vector.tensor_copy(zi[:, 0:kk], z[:, 0:kk])
                    vector.drain()
                    vector.tensor_copy(zc[:, 0:kk], zi[:, 0:kk])
                    vector.drain()
                    vector.scalar_tensor_tensor(A[:, 0:kk], zc[:, 0:kk], -0.5,
                                                z[:, 0:kk], op0=OP.add,
                                                op1=OP.is_le)
                    vector.scalar_tensor_tensor(Bb[:, 0:kk], zc[:, 0:kk], 0.5,
                                                z[:, 0:kk], op0=OP.add,
                                                op1=OP.is_le)
                    vector.drain()
                    vector.tensor_add(z[:, 0:kk], zc[:, 0:kk], A[:, 0:kk])
                    vector.drain()
                    vector.tensor_add(z[:, 0:kk], z[:, 0:kk], Bb[:, 0:kk])
                    vector.drain()
                    vector.tensor_scalar_add(z[:, 0:kk], z[:, 0:kk], -1.0)
                    vector.drain()
                    if n >= 1:
                        vector.wait_ge(tr, cumkc[n])
                    vector.tensor_tensor(
                        wq[:, 0:kk].rearrange("q (g c) -> q g c", c=32),
                        z[:, 0:kk].rearrange("q (g c) -> q g c", c=32),
                        sc_[:, 0:nb].rearrange("q g -> q g ()"
                                               ).broadcast_to([128, nb, 32]),
                        op=OP.mult)
                    vector.drain()
                    vector.nop().then_inc(vq, 1)
                    if n >= 2:
                        vector.wait_ge(sd[n % 2], (n // 2) * 32)
                    for kc in range(kk // 128):
                        vector.wait_ge(tr, ntr + 1)
                        vector.tensor_copy(hi[n % 2][:, kc, :], ps[ntr % 2][:])
                        vector.drain()
                        vector.scalar_tensor_tensor(
                            lo[n % 2][:, kc, :], ps[ntr % 2][:], 1.0,
                            hi[n % 2][:, kc, :], op0=OP.mult, op1=OP.subtract)
                        vector.drain()
                        vector.nop().then_inc(pe, 1)
                        ntr += 1
        nc.clear_and_free_semaphores(slist)

def _bcast_cols(p, tensor, vector, psum, src_cols, dst_cols, sem_pair):
    n = src_cols.stop - src_cols.start
    tensor.matmul(psum[:, 0:n], p.ones_sb[:], p.row_sb[:, src_cols],
                  start=True, stop=True).then_inc(sem_pair[0], 1)
    vector.wait_ge(sem_pair[0], sem_pair[1])
    vector.tensor_copy(p.sig_sb[:, dst_cols], psum[:, 0:n])
    vector.drain()


def emit_p4(p):
    nc = p.nc
    with ExitStack() as pc:
        sb = lambda nm, sh, d=F32: pc.enter_context(nc.sbuf_tensor(nm, sh, d))
        xqh = sb("p4_xqh", [128, KC, T], BF16)
        xql = sb("p4_xql", [128, KC, T], BF16)
        x1t = [sb(f"p4_x1t{i}", [128, KC, TT // 2]) for i in range(2)]
        z = sb("p4_z", [128, KC * TT // 2]); zi = sb("p4_zi", [128, KC * TT // 2], I32)
        zc = sb("p4_zc", [128, KC * TT // 2]); A = sb("p4_A", [128, KC * TT // 2])
        Bb = sb("p4_B", [128, KC * TT // 2])
        wch = [[sb(f"p4_wc{i}_{wp}", [128, KC, 128], BF16) for wp in range(2)] for i in range(2)]
        stg = [sb(f"p4_stg{i}", [128, TT]) for i in range(2)]
        tmx = sb("p4_tmx", [128, 1])
        ps = [pc.enter_context(nc.psum_tensor(f"p4_ps{i}", [128, TT], F32))
              for i in range(8)]
        slist = []
        def sem(nm):
            h = nc.alloc_semaphore(name=nm)
            slist.append(h)
            return h
        cc = sem("p4_cc"); ccc = sem("p4_ccc"); bc = sem("p4_bc")
        xld = [sem(f"p4_xld{i}") for i in range(2)]
        xq = sem("p4_xq")
        wld = [sem(f"p4_wld{i}") for i in range(2)]
        mmg = sem("p4_mmg")
        evg = sem("p4_evg")
        sdg = [sem(f"p4_sdg{i}") for i in range(2)]
        NHT = 2 * NTT
        HC = KC * TT // 2
        with nc.Block() as blk:
            @blk.gpsimd
            def _(g):
                g.wait_ge(bc, 2)
                g.dma_start(out=p.cc_in.ap(), in_=p.ccrow[:]).then_inc(cc, 16)
                g.wait_ge(cc, 16)
                g.collective_compute(
                    "AllReduce", OP.max,
                    replica_groups=[CC_GROUP or list(range(NCORES))],
                    ins=[p.cc_in.ap()], outs=[p.cc_out.ap()]).then_inc(ccc, 1)
                g.wait_ge(ccc, 1)
                g.dma_start(out=p.ccrow[:], in_=p.cc_out.ap()).then_inc(cc, 16)
                g.wait_ge(cc, 32)
                g.nop().then_inc(ccc, 1)

            @blk.tensor
            def _(tensor):
                tensor.transpose(ps[0][0:1, 0:128], p.sig_sb[:, 0:1],
                                 p.ident_sb[:]).then_inc(bc, 1)
                tensor.wait_ge(bc, 3)
                tensor.matmul(ps[1][:, 0:2], p.ones_sb[:], p.row_sb[:, 0:2],
                              start=True, stop=True).then_inc(bc, 1)
                tensor.wait_ge(xq, NHT)
                ng = 0
                for wsel in range(2):
                    for mc in range(MC3):
                        n = wsel * MC3 + mc
                        tensor.wait_ge(wld[n % 2], (n // 2 + 1) * 32)
                        for tg in range(8):
                            if ng >= 8:
                                tensor.wait_ge(evg, ng - 7)
                            k3 = 0
                            for (xs, wp) in ((xqh, 0), (xql, 0), (xqh, 1)):
                                for kc in range(KC):
                                    ins = tensor.matmul(
                                        ps[ng % 8][:],
                                        wch[n % 2][wp][:, kc, :],
                                        xs[:, kc, tg * TT:(tg + 1) * TT],
                                        start=(k3 == 0), stop=(k3 == 3 * KC - 1))
                                    k3 += 1
                            ins.then_inc(mmg, 1)
                            ng += 1

            @blk.vector
            def _(vector):
                vector.wait_ge(bc, 1)
                vector.tensor_copy(p.ccrow[0:1, 0:128], ps[0][0:1, 0:128])
                vector.drain()
                vector.nop().then_inc(bc, 1)
                vector.wait_ge(ccc, 2)
                vector.reduce_max(p.row_sb[0:1, 0:1], p.ccrow[0:1, 0:128], axis=AX)
                vector.drain()
                vector.tensor_scalar(p.row_sb[0:1, 0:1], p.row_sb[0:1, 0:1],
                                     1.0 / QMAX, 1e-12, op0=OP.mult, op1=OP.max)
                vector.drain()
                vector.reciprocal(p.row_sb[0:1, 1:2], p.row_sb[0:1, 0:1])
                vector.drain()
                vector.nop().then_inc(bc, 1)
                vector.wait_ge(bc, 4)
                vector.tensor_copy(p.sig_sb[:, 1:3], ps[1][:, 0:2])
                vector.drain()
                vector.nop().then_inc(bc, 1)
                for h in range(NHT):
                    vector.wait_ge(xld[h % 2], (h // 2 + 1) * 16)
                    vector.wait_ge(bc, 5)
                    xv = x1t[h % 2][:].rearrange("q c t -> q (c t)")
                    vector.tensor_scalar_mul(z[:], xv, p.sig_sb[:, 2:3])
                    vector.drain()
                    vector.tensor_copy(zi[:], z[:])
                    vector.drain()
                    vector.tensor_copy(zc[:], zi[:])
                    vector.drain()
                    vector.scalar_tensor_tensor(A[:], zc[:], -0.5, z[:],
                                                op0=OP.add, op1=OP.is_le)
                    vector.scalar_tensor_tensor(Bb[:], zc[:], 0.5, z[:],
                                                op0=OP.add, op1=OP.is_le)
                    vector.drain()
                    vector.tensor_add(z[:], zc[:], A[:])
                    vector.drain()
                    vector.tensor_add(z[:], z[:], Bb[:])
                    vector.drain()
                    vector.tensor_scalar(zc[:], z[:], -1.0, p.sig_sb[:, 1:2],
                                         op0=OP.add, op1=OP.mult)
                    vector.drain()
                    ho = (h // 2) * TT + (h % 2) * (TT // 2)
                    dsth = xqh[:, :, ho:ho + TT // 2]
                    dstl = xql[:, :, ho:ho + TT // 2]
                    zc3 = zc[:].rearrange("q (c t) -> q c t", c=KC)
                    vector.tensor_copy(dsth, zc3)
                    vector.drain()
                    vector.scalar_tensor_tensor(dstl, zc3, 1.0, dsth,
                                                op0=OP.mult, op1=OP.subtract)
                    vector.drain()
                    vector.nop().then_inc(xq, 1)
                ng = 0
                for wsel in range(2):
                    mxcol = 8 + wsel
                    for mc in range(MC3):
                        for tg in range(8):
                            vector.wait_ge(mmg, ng + 1)
                            if ng >= 2:
                                vector.wait_ge(sdg[ng % 2], (ng // 2) * 16)
                            vector.tensor_copy(stg[ng % 2][:], ps[ng % 8][:])
                            vector.drain()
                            vector.reduce_max(tmx[:], stg[ng % 2][:], axis=AX,
                                              apply_absolute_value=True)
                            vector.drain()
                            vector.tensor_max(p.maxacc[:, mxcol:mxcol + 1],
                                              p.maxacc[:, mxcol:mxcol + 1],
                                              tmx[:])
                            vector.drain()
                            vector.nop().then_inc(evg, 1)
                            ng += 1

            @blk.sync
            def _(sync):
                for h in range(NHT):
                    if h >= 2:
                        sync.wait_ge(xq, h - 1)
                    base = h * (TT // 2)
                    sync.dma_start(
                        out=x1t[h % 2][:],
                        in_=p.x1[:, :, base:base + TT // 2].rearrange(
                            "c q t -> q c t")).then_inc(xld[h % 2], 16)
                ng = 0
                for wsel in range(2):
                    hi_s, lo_s = ((p.wupT_hi, p.wupT_lo),
                                  (p.wgT_hi, p.wgT_lo))[wsel]
                    dstscr = (p.upraw, p.gateraw)[wsel]
                    for mc in range(MC3):
                        n = wsel * MC3 + mc
                        if n >= 2:
                            sync.wait_ge(mmg, (n - 1) * 8)
                        for wp, srcw in ((0, hi_s), (1, lo_s)):
                            sync.dma_start(
                                out=wch[n % 2][wp][:],
                                in_=srcw[:, :, mc * 128:(mc + 1) * 128].rearrange(
                                    "k q m -> q k m")).then_inc(wld[n % 2], 16)
                        for tg in range(8):
                            sync.wait_ge(evg, ng + 1)
                            sync.dma_start(
                                out=dstscr[mc, :, tg * TT:(tg + 1) * TT],
                                in_=stg[ng % 2][:]).then_inc(sdg[ng % 2], 16)
                            ng += 1
                sync.wait_ge(sdg[0], 2 * MC3 * 8 // 2 * 16)
                sync.wait_ge(sdg[1], 2 * MC3 * 8 // 2 * 16)
        nc.clear_and_free_semaphores(slist)

def _cc_parts(p, nfold):
    def gp_fn(g, cc, ccc, bc):
        g.wait_ge(bc, 2)
        g.dma_start(out=p.cc_in.ap(), in_=p.ccrow[:]).then_inc(cc, 16)
        g.wait_ge(cc, 16)
        g.collective_compute(
            "AllReduce", OP.max,
            replica_groups=[CC_GROUP or list(range(NCORES))],
            ins=[p.cc_in.ap()], outs=[p.cc_out.ap()]).then_inc(ccc, 1)
        g.wait_ge(ccc, 1)
        g.dma_start(out=p.ccrow[:], in_=p.cc_out.ap()).then_inc(cc, 16)
        g.wait_ge(cc, 32)
        g.nop().then_inc(ccc, 1)

    def tensor_fn(tensor, psum, mcols, bc):
        tensor.transpose(psum[0:nfold, 0:128],
                         p.maxacc[:, mcols[0]:mcols[0] + nfold],
                         p.ident_sb[:]).then_inc(bc, 1)
        tensor.wait_ge(bc, 3)
        tensor.matmul(psum[:, 0:2 * nfold], p.ones_sb[:],
                      p.row_sb[:, 4:4 + 2 * nfold],
                      start=True, stop=True).then_inc(bc, 1)

    def vector_fn(vector, psum, dst, cc, ccc, bc):
        vector.wait_ge(bc, 1)
        vector.tensor_copy(p.ccrow[0:nfold, 0:128], psum[0:nfold, 0:128])
        vector.drain()
        if nfold == 1:
            vector.memset(p.ccrow[1:2, 0:128], 0.0) if False else None
        vector.nop().then_inc(bc, 1)
        vector.wait_ge(ccc, 2)
        vector.reduce_max(p.row_sb[0:nfold, 0:1], p.ccrow[0:nfold, 0:128],
                          axis=AX)
        vector.drain()
        vector.tensor_scalar(p.row_sb[0:nfold, 0:1], p.row_sb[0:nfold, 0:1],
                             1.0 / QMAX, 1e-12, op0=OP.mult, op1=OP.max)
        vector.drain()
        vector.reciprocal(p.row_sb[0:nfold, 1:2], p.row_sb[0:nfold, 0:1])
        vector.drain()
        vector.tensor_tensor(
            p.row_sb[0:nfold, 4:4 + nfold],
            p.row_sb[0:nfold, 0:1].broadcast_to([nfold, nfold]),
            p.ident_sb[0:nfold, 0:nfold], op=OP.mult)
        vector.tensor_tensor(
            p.row_sb[0:nfold, 4 + nfold:4 + 2 * nfold],
            p.row_sb[0:nfold, 1:2].broadcast_to([nfold, nfold]),
            p.ident_sb[0:nfold, 0:nfold], op=OP.mult)
        vector.drain()
        vector.nop().then_inc(bc, 1)
        vector.wait_ge(bc, 4)
        vector.tensor_copy(p.sig_sb[:, dst:dst + 2 * nfold],
                           psum[:, 0:2 * nfold])
        vector.drain()
        vector.nop().then_inc(bc, 1)

    return gp_fn, tensor_fn, vector_fn


def _ladder(vector, z, zi, zc, A, Bb, n):
    vector.tensor_copy(zi[:, 0:n], z[:, 0:n])
    vector.drain()
    vector.tensor_copy(zc[:, 0:n], zi[:, 0:n])
    vector.drain()
    vector.scalar_tensor_tensor(A[:, 0:n], zc[:, 0:n], -0.5, z[:, 0:n],
                                op0=OP.add, op1=OP.is_le)
    vector.scalar_tensor_tensor(Bb[:, 0:n], zc[:, 0:n], 0.5, z[:, 0:n],
                                op0=OP.add, op1=OP.is_le)
    vector.drain()
    vector.tensor_add(z[:, 0:n], zc[:, 0:n], A[:, 0:n])
    vector.drain()
    vector.tensor_add(z[:, 0:n], z[:, 0:n], Bb[:, 0:n])
    vector.drain()


def emit_p5(p):
    nc = p.nc
    HN = 2 * MC3
    HC = T // 2
    gp_fn, t_fn, v_fn = _cc_parts(p, 2)
    with ExitStack() as pc:
        sb = lambda nm, sh, d=F32: pc.enter_context(nc.sbuf_tensor(nm, sh, d))
        upb = [sb(f"p5_up{i}", [128, HC]) for i in range(2)]
        gtb = [sb(f"p5_gt{i}", [128, HC]) for i in range(2)]
        z = sb("p5_z", [128, HC]); zi = sb("p5_zi", [128, HC], I32)
        zc = sb("p5_zc", [128, HC]); A = sb("p5_A", [128, HC])
        Bb = sb("p5_B", [128, HC])
        upq = [sb(f"p5_upq{i}", [128, HC]) for i in range(2)]
        gq = sb("p5_gq", [128, HC]); sg = sb("p5_sg", [128, HC])
        g2 = [sb(f"p5_g2{i}", [128, HC]) for i in range(2)]
        tmx = sb("p5_tmx", [128, 1])
        ps = pc.enter_context(nc.psum_tensor("p5_ps", [128, TT], F32))
        slist = []
        def sem(nm):
            h = nc.alloc_semaphore(name=nm)
            slist.append(h)
            return h
        cc = sem("p5_cc"); ccc = sem("p5_ccc"); bc = sem("p5_bc")
        ld = [sem(f"p5_ld{i}") for i in range(2)]
        ev = sem("p5_ev")
        sd = [sem(f"p5_sd{i}") for i in range(2)]
        with nc.Block() as blk:
            @blk.gpsimd
            def _(g):
                gp_fn(g, cc, ccc, bc)

            @blk.tensor
            def _(tensor):
                t_fn(tensor, ps, [8, 9], bc)

            @blk.scalar
            def _(scalar):
                for h in range(HN):
                    scalar.wait_ge(ev, 2 * h + 1)
                    scalar.activation(sg[:], gq[:], AF.Sigmoid).then_inc(bc, 1)
                    scalar.drain()

            @blk.vector
            def _(vector):
                v_fn(vector, ps, 3, cc, ccc, bc)
                for h in range(HN):
                    mc, half = h // 2, h % 2
                    vector.wait_ge(ld[h % 2], (h // 2 + 1) * 32)
                    vector.tensor_scalar_mul(z[:], upb[h % 2][:],
                                             p.sig_sb[:, 5:6])
                    vector.drain()
                    _ladder(vector, z, zi, zc, A, Bb, HC)
                    if h >= 2:
                        vector.wait_ge(sd[h % 2], (h // 2) * 32)
                    vector.tensor_scalar(upq[h % 2][:], z[:], -1.0,
                                         p.sig_sb[:, 3:4], op0=OP.add, op1=OP.mult)
                    vector.drain()
                    vector.tensor_scalar_mul(z[:], gtb[h % 2][:],
                                             p.sig_sb[:, 6:7])
                    vector.drain()
                    _ladder(vector, z, zi, zc, A, Bb, HC)
                    vector.tensor_scalar(gq[:], z[:], -1.0,
                                         p.sig_sb[:, 4:5], op0=OP.add, op1=OP.mult)
                    vector.drain()
                    vector.nop().then_inc(ev, 1)
                    vector.wait_ge(bc, 5 + h + 1)
                    vector.tensor_scalar_mul(z[:], sg[:], 65536.0)
                    vector.drain()
                    _ladder(vector, z, zi, zc, A, Bb, HC)
                    vector.tensor_scalar_min(z[:], z[:], 65535.0)
                    vector.drain()
                    vector.tensor_mul(g2[h % 2][:], gq[:], z[:])
                    vector.drain()
                    vector.tensor_scalar_mul(g2[h % 2][:], g2[h % 2][:],
                                             1.0 / 65536.0)
                    vector.drain()
                    vector.reduce_max(tmx[:], g2[h % 2][:], axis=AX,
                                      apply_absolute_value=True)
                    vector.drain()
                    vector.tensor_max(p.maxacc[:, 10:11], p.maxacc[:, 10:11],
                                      tmx[:])
                    vector.drain()
                    vector.nop().then_inc(ev, 1)

            @blk.sync
            def _(sync):
                for h in range(HN):
                    mc, half = h // 2, h % 2
                    if h >= 2:
                        sync.wait_ge(ev, 2 * (h - 1) + 2)
                    co = half * HC
                    sync.dma_start(out=upb[h % 2][:],
                                   in_=p.upraw[mc, :, co:co + HC]
                                   ).then_inc(ld[h % 2], 16)
                    sync.dma_start(out=gtb[h % 2][:],
                                   in_=p.gateraw[mc, :, co:co + HC]
                                   ).then_inc(ld[h % 2], 16)
                    sync.wait_ge(ev, 2 * h + 2)
                    sync.dma_start(out=p.upq[mc, :, co:co + HC],
                                   in_=upq[h % 2][:]).then_inc(sd[h % 2], 16)
                    sync.dma_start(out=p.g2raw[mc, :, co:co + HC],
                                   in_=g2[h % 2][:]).then_inc(sd[h % 2], 16)
                sync.wait_ge(sd[0], HN // 2 * 32)
                sync.wait_ge(sd[1], HN // 2 * 32)
        nc.clear_and_free_semaphores(slist)

def emit_p6(p):
    nc = p.nc
    HN = 2 * MC3
    HC = T // 2
    gp_fn, t_fn, v_fn = _cc_parts(p, 1)
    with ExitStack() as pc:
        sb = lambda nm, sh, d=F32: pc.enter_context(nc.sbuf_tensor(nm, sh, d))
        g2b = [sb(f"p6_g2{i}", [128, HC]) for i in range(2)]
        uqb = [sb(f"p6_uq{i}", [128, HC]) for i in range(2)]
        z = sb("p6_z", [128, HC]); zi = sb("p6_zi", [128, HC], I32)
        zc = sb("p6_zc", [128, HC]); A = sb("p6_A", [128, HC])
        Bb = sb("p6_B", [128, HC])
        orb = [sb(f"p6_or{i}", [128, HC]) for i in range(2)]
        tmx = sb("p6_tmx", [128, 1])
        ps = pc.enter_context(nc.psum_tensor("p6_ps", [128, TT], F32))
        slist = []
        def sem(nm):
            h = nc.alloc_semaphore(name=nm)
            slist.append(h)
            return h
        cc = sem("p6_cc"); ccc = sem("p6_ccc"); bc = sem("p6_bc")
        ld = [sem(f"p6_ld{i}") for i in range(2)]
        ev = sem("p6_ev")
        sd = [sem(f"p6_sd{i}") for i in range(2)]
        with nc.Block() as blk:
            @blk.gpsimd
            def _(g):
                gp_fn(g, cc, ccc, bc)

            @blk.tensor
            def _(tensor):
                t_fn(tensor, ps, [10], bc)

            @blk.vector
            def _(vector):
                v_fn(vector, ps, 7, cc, ccc, bc)
                for h in range(HN):
                    vector.wait_ge(ld[h % 2], (h // 2 + 1) * 32)
                    vector.tensor_scalar_mul(z[:], g2b[h % 2][:],
                                             p.sig_sb[:, 8:9])
                    vector.drain()
                    _ladder(vector, z, zi, zc, A, Bb, HC)
                    vector.tensor_scalar(zc[:], z[:], -1.0,
                                         p.sig_sb[:, 7:8], op0=OP.add, op1=OP.mult)
                    vector.drain()
                    if h >= 2:
                        vector.wait_ge(sd[h % 2], (h // 2) * 16)
                    vector.tensor_mul(orb[h % 2][:], zc[:], uqb[h % 2][:])
                    vector.drain()
                    vector.reduce_max(tmx[:], orb[h % 2][:], axis=AX,
                                      apply_absolute_value=True)
                    vector.drain()
                    vector.tensor_max(p.maxacc[:, 11:12], p.maxacc[:, 11:12],
                                      tmx[:])
                    vector.drain()
                    vector.nop().then_inc(ev, 1)

            @blk.sync
            def _(sync):
                for h in range(HN):
                    mc, half = h // 2, h % 2
                    if h >= 2:
                        sync.wait_ge(ev, h - 1)
                    co = half * HC
                    sync.dma_start(out=g2b[h % 2][:],
                                   in_=p.g2raw[mc, :, co:co + HC]
                                   ).then_inc(ld[h % 2], 16)
                    sync.dma_start(out=uqb[h % 2][:],
                                   in_=p.upq[mc, :, co:co + HC]
                                   ).then_inc(ld[h % 2], 16)
                    sync.wait_ge(ev, h + 1)
                    sync.dma_start(out=p.oraw[mc, :, co:co + HC],
                                   in_=orb[h % 2][:]).then_inc(sd[h % 2], 16)
                sync.wait_ge(sd[0], HN // 2 * 16)
                sync.wait_ge(sd[1], HN // 2 * 16)
        nc.clear_and_free_semaphores(slist)

def emit_p7(p):
    nc = p.nc
    gp_fn, t_fn, v_fn = _cc_parts(p, 1)
    T7 = 256
    NT7 = T // T7
    NQG = 6
    QW = 4 * T7
    with ExitStack() as pc:
        sb = lambda nm, sh, d=F32: pc.enter_context(nc.sbuf_tensor(nm, sh, d))
        orb = [sb(f"p7_or{i}", [128, MC3, T7]) for i in range(2)]
        oph = [sb(f"p7_oph{i}", [128, MC3, T7], BF16) for i in range(2)]
        opl = [sb(f"p7_opl{i}", [128, MC3, T7], BF16) for i in range(2)]
        z = sb("p7_z", [128, QW]); zi = sb("p7_zi", [128, QW], I32)
        zc = sb("p7_zc", [128, QW]); A = sb("p7_A", [128, QW])
        Bb = sb("p7_B", [128, QW])
        wdc = [[sb(f"p7_wd{i}_{wp}", [128, MC3, 128], BF16) for wp in range(2)]
               for i in range(2)]
        x1c = sb("p7_x1", [128, KC, T7])
        xr2 = sb("p7_xr2", [128, KC, T7])
        sq2 = sb("p7_sq2", [128, KC, T7])
        outst = [sb(f"p7_out{i}", [128, 2, DM]) for i in range(2)]
        ps = [pc.enter_context(nc.psum_tensor(f"p7_ps{i}", [128, T7], F32))
              for i in range(6)]
        m_sb = z[:, 0:T7]
        msq = z[:, T7:2 * T7]
        v_sb = zc[:, 0:T7]
        sd_sb = zc[:, T7:2 * T7]
        slist = []
        def sem(nm):
            h = nc.alloc_semaphore(name=nm)
            slist.append(h)
            return h
        cc = sem("p7_cc"); ccc = sem("p7_ccc"); bc = sem("p7_bc")
        old = [sem(f"p7_old{i}") for i in range(2)]
        x1l = sem("p7_x1l")
        wls = [sem(f"p7_wl{i}") for i in range(2)]
        qg = sem("p7_qg")
        mmg = sem("p7_mmg")
        evg = sem("p7_evg")
        LT1 = sem("p7_LT1"); LT2 = sem("p7_LT2"); LSq = sem("p7_LSq")
        LNv = sem("p7_LNv"); LNs = sem("p7_LNs"); LNx = sem("p7_LNx")
        otr = sem("p7_otr"); oev = sem("p7_oev")
        osd = [sem(f"p7_osd{i}") for i in range(2)]
        with nc.Block() as blk:
            @blk.gpsimd
            def _(g):
                gp_fn(g, cc, ccc, bc)

            @blk.tensor
            def _(tensor):
                t_fn(tensor, ps[2], [11], bc)
                ntr = 0
                for i in range(NT7):
                    tensor.wait_ge(qg, (i + 1) * NQG)
                    for mc in range(KC):
                        n = i * KC + mc
                        tensor.wait_ge(wls[n % 2], (n // 2 + 1) * 32)
                        if n >= 2:
                            tensor.wait_ge(evg, n - 1)
                        k3 = 0
                        for (xs, wp) in ((oph, 0), (opl, 0), (oph, 1)):
                            for kc in range(MC3):
                                ins = tensor.matmul(
                                    ps[n % 2][:],
                                    wdc[n % 2][wp][:, kc, :],
                                    xs[i % 2][:, kc, :],
                                    start=(k3 == 0), stop=(k3 == 3 * MC3 - 1))
                                k3 += 1
                        ins.then_inc(mmg, 1)
                    tensor.wait_ge(evg, (i + 1) * KC)
                    for kc in range(KC):
                        ins = tensor.matmul(ps[2][:], p.ones_sb[:],
                                            xr2[:, kc, :],
                                            start=(kc == 0), stop=(kc == KC - 1))
                    ins.then_inc(LT1, 1)
                    tensor.wait_ge(LSq, i + 1)
                    for kc in range(KC):
                        ins = tensor.matmul(ps[3][:], p.ones_sb[:],
                                            sq2[:, kc, :],
                                            start=(kc == 0), stop=(kc == KC - 1))
                    ins.then_inc(LT2, 1)
                    tensor.wait_ge(LNx, i + 1)
                    for s4 in range(2):
                        for kc in range(KC):
                            if ntr >= 2:
                                tensor.wait_ge(oev, ntr - 1)
                            tensor.transpose(
                                ps[4 + ntr % 2][0:128, 0:128],
                                xr2[:, kc, s4 * 128:(s4 + 1) * 128],
                                p.ident_sb[:]).then_inc(otr, 1)
                            ntr += 1

            @blk.scalar
            def _(scalar):
                for i in range(NT7):
                    scalar.wait_ge(evg, (i + 1) * KC)
                    if i >= 1:
                        scalar.wait_ge(LT2, i)
                        scalar.wait_ge(osd[(i - 1) % 2], ((i - 1) // 2 + 1) * 32)
                    scalar.activation(sq2[:], xr2[:], AF.Square).then_inc(LSq, 1)
                    scalar.drain()
                    scalar.wait_ge(LNv, i + 1)
                    scalar.activation(sd_sb, v_sb, AF.Sqrt, bias=p.eps_sb[:]).then_inc(LNs, 1)
                    scalar.drain()

            @blk.vector
            def _(vector):
                v_fn(vector, ps[2], 9, cc, ccc, bc)
                ntr = 0
                for i in range(NT7):
                    vector.wait_ge(old[i % 2], (i // 2 + 1) * 16)
                    for qv in range(NQG):
                        src = orb[i % 2][:, 4 * qv:4 * qv + 4, :].rearrange(
                            "q c t -> q (c t)")
                        vector.tensor_scalar_mul(z[:], src, p.sig_sb[:, 10:11])
                        vector.drain()
                        _ladder(vector, z, zi, zc, A, Bb, QW)
                        vector.tensor_scalar(zc[:], z[:], -1.0,
                                             p.sig_sb[:, 9:10],
                                             op0=OP.add, op1=OP.mult)
                        vector.drain()
                        zc3 = zc[:].rearrange("q (c t) -> q c t", c=4)
                        dh = oph[i % 2][:, 4 * qv:4 * qv + 4, :]
                        dl = opl[i % 2][:, 4 * qv:4 * qv + 4, :]
                        vector.tensor_copy(dh, zc3)
                        vector.drain()
                        vector.scalar_tensor_tensor(dl, zc3, 1.0, dh,
                                                    op0=OP.mult, op1=OP.subtract)
                        vector.drain()
                        vector.nop().then_inc(qg, 1)
                    vector.wait_ge(x1l, (i + 1) * 16)
                    for mc in range(KC):
                        n = i * KC + mc
                        vector.wait_ge(mmg, n + 1)
                        vector.tensor_add(xr2[:, mc, :], ps[n % 2][:],
                                          x1c[:, mc, :])
                        vector.drain()
                        vector.nop().then_inc(evg, 1)
                    vector.wait_ge(LT1, i + 1)
                    vector.tensor_scalar_mul(m_sb, ps[2][:], 1.0 / DM)
                    vector.drain()
                    vector.tensor_mul(msq, m_sb, m_sb)
                    vector.drain()
                    vector.wait_ge(LT2, i + 1)
                    vector.scalar_tensor_tensor(v_sb, ps[3][:], 1.0 / DM, msq,
                                                op0=OP.mult, op1=OP.subtract)
                    vector.drain()
                    vector.nop().then_inc(LNv, 1)
                    vector.wait_ge(LNs, i + 1)
                    vector.reciprocal(sd_sb, sd_sb)
                    vector.drain()
                    vector.tensor_tensor(
                        xr2[:], xr2[:],
                        m_sb.rearrange("q t -> q () t").broadcast_to([128, KC, T7]),
                        op=OP.subtract)
                    vector.drain()
                    vector.tensor_tensor(
                        xr2[:], xr2[:],
                        sd_sb.rearrange("q t -> q () t").broadcast_to([128, KC, T7]),
                        op=OP.mult)
                    vector.drain()
                    vector.nop().then_inc(LNx, 1)
                    if i >= 2:
                        vector.wait_ge(osd[i % 2], (i // 2) * 32)
                    for s4 in range(2):
                        for kc in range(KC):
                            vector.wait_ge(otr, ntr + 1)
                            vector.tensor_copy(
                                outst[i % 2][:, s4, kc * 128:(kc + 1) * 128],
                                ps[4 + ntr % 2][0:128, 0:128])
                            vector.drain()
                            vector.nop().then_inc(oev, 1)
                            ntr += 1

            @blk.sync
            def _(sync):
                for i in range(NT7):
                    if i >= 2:
                        sync.wait_ge(qg, (i - 1) * NQG)
                    sync.dma_start(
                        out=orb[i % 2][:],
                        in_=p.oraw[:, :, i * T7:(i + 1) * T7].rearrange(
                            "c q t -> q c t")).then_inc(old[i % 2], 16)
                    sync.wait_ge(evg, i * KC)
                    sync.dma_start(
                        out=x1c[:],
                        in_=p.x1[:, :, i * T7:(i + 1) * T7].rearrange(
                            "c q t -> q c t")).then_inc(x1l, 16)
                    for mc in range(KC):
                        n = i * KC + mc
                        if n >= 2:
                            sync.wait_ge(mmg, n - 1)
                        for wp, srcw in ((0, p.wdT_hi), (1, p.wdT_lo)):
                            sync.dma_start(
                                out=wdc[n % 2][wp][:],
                                in_=srcw[:, :, mc * 128:(mc + 1) * 128].rearrange(
                                    "k q m -> q k m")).then_inc(wls[n % 2], 16)
                    if i >= 1:
                        for s4 in range(2):
                            sync.wait_ge(oev, (i - 1) * 16 + (s4 + 1) * KC)
                            r0 = (i - 1) * T7 + s4 * 128
                            sync.dma_start(
                                out=p.out[r0:r0 + 128, :],
                                in_=outst[(i - 1) % 2][:, s4, :]
                                ).then_inc(osd[(i - 1) % 2], 16)
                for s4 in range(2):
                    sync.wait_ge(oev, (NT7 - 1) * 16 + (s4 + 1) * KC)
                    r0 = (NT7 - 1) * T7 + s4 * 128
                    sync.dma_start(out=p.out[r0:r0 + 128, :],
                                   in_=outst[(NT7 - 1) % 2][:, s4, :]
                                   ).then_inc(osd[(NT7 - 1) % 2], 16)
                sync.wait_ge(osd[0], NT7 // 2 * 32)
                sync.wait_ge(osd[1], NT7 // 2 * 32)
        nc.clear_and_free_semaphores(slist)

def emit_debug(p):
    nc = p.nc
    if not p.debug_taps:
        return
    srcs = dict(xf=p.xf, qf=p.qf, kf=p.kf, af=p.af, vt=p.vt, x1=p.x1,
                upraw=p.upraw, gateraw=p.gateraw, upq=p.upq, g2raw=p.g2raw,
                oraw=p.oraw)
    with ExitStack() as pc:
        st = nc.alloc_semaphore(name="dbg_st")
        with nc.Block() as blk:
            @blk.sync
            def _(sync):
                n = 0
                for name, ap in p.debug_taps.items():
                    sync.dma_start(out=ap, in_=srcs[name]).then_inc(st, 16)
                    n += 1
                sync.wait_ge(st, n * 16)


def make_consts():
    ident = np.eye(128, dtype=np.float32)
    ones = np.ones((128, 128), dtype=np.float32)
    cmask = np.zeros((128, 64), dtype=np.float32)
    for u in range(2):
        for t in range(64):
            cmask[u * 64 + t, t + 1:] = NEG
    tmask = np.zeros((128, 64), dtype=np.float32)
    for pr in range(128):
        sub_s, s_ = (pr % 64) // 32, pr % 32
        for col in range(64):
            sub_t, tau = col // 32, col % 32
            if sub_s == sub_t and s_ < tau:
                tmask[pr, col] = 1.0
    return ident, ones, cmask, tmask


def make_in_maps(x, W):
    ident, ones, cmask, tmask = make_consts()
    shared = dict(W)
    shared.update(ident=ident, ones=ones, cmask=cmask, tmask=tmask)
    in_maps = []
    for c in range(NCORES):
        m = dict(shared)
        m["x"] = np.ascontiguousarray(
            np.asarray(x[c * B_PER:(c + 1) * B_PER], dtype=np.float32
                       ).reshape(T, DM))
        in_maps.append(m)
    return in_maps


def kernel(L0_attn_x, L0_attn_Wq, L0_attn_bq, L0_attn_Wk, L0_attn_bk,
           L0_attn_Wv, L0_attn_bv, L0_attn_Wg1, L0_attn_bg1, L0_attn_Wg2,
           L0_attn_bg2, L0_ffn_w_gate, L0_ffn_w_up, L0_ffn_w_down):
    W = dict(
        Wq=np.ascontiguousarray(L0_attn_Wq), Wk=np.ascontiguousarray(L0_attn_Wk),
        Wv=np.ascontiguousarray(L0_attn_Wv),
        bq=np.ascontiguousarray(L0_attn_bq), bk=np.ascontiguousarray(L0_attn_bk),
        bv=np.ascontiguousarray(L0_attn_bv),
        Wg1=np.ascontiguousarray(L0_attn_Wg1), bg1=np.ascontiguousarray(L0_attn_bg1),
        Wg2=np.ascontiguousarray(L0_attn_Wg2), bg2=np.ascontiguousarray(L0_attn_bg2),
        wgate=np.ascontiguousarray(L0_ffn_w_gate), wup=np.ascontiguousarray(L0_ffn_w_up),
        wdown=np.ascontiguousarray(L0_ffn_w_down))
    nc, p = build()
    in_maps = make_in_maps(np.asarray(L0_attn_x), W)
    res = run_bass_kernel_spmd(nc, in_maps, core_ids=list(range(NCORES)))
    outs = [np.asarray(res.results[c]["out"]).reshape(B_PER, L, DM)
            for c in range(NCORES)]
    return np.concatenate(outs, axis=0)


# version 02
# speedup vs baseline: 1.0159x; 1.0159x over previous
import numpy as np
from contextlib import ExitStack

import concourse.bass as bass
import concourse.mybir as mybir
from concourse.bass_utils import run_bass_kernel_spmd

F32 = mybir.dt.float32
BF16 = mybir.dt.bfloat16
I32 = mybir.dt.int32
OP = mybir.AluOpType
AF = mybir.ActivationFunctionType
AX = mybir.AxisListType.X

NCORES = 8
B_PER = 2
L = 2048
DM = 1024
T = B_PER * L
TT = 512
NTT = T // TT
KC = DM // 128
H = 16
D = 64
C = 64
CS = 32
NCH = L // C
I3 = 3072
MC3 = I3 // 128
QMAX = 32767.0
NEG = -1.0e30
PHASES = 8
P2_STEPS = None
P2_SKIP = set()
CC_GROUP = None


class P:
    pass


def build(phases=PHASES, debug_taps=()):
    nc = bass.Bass("TRN2", target_bir_lowering=False)
    p = P()
    p.nc = nc
    p.phases = phases
    dt = nc.dram_tensor

    p.x = dt("x", [T, DM], F32, kind="ExternalInput").ap()
    p.Wq = dt("Wq", [DM, DM], F32, kind="ExternalInput").ap()
    p.Wk = dt("Wk", [DM, DM], F32, kind="ExternalInput").ap()
    p.Wv = dt("Wv", [DM, DM], F32, kind="ExternalInput").ap()
    p.bq = dt("bq", [DM], F32, kind="ExternalInput").ap()
    p.bk = dt("bk", [DM], F32, kind="ExternalInput").ap()
    p.bv = dt("bv", [DM], F32, kind="ExternalInput").ap()
    p.Wg1 = dt("Wg1", [DM, 16], F32, kind="ExternalInput").ap()
    p.bg1 = dt("bg1", [16], F32, kind="ExternalInput").ap()
    p.Wg2 = dt("Wg2", [16, DM], F32, kind="ExternalInput").ap()
    p.bg2 = dt("bg2", [DM], F32, kind="ExternalInput").ap()
    p.wgate = dt("wgate", [I3, DM], F32, kind="ExternalInput").ap()
    p.wup = dt("wup", [I3, DM], F32, kind="ExternalInput").ap()
    p.wdown = dt("wdown", [DM, I3], F32, kind="ExternalInput").ap()
    p.ident = dt("ident", [128, 128], F32, kind="ExternalInput").ap()
    p.ones = dt("ones", [128, 128], F32, kind="ExternalInput").ap()
    p.cmask = dt("cmask", [128, 64], F32, kind="ExternalInput").ap()
    p.tmask = dt("tmask", [128, 64], F32, kind="ExternalInput").ap()

    p.out = dt("out", [T, DM], F32, kind="ExternalOutput").ap()

    p.xf = dt("xf_scr", [KC, 128, T], F32).ap()
    p.qf = dt("qf_scr", [KC, 128, T], F32).ap()
    p.kf = dt("kf_scr", [KC, 128, T], F32).ap()
    p.af = dt("af_scr", [KC, 128, T], F32).ap()
    p.vt = dt("vt_scr", [T, DM], F32).ap()
    p.x1 = dt("x1_scr", [KC, 128, T], F32).ap()
    p.upraw = dt("upraw_scr", [MC3, 128, T], F32).ap()
    p.gateraw = dt("gateraw_scr", [MC3, 128, T], F32).ap()
    p.upq = dt("upq_scr", [MC3, 128, T], F32).ap()
    p.g2raw = dt("g2raw_scr", [MC3, 128, T], F32).ap()
    p.oraw = dt("oraw_scr", [MC3, 128, T], F32).ap()
    p.wupT_hi = dt("wupT_hi", [KC, 128, I3], BF16).ap()
    p.wupT_lo = dt("wupT_lo", [KC, 128, I3], BF16).ap()
    p.wgT_hi = dt("wgT_hi", [KC, 128, I3], BF16).ap()
    p.wgT_lo = dt("wgT_lo", [KC, 128, I3], BF16).ap()
    p.wdT_hi = dt("wdT_hi", [MC3, 128, DM], BF16).ap()
    p.wdT_lo = dt("wdT_lo", [MC3, 128, DM], BF16).ap()
    p.cc_in = dt("cc_in", [2, 128], F32)
    p.cc_out = dt("cc_out", [2, 128], F32)

    p.debug_taps = {}
    for name, shape in debug_taps:
        p.debug_taps[name] = dt("dbg_" + name, list(shape), F32,
                                kind="ExternalOutput").ap()

    with ExitStack() as ctx:
        p.semctx = ctx
        sbt = lambda nm, sh, d=F32: ctx.enter_context(nc.sbuf_tensor(nm, sh, d))
        p.ident_sb = sbt("ident_sb", [128, 128])
        p.ones_sb = sbt("ones_sb", [128, 128])
        p.cmask_sb = sbt("cmask_sb", [128, 64])
        p.tmask_sb = sbt("tmask_sb", [128, 64])
        p.bq_sb = sbt("bq_sb", [128, KC])
        p.bk_sb = sbt("bk_sb", [128, KC])
        p.bg1_sb = sbt("bg1_sb", [16, 1])
        p.bg2_sb = sbt("bg2_sb", [128, KC])
        p.bvrow = sbt("bvrow", [128, DM])
        p.maxacc = sbt("maxacc", [128, 16])
        p.sig_sb = sbt("sig_sb", [128, 16])
        p.row_sb = sbt("row_sb", [128, 16])
        p.ccrow = sbt("ccrow", [2, 128])
        p.eps_sb = sbt("eps_sb", [128, 1])
        p.b32k = sbt("b32k", [128, 1])

        emit_setup(p)
        if phases >= 1:
            emit_p0(p)
        if phases >= 2:
            emit_p1(p)
        if phases >= 3:
            emit_p2(p)
        if phases >= 4:
            emit_p3(p)
        if phases >= 5:
            emit_p4(p)
        if phases >= 6:
            emit_p5(p)
        if phases >= 7:
            emit_p6(p)
        if phases >= 8:
            emit_p7(p)
        emit_debug(p)
    return nc, p


def emit_setup(p):
    nc = p.nc
    with nc.semaphore("su_ld") as ld, nc.semaphore("su_ms") as ms, nc.Block() as blk:
        @blk.sync
        def _(sync):
            n = 0
            def d(out, in_):
                nonlocal n
                with nc.allow_non_contiguous_dma(reason="tiny one-time bias loads"):
                    sync.dma_start(out=out, in_=in_).then_inc(ld, 16)
                n += 1
            d(p.ident_sb[:], p.ident)
            d(p.ones_sb[:], p.ones)
            d(p.cmask_sb[:], p.cmask)
            d(p.tmask_sb[:], p.tmask)
            d(p.bq_sb[:], p.bq.rearrange("(c q) -> q c", q=128))
            d(p.bk_sb[:], p.bk.rearrange("(c q) -> q c", q=128))
            d(p.bg2_sb[:], p.bg2.rearrange("(c q) -> q c", q=128))
            d(p.bg1_sb[:], p.bg1.rearrange("g -> g ()"))
            sync.wait_ge(ms, 1)
            d(p.bvrow[0:1, :], p.bv.rearrange("f -> () f"))
            sync.wait_ge(ld, n * 16)

        @blk.vector
        def _(vector):
            vector.memset(p.bvrow[:], 0.0)
            vector.memset(p.maxacc[:], 0.0)
            vector.memset(p.row_sb[:], 0.0)
            vector.memset(p.ccrow[:], 0.0)
            vector.memset(p.eps_sb[:], 1e-5)
            vector.memset(p.b32k[:], 32768.5)
            vector.drain()
            vector.nop().then_inc(ms, 1)


def emit_p0(p):
    nc = p.nc
    with ExitStack() as pc:
        sb = lambda nm, sh, d=F32: pc.enter_context(nc.sbuf_tensor(nm, sh, d))
        xin = [sb(f"p0_xin{i}", [128, 4, DM]) for i in range(2)]
        xfo = [sb(f"p0_xfo{i}", [128, KC, TT]) for i in range(2)]
        ps = [pc.enter_context(nc.psum_tensor(f"p0_ps{i}", [128, 128], F32))
              for i in range(4)]
        slist = []
        def sem(nm):
            h = nc.alloc_semaphore(name=nm)
            slist.append(h)
            return h
        ld = [sem(f"p0_ld{i}") for i in range(2)]
        tr = sem("p0_tr")
        ev = sem("p0_ev")
        st = [sem(f"p0_st{i}") for i in range(2)]
        NTR = 32
        with nc.Block() as blk:
            @blk.sync
            def _(sync):
                def store(j):
                    sync.wait_ge(ev, (j + 1) * NTR)
                    sync.dma_start(
                        out=p.xf[:, :, j * TT:(j + 1) * TT].rearrange("c q t -> q c t"),
                        in_=xfo[j % 2][:]).then_inc(st[j % 2], 16)
                for i in range(NTT):
                    if i >= 2:
                        sync.wait_ge(tr, (i - 1) * NTR)
                    src = p.x[i * TT:(i + 1) * TT, :].rearrange(
                        "(s q) f -> q s f", q=128)
                    sync.dma_start(out=xin[i % 2][:], in_=src).then_inc(ld[i % 2], 16)
                    if i >= 1:
                        store(i - 1)
                store(NTT - 1)
                sync.wait_ge(st[0], NTT // 2 * 16)
                sync.wait_ge(st[1], NTT // 2 * 16)

            @blk.tensor
            def _(tensor):
                n = 0
                for i in range(NTT):
                    tensor.wait_ge(ld[i % 2], (i // 2 + 1) * 16)
                    for s in range(4):
                        for c in range(KC):
                            if n >= 4:
                                tensor.wait_ge(ev, n - 3)
                            tensor.transpose(
                                ps[n % 4][:], xin[i % 2][:, s, c * 128:(c + 1) * 128],
                                p.ident_sb[:]).then_inc(tr, 1)
                            n += 1

            @blk.vector
            def _(vector):
                n = 0
                for i in range(NTT):
                    if i >= 2:
                        vector.wait_ge(st[i % 2], (i // 2) * 16)
                    for s in range(4):
                        for c in range(KC):
                            vector.wait_ge(tr, n + 1)
                            vector.tensor_copy(
                                xfo[i % 2][:, c, s * 128:(s + 1) * 128],
                                ps[n % 4][:]).then_inc(ev, 1)
                            n += 1
        nc.clear_and_free_semaphores(slist)

def emit_p1(p):
    nc = p.nc
    with ExitStack() as pc:
        sb = lambda nm, sh, d=F32: pc.enter_context(nc.sbuf_tensor(nm, sh, d))
        wq_hi = sb("wq_hi", [128, KC, DM], BF16); wq_lo = sb("wq_lo", [128, KC, DM], BF16)
        wk_hi = sb("wk_hi", [128, KC, DM], BF16); wk_lo = sb("wk_lo", [128, KC, DM], BF16)
        wv_hi = sb("wv_hi", [128, KC, DM], BF16); wv_lo = sb("wv_lo", [128, KC, DM], BF16)
        wg1_hi = sb("wg1_hi", [128, KC, 16], BF16); wg1_lo = sb("wg1_lo", [128, KC, 16], BF16)
        wg2_sb = sb("wg2_sb", [16, DM])
        wtmp = [sb(f"p1_wtmp{i}", [128, DM]) for i in range(2)]
        xf_t = [sb(f"p1_xf{i}", [128, KC, TT]) for i in range(2)]
        xh = [sb(f"p1_xh{i}", [128, KC, TT], BF16) for i in range(2)]
        xl = [sb(f"p1_xl{i}", [128, KC, TT], BF16) for i in range(2)]
        qst = [sb(f"p1_qst{i}", [128, TT]) for i in range(2)]
        kst = [sb(f"p1_kst{i}", [128, TT]) for i in range(2)]
        vst = [sb(f"p1_vst{i}", [128, TT]) for i in range(2)]
        ast = [sb(f"p1_ast{i}", [128, TT]) for i in range(2)]
        g1b = sb("p1_g1b", [16, TT]); sg1 = sb("p1_sg1", [16, TT])
        g1s = [sb(f"p1_g1s{i}", [16, TT]) for i in range(2)]
        asig = sb("p1_asig", [128, TT])
        psq = pc.enter_context(nc.psum_tensor("p1_psq", [128, TT], F32))
        psk = pc.enter_context(nc.psum_tensor("p1_psk", [128, TT], F32))
        psv = pc.enter_context(nc.psum_tensor("p1_psv", [128, TT], F32))
        psg1 = pc.enter_context(nc.psum_tensor("p1_psg1", [16, TT], F32))
        psg2 = pc.enter_context(nc.psum_tensor("p1_psg2", [128, TT], F32))
        slist = []
        def sem(nm):
            h = nc.alloc_semaphore(name=nm)
            slist.append(h)
            return h
        wld = [sem(f"p1_wld{i}") for i in range(2)]
        wsp = sem("p1_wsp")
        xld = [sem(f"p1_xld{i}") for i in range(2)]
        xsp = sem("p1_xsp")
        qmm = sem("p1_qmm"); kmm = sem("p1_kmm"); vmm = sem("p1_vmm")
        g1mm = sem("p1_g1mm"); g2mm = sem("p1_g2mm")
        qev = sem("p1_qev"); kev = sem("p1_kev"); vev = sem("p1_vev")
        aev = sem("p1_aev")
        sc1 = sem("p1_sc1")
        g1sv = sem("p1_g1sv")
        scs = sem("p1_scs")
        qsd = [sem(f"p1_qsd{i}") for i in range(2)]
        ksd = [sem(f"p1_ksd{i}") for i in range(2)]
        vsd = [sem(f"p1_vsd{i}") for i in range(2)]
        asd = [sem(f"p1_asd{i}") for i in range(2)]

        NW = 3 * KC + 1
        with nc.Block() as blk:
            @blk.sync
            def _(sync):
                n = 0
                for (wsrc, _, _) in ((p.Wq, wq_hi, wq_lo), (p.Wk, wk_hi, wk_lo),
                                     (p.Wv, wv_hi, wv_lo)):
                    for kc in range(KC):
                        if n >= 2:
                            sync.wait_ge(wsp, n - 1)
                        sync.dma_start(out=wtmp[n % 2][:],
                                       in_=wsrc[kc * 128:(kc + 1) * 128, :]
                                       ).then_inc(wld[n % 2], 16)
                        n += 1
                if n >= 2:
                    sync.wait_ge(wsp, n - 1)
                with nc.allow_non_contiguous_dma(reason="tiny Wg1 one-time"):
                    sync.dma_start(out=wtmp[n % 2][:, 0:16 * KC].rearrange(
                                       "q (c g) -> q c g", g=16),
                                   in_=p.Wg1.rearrange("(c q) g -> q c g", q=128)
                                   ).then_inc(wld[n % 2], 16)
                n += 1
                sync.wait_ge(wsp, n - 1)
                sync.dma_start(out=wg2_sb[:], in_=p.Wg2).then_inc(wld[n % 2], 16)
                n += 1
                nq = nk = nv = na = 0
                def stores_for(j):
                    nonlocal nq, nk, nv, na
                    for mc in range(KC):
                        sync.wait_ge(qev, nq + 1)
                        sync.dma_start(out=p.qf[mc, :, j * TT:(j + 1) * TT],
                                       in_=qst[nq % 2][:]).then_inc(qsd[nq % 2], 16)
                        nq += 1
                        sync.wait_ge(kev, nk + 1)
                        sync.dma_start(out=p.kf[mc, :, j * TT:(j + 1) * TT],
                                       in_=kst[nk % 2][:]).then_inc(ksd[nk % 2], 16)
                        nk += 1
                        sync.wait_ge(vev, nv + 1)
                        tsub, half = mc % 4, mc // 4
                        r0 = j * TT + tsub * 128
                        sync.dma_start(out=p.vt[r0:r0 + 128,
                                                half * TT:(half + 1) * TT],
                                       in_=vst[nv % 2][:]).then_inc(vsd[nv % 2], 16)
                        nv += 1
                        sync.wait_ge(aev, na + 1)
                        sync.dma_start(out=p.af[mc, :, j * TT:(j + 1) * TT],
                                       in_=ast[na % 2][:]).then_inc(asd[na % 2], 16)
                        na += 1
                for i in range(NTT):
                    if i >= 1:
                        sync.wait_ge(xsp, i)
                    sync.dma_start(
                        out=xf_t[i % 2][:],
                        in_=p.xf[:, :, i * TT:(i + 1) * TT].rearrange("c q t -> q c t")
                        ).then_inc(xld[i % 2], 16)
                    if i >= 1:
                        stores_for(i - 1)
                stores_for(NTT - 1)
                for s2 in (qsd, ksd, vsd, asd):
                    sync.wait_ge(s2[0], NTT * KC // 2 * 16)
                    sync.wait_ge(s2[1], NTT * KC // 2 * 16)

            @blk.vector
            def _(vector):
                n = 0
                for (_, whi, wlo) in ((p.Wq, wq_hi, wq_lo), (p.Wk, wk_hi, wk_lo),
                                      (p.Wv, wv_hi, wv_lo)):
                    for kc in range(KC):
                        vector.wait_ge(wld[n % 2], (n // 2 + 1) * 16)
                        vector.tensor_copy(whi[:, kc, :], wtmp[n % 2][:])
                        vector.drain()
                        vector.tensor_sub(wlo[:, kc, :], wtmp[n % 2][:], whi[:, kc, :])
                        vector.drain()
                        vector.nop().then_inc(wsp, 1)
                        n += 1
                vector.wait_ge(wld[n % 2], (n // 2 + 1) * 16)
                wg1view = wtmp[n % 2][:, 0:16 * KC].rearrange("q (c g) -> q c g", g=16)
                vector.tensor_copy(wg1_hi[:], wg1view)
                vector.drain()
                vector.tensor_sub(wg1_lo[:], wg1view, wg1_hi[:])
                vector.drain()
                vector.nop().then_inc(wsp, 1)
                n += 1
                nq = nk = nv = na = 0
                for i in range(NTT):
                    vector.wait_ge(xld[i % 2], (i // 2 + 1) * 16)
                    if i >= 2:
                        vector.wait_ge(g2mm, (i - 1) * KC)
                    vector.tensor_copy(xh[i % 2][:], xf_t[i % 2][:])
                    vector.drain()
                    vector.tensor_sub(xl[i % 2][:], xf_t[i % 2][:], xh[i % 2][:])
                    vector.drain()
                    vector.nop().then_inc(xsp, 1)
                    vector.wait_ge(sc1, 2 * (i + 1))
                    vector.tensor_mul(g1s[i % 2][:], g1b[:], sg1[:])
                    vector.drain()
                    vector.nop().then_inc(g1sv, 1)
                    for mc in range(KC):
                        vector.wait_ge(qmm, nq + 1)
                        if nq >= 2:
                            vector.wait_ge(qsd[nq % 2], (nq // 2) * 16)
                        vector.tensor_scalar_add(qst[nq % 2][:], psq[:],
                                                 p.bq_sb[:, mc:mc + 1])
                        vector.drain()
                        vector.nop().then_inc(qev, 1)
                        nq += 1
                        vector.wait_ge(kmm, nk + 1)
                        if nk >= 2:
                            vector.wait_ge(ksd[nk % 2], (nk // 2) * 16)
                        vector.tensor_scalar_add(kst[nk % 2][:], psk[:],
                                                 p.bk_sb[:, mc:mc + 1])
                        vector.drain()
                        vector.nop().then_inc(kev, 1)
                        nk += 1
                        vector.wait_ge(vmm, nv + 1)
                        if nv >= 2:
                            vector.wait_ge(vsd[nv % 2], (nv // 2) * 16)
                        vector.tensor_copy(vst[nv % 2][:], psv[:])
                        vector.drain()
                        vector.nop().then_inc(vev, 1)
                        nv += 1
                        vector.wait_ge(scs, na + 1)
                        if na >= 2:
                            vector.wait_ge(asd[na % 2], (na // 2) * 16)
                        vector.tensor_scalar(ast[na % 2][:], asig[:], 0.9, 0.1,
                                             op0=OP.mult, op1=OP.add)
                        vector.drain()
                        vector.nop().then_inc(aev, 1)
                        na += 1

            @blk.scalar
            def _(scalar):
                for i in range(NTT):
                    scalar.wait_ge(g1mm, i + 1)
                    if i >= 1:
                        scalar.wait_ge(g1sv, i)
                    scalar.activation(g1b[:], psg1[:], AF.Identity,
                                      bias=p.bg1_sb[:, 0:1], scale=1.0).then_inc(sc1, 1)
                    scalar.drain()
                    scalar.activation(sg1[:], psg1[:], AF.Sigmoid,
                                      bias=p.bg1_sb[:, 0:1], scale=1.0).then_inc(sc1, 1)
                    scalar.drain()
                    for mc in range(KC):
                        nag = i * KC + mc
                        scalar.wait_ge(g2mm, nag + 1)
                        if nag >= 1:
                            scalar.wait_ge(aev, nag)
                        scalar.activation(asig[:], psg2[:], AF.Sigmoid,
                                          bias=p.bg2_sb[:, mc:mc + 1], scale=1.0).then_inc(scs, 1)
                        scalar.drain()

            @blk.tensor
            def _(tensor):
                tensor.wait_ge(wsp, NW)
                tensor.wait_ge(wld[1], 13 * 16)
                nq = nk = nv = na = 0
                for i in range(NTT):
                    bi = i % 2
                    tensor.wait_ge(xsp, i + 1)
                    if i >= 1:
                        tensor.wait_ge(sc1, 2 * i)
                    for n3, (xs, wk_) in enumerate(
                            ((xh, 0), (xl, 0), (xh, 1))):
                        w = (wg1_hi, wg1_lo)[wk_]
                        for kc in range(KC):
                            ins = tensor.matmul(psg1[:], w[:, kc, :],
                                                xs[bi][:, kc, :],
                                                start=(n3 == 0 and kc == 0),
                                                stop=(n3 == 2 and kc == KC - 1))
                    ins.then_inc(g1mm, 1)
                    for mc in range(KC):
                        tensor.wait_ge(qev, nq)
                        k3 = 0
                        for (xs, wpc) in ((xh, wq_hi), (xl, wq_hi), (xh, wq_lo)):
                            for kc in range(KC):
                                ins = tensor.matmul(
                                    psq[:], wpc[:, kc, mc * 128:(mc + 1) * 128],
                                    xs[bi][:, kc, :],
                                    start=(k3 == 0), stop=(k3 == 3 * KC - 1))
                                k3 += 1
                        ins.then_inc(qmm, 1)
                        nq += 1
                        tensor.wait_ge(kev, nk)
                        k3 = 0
                        for (xs, wpc) in ((xh, wk_hi), (xl, wk_hi), (xh, wk_lo)):
                            for kc in range(KC):
                                ins = tensor.matmul(
                                    psk[:], wpc[:, kc, mc * 128:(mc + 1) * 128],
                                    xs[bi][:, kc, :],
                                    start=(k3 == 0), stop=(k3 == 3 * KC - 1))
                                k3 += 1
                        ins.then_inc(kmm, 1)
                        nk += 1
                        tensor.wait_ge(vev, nv)
                        tsub, half = mc % 4, mc // 4
                        tensor.matmul(psv[:], p.ones_sb[:],
                                      p.bvrow[:, half * TT:(half + 1) * TT],
                                      start=True, stop=False)
                        k3 = 0
                        for (xs, wpc) in ((xh, wv_hi), (xl, wv_hi), (xh, wv_lo)):
                            for kc in range(KC):
                                ins = tensor.matmul(
                                    psv[:], xs[bi][:, kc, tsub * 128:(tsub + 1) * 128],
                                    wpc[:, kc, half * TT:(half + 1) * TT],
                                    start=False, stop=(k3 == 3 * KC - 1))
                                k3 += 1
                        ins.then_inc(vmm, 1)
                        nv += 1
                        tensor.wait_ge(g1sv, i + 1)
                        tensor.wait_ge(scs, na)
                        ins = tensor.matmul(psg2[:], wg2_sb[:, mc * 128:(mc + 1) * 128],
                                            g1s[bi][:], start=True, stop=True)
                        ins.then_inc(g2mm, 1)
                        na += 1
        nc.clear_and_free_semaphores(slist)

def emit_p2(p):
    nc = p.nc
    NST = 8
    NG = NTT * NST
    if P2_STEPS is not None:
        NG = P2_STEPS
    NTRS = 24
    with ExitStack() as pc:
        sb = lambda nm, sh, d=F32: pc.enter_context(nc.sbuf_tensor(nm, sh, d))
        qt = [sb(f"p2_qt{i}", [128, KC, TT]) for i in range(2)]
        kt = [sb(f"p2_kt{i}", [128, KC, TT]) for i in range(2)]
        at0 = sb("p2_at", [128, KC, TT])
        at = [at0, at0]
        xft = sb("p2_xft", [128, KC, TT])
        xr = sb("p2_xr", [128, KC, TT])
        x1t = sb("p2_x1t", [128, KC, TT])
        sq = x1t
        vch = [sb(f"p2_vch{i}", [128, DM]) for i in range(2)]
        vs1 = [sb(f"p2_vs1{i}", [32, DM]) for i in range(2)]
        S_all = sb("p2_S", [128, KC, 64])
        LY = sb("p2_LY", [128, KC, 2, 33])
        rc = sb("p2_rc", [128, KC, 2, 32])
        Qt_ = sb("p2_Qt", [128, KC, 2, 32])
        QtS = sb("p2_QtS", [128, KC, 2, 32])
        KtBD = sb("p2_KtBD", [128, KC, 128])
        K2BD = sb("p2_K2BD", [128, KC, 128])
        ToffSB = sb("p2_ToffSB", [128, KC, 32])
        K2_ = sb("p2_K2", [128, KC, 2, 32])
        ct01 = sb("p2_ct01", [128, KC])
        scb = sb("p2_scb", [128, KC, 64])
        mx = sb("p2_mx", [128, KC])
        dsb = sb("p2_dsb", [128, KC, 64])
        psb = sb("p2_psb", [128, KC, 64])
        sm = sb("p2_sm", [128, KC])
        rs = sb("p2_rs", [128, KC])
        pn = sb("p2_pn", [128, KC, 64])
        pT = sb("p2_pT", [64, KC, 128])
        K2T = sb("p2_K2T", [32, KC, 2, 128])
        TtSB = sb("p2_TtSB", [128, KC, 2, 32])
        att = sb("p2_att", [128, KC, 64])
        m_sb = pn[:].rearrange("q c e -> q (c e)")
        msq = dsb[:].rearrange("q c e -> q (c e)")
        v_sb = scb[:].rearrange("q c e -> q (c e)")
        sd_sb = att[:].rearrange("q c e -> q (c e)")
        rstd = sd_sb
        B = [pc.enter_context(nc.psum_tensor(f"p2_B{i}", [128, TT], F32))
             for i in range(8)]
        B0v = B[0][:].rearrange("q (c e) -> q c e", e=64)
        Ttv = B[0][:].rearrange("q (c s2 t) -> q c s2 t", s2=2, t=32)
        B1v = B[1][:].rearrange("q (c e) -> q c e", e=64)
        B2v = B[2][:].rearrange("q (c e) -> q c e", e=64)
        B3v = B[3][:].rearrange("q (c e) -> q c e", e=64)
        Dv = B[4][:].rearrange("q (c e) -> q c e", e=64)
        B5tv = B[5][:, 0:KC * 32].rearrange("q (c t) -> q c t", t=32)
        TR = [B[6], B[7]]

        slist = []
        def sem(nm):
            h = nc.alloc_semaphore(name=nm)
            slist.append(h)
            return h
        qld = [sem(f"p2_qld{i}") for i in range(2)]
        xfl = sem("p2_xfl")
        vld = [sem(f"p2_vld{i}") for i in range(2)]
        x1sd = [sem(f"p2_x1sd{i}") for i in range(2)]
        V1 = sem("p2_V1"); V2 = sem("p2_V2"); V2b = sem("p2_V2b")
        V3 = sem("p2_V3"); V4 = sem("p2_V4"); V5 = sem("p2_V5")
        V6 = sem("p2_V6"); V7 = sem("p2_V7")
        S1e = sem("p2_S1e")
        T1 = sem("p2_T1"); T3 = sem("p2_T3"); T4 = sem("p2_T4"); T5 = sem("p2_T5")
        ATr = sem("p2_ATr")
        trsem = sem("p2_trsem"); trev = sem("p2_trev")
        LSq = sem("p2_LSq"); LT1 = sem("p2_LT1"); LT2 = sem("p2_LT2")
        LNx = sem("p2_LNx"); LNe = sem("p2_LNe"); LNv = sem("p2_LNv")
        LNs = sem("p2_LNs")

        def v6req(G):
            return G + 1 + (1 if G >= NG // 2 else 0)

        with nc.Block() as blk:
            @blk.sync
            def _(sync):
                for G in range(NG):
                    g, j = G // NST, G % NST
                    if j == 0:
                        if g >= 2:
                            sync.wait_ge(T3, (g - 1) * NST)
                            sync.wait_ge(V1, (g - 1) * NST)
                        base = g * TT
                        for (dst, src) in ((qt, p.qf), (kt, p.kf)):
                            sync.dma_start(
                                out=dst[g % 2][:],
                                in_=src[:, :, base:base + TT].rearrange(
                                    "c q t -> q c t")).then_inc(qld[g % 2], 16)
                        if g >= 1:
                            sync.wait_ge(V1, g * NST)
                        sync.dma_start(
                            out=at[g % 2][:],
                            in_=p.af[:, :, base:base + TT].rearrange(
                                "c q t -> q c t")).then_inc(qld[g % 2], 16)
                        sync.wait_ge(V7, g * NST)
                        sync.dma_start(
                            out=xft[:],
                            in_=p.xf[:, :, base:base + TT].rearrange("c q t -> q c t")
                            ).then_inc(xfl, 16)
                    if G >= 2:
                        sync.wait_ge(T4, G - 1)
                    r0 = g * TT + j * 64
                    sync.dma_start(out=vch[G % 2][0:64, :],
                                   in_=p.vt[r0:r0 + 64, :]).then_inc(vld[G % 2], 16)
                    sync.dma_start(out=vch[G % 2][64:128, :],
                                   in_=p.vt[r0:r0 + 64, :]).then_inc(vld[G % 2], 16)
                    sync.dma_start(out=vs1[G % 2][0:32, :],
                                   in_=p.vt[r0 + 32:r0 + 64, :]).then_inc(vld[G % 2], 16)
                    if j == NST - 1 and g >= 1:
                        gs = g - 1
                        sync.wait_ge(LNx, gs + 1)
                        sync.dma_start(
                            out=p.x1[:, :, gs * TT:(gs + 1) * TT].rearrange(
                                "c q t -> q c t"),
                            in_=x1t[:]).then_inc(x1sd[gs % 2], 16)
                NT2 = NG // NST
                gs = NT2 - 1
                sync.wait_ge(LNx, NT2)
                sync.dma_start(
                    out=p.x1[:, :, gs * TT:(gs + 1) * TT].rearrange("c q t -> q c t"),
                    in_=x1t[:]).then_inc(x1sd[gs % 2], 16)
                sync.wait_ge(x1sd[0], (NT2 + 1) // 2 * 16)
                if NT2 >= 2:
                    sync.wait_ge(x1sd[1], NT2 // 2 * 16)

            @blk.tensor
            def _(tensor):
                ntr = 0

                def rot_transpose(in_ap):
                    nonlocal ntr
                    if ntr >= 2:
                        tensor.wait_ge(trev, ntr - 1)
                    nrows = in_ap.shape[-1]
                    tensor.transpose(TR[ntr % 2][0:nrows, 0:128], in_ap,
                                     p.ident_sb[:]).then_inc(trsem, 1)
                    ntr += 1

                for G in range(NG):
                    g, j = G // NST, G % NST
                    jo = j * 64
                    tensor.wait_ge(qld[g % 2], (g // 2 + 1) * 48)
                    tensor.wait_ge(V4, G)
                    tensor.wait_ge(LNe, g)
                    for c in range(KC):
                        for u in range(2):
                            ins = tensor.matmul(
                                B0v[64 * u:64 * u + 64, c, :],
                                qt[g % 2][64 * u:64 * u + 64, c, jo:jo + 64],
                                kt[g % 2][64 * u:64 * u + 64, c, jo:jo + 64],
                                start=True, stop=True)
                    ins.then_inc(T1, 1)
                    tensor.wait_ge(V1, G + 1)
                    for c in range(KC):
                        for sub in range(2):
                            rot_transpose(K2_[:, c, sub, :])
                    tensor.wait_ge(V2, G + 1)
                    tensor.wait_ge(V4, G)
                    for c in range(KC):
                        tensor.matmul(
                            Ttv[:, c, :, :], KtBD[:, c, :],
                            Qt_[:, c, :, :], start=True, stop=True)
                    for c in range(KC):
                        ins = tensor.matmul(
                            B5tv[:, c, :], K2BD[:, c, :],
                            Qt_[:, c, 1, :], start=True, stop=True)
                    ins.then_inc(T3, 1)
                    tensor.wait_ge(V6, v6req(G))
                    tensor.wait_ge(V7, G)
                    tensor.wait_ge(V4, G + 1)
                    tensor.wait_ge(vld[G % 2], (G // 2 + 1) * 48)
                    for c in range(KC):
                        for u in range(2):
                            tensor.matmul(
                                B3v[64 * u:64 * u + 64, c, :],
                                S_all[64 * u:64 * u + 64, c, :],
                                QtS[64 * u:64 * u + 64, c, :, :],
                                start=True, stop=False)
                            for st2 in range(2):
                                tensor.matmul(
                                    B3v[64 * u:64 * u + 64, c,
                                        32 * st2:32 * st2 + 32],
                                    vch[G % 2][64 * u:64 * u + 64,
                                               c * 128 + u * 64:c * 128 + u * 64 + 64],
                                    TtSB[64 * u:64 * u + 64, c, st2, :],
                                    start=False, stop=False)
                            tensor.matmul(
                                B3v[64 * u:64 * u + 64, c, 32:64],
                                vch[G % 2][64 * u:64 * u + 32,
                                           c * 128 + u * 64:c * 128 + u * 64 + 64],
                                ToffSB[64 * u:64 * u + 32, c, :],
                                start=False, stop=True)
                    tensor.wait_ge(trev, NTRS * G + 16)
                    tensor.wait_ge(V6, v6req(G))
                    for c in range(KC):
                        for u in range(2):
                            for sub in range(2):
                                vsrc = vch[G % 2] if sub == 0 else vs1[G % 2]
                                ins = tensor.matmul(
                                    Dv[64 * u:64 * u + 64, c, :],
                                    K2T[0:32, c, sub, u * 64:u * 64 + 64],
                                    vsrc[0:32,
                                         c * 128 + u * 64:c * 128 + u * 64 + 64],
                                    start=(sub == 0), stop=(sub == 1))
                    ins.then_inc(T5, 1)
                    tensor.wait_ge(V3, G + 1)
                    for c in range(KC):
                        rot_transpose(pn[:, c, :])
                    tensor.wait_ge(trev, NTRS * G + NTRS)
                    tensor.wait_ge(V5, G)
                    tensor.wait_ge(LNe, g)
                    for c in range(KC):
                        for u in range(2):
                            ins = tensor.matmul(
                                B1v[64 * u:64 * u + 64, c, :],
                                pT[0:64, c, u * 64:u * 64 + 64],
                                vch[G % 2][0:64,
                                           c * 128 + u * 64:c * 128 + u * 64 + 64],
                                start=True, stop=True)
                    ins.then_inc(T4, 1)
                    tensor.wait_ge(V5, G + 1)
                    tensor.wait_ge(V7, G)
                    for c in range(KC):
                        for u in range(2):
                            idq = p.ident_sb[64 * u:64 * u + 64, 64 * u:64 * u + 64]
                            ins = tensor.matmul(
                                B2v[64 * u:64 * u + 64, c, :],
                                att[64 * u:64 * u + 64, c, :], idq,
                                start=True, stop=True)
                    ins.then_inc(ATr, 1)
                    if j == NST - 1:
                        tensor.wait_ge(V7, (g + 1) * NST)
                        tensor.wait_ge(V4, (g + 1) * NST)
                        for kc in range(KC):
                            ins = tensor.matmul(B[0][:], p.ones_sb[:],
                                                xr[:, kc, :],
                                                start=(kc == 0), stop=(kc == KC - 1))
                        ins.then_inc(LT1, 1)
                        tensor.wait_ge(LSq, g + 1)
                        tensor.wait_ge(V5, (g + 1) * NST)
                        for kc in range(KC):
                            ins = tensor.matmul(B[1][:], p.ones_sb[:],
                                                sq[:, kc, :],
                                                start=(kc == 0), stop=(kc == KC - 1))
                        ins.then_inc(LT2, 1)

            @blk.vector
            def _(vector):
                vector.memset(LY[:, :, :, 0:1], 1.0)
                vector.memset(KtBD[:], 0.0)
                vector.memset(K2BD[:], 0.0)
                vector.drain()
                for G in range(NG):
                    g, j = G // NST, G % NST
                    jo = j * 64
                    b = G // (NG // 2)
                    if G % (NG // 2) == 0 and NG > NST // 2:
                        if b == 1:
                            vector.wait_ge(T4, NG // 2)
                        vector.memset(S_all[:], 0.0)
                        vector.drain()
                        vector.nop().then_inc(V6, 1)
                    vector.wait_ge(qld[g % 2], (g // 2 + 1) * 48)
                    vector.wait_ge(T4, G)
                    atj = at[g % 2][:, :, jo:jo + 64].rearrange(
                        "q c (s2 t) -> q c s2 t", t=32)
                    qtj = qt[g % 2][:, :, jo:jo + 64].rearrange(
                        "q c (s2 t) -> q c s2 t", t=32)
                    ktj = kt[g % 2][:, :, jo:jo + 64].rearrange(
                        "q c (s2 t) -> q c s2 t", t=32)
                    for c in range(KC):
                        for sub in range(2):
                            vector.tensor_tensor_scan(
                                LY[:, c, sub, 1:33], atj[:, c, sub, :],
                                atj[:, c, sub, :], 1.0, op0=OP.mult, op1=OP.bypass)
                    vector.drain()
                    vector.reciprocal(rc[:], LY[:, :, :, 1:33])
                    vector.drain()
                    vector.tensor_mul(Qt_[:], qtj, LY[:, :, :, 0:32])
                    vector.tensor_mul(
                        KtBD[0:64, :, 0:64].rearrange("q c (s2 t) -> q c s2 t", t=32),
                        ktj[0:64, :, :, :], rc[0:64, :, :, :])
                    vector.tensor_mul(
                        KtBD[64:128, :, 64:128].rearrange("q c (s2 t) -> q c s2 t", t=32),
                        ktj[64:128, :, :, :], rc[64:128, :, :, :])
                    vector.drain()
                    vector.tensor_tensor(
                        rc[:], rc[:],
                        LY[:, :, :, 32:33].broadcast_to([128, KC, 2, 32]), op=OP.mult)
                    vector.drain()
                    vector.tensor_mul(
                        K2BD[0:64, :, 0:32], ktj[0:64, :, 0, :], rc[0:64, :, 0, :])
                    vector.tensor_mul(
                        K2BD[64:128, :, 64:96], ktj[64:128, :, 0, :],
                        rc[64:128, :, 0, :])
                    vector.drain()
                    vector.tensor_tensor(
                        rc[:, :, 0, :], rc[:, :, 0, :],
                        LY[:, :, 1, 32:33].broadcast_to([128, KC, 32]), op=OP.mult)
                    vector.drain()
                    vector.tensor_mul(K2_[:], ktj, rc[:])
                    vector.tensor_mul(ct01[:], LY[:, :, 0, 32], LY[:, :, 1, 32])
                    vector.tensor_copy(QtS[:, :, 0, :], Qt_[:, :, 0, :])
                    vector.drain()
                    vector.tensor_tensor(
                        QtS[:, :, 1, :], Qt_[:, :, 1, :],
                        LY[:, :, 0, 32:33].broadcast_to([128, KC, 32]), op=OP.mult)
                    vector.drain()
                    vector.nop().then_inc(V1, 1)
                    vector.wait_ge(T1, G + 1)
                    vector.tensor_tensor(
                        scb[:], B0v[:],
                        p.cmask_sb[:].rearrange("q s -> q () s"
                                                ).broadcast_to([128, KC, 64]),
                        op=OP.add)
                    vector.drain()
                    vector.nop().then_inc(V2, 1)
                    vector.reduce_max(
                        mx[:].rearrange("q c -> q c ()"), scb[:], axis=AX)
                    vector.drain()
                    vector.tensor_tensor(
                        dsb[:], scb[:],
                        mx[:].rearrange("q c -> q c ()").broadcast_to([128, KC, 64]),
                        op=OP.subtract)
                    vector.drain()
                    vector.nop().then_inc(V2b, 1)
                    vector.wait_ge(S1e, G + 1)
                    vector.reduce_sum(
                        sm[:].rearrange("q c -> q c ()"), psb[:], axis=AX)
                    vector.drain()
                    vector.reciprocal(rs[:], sm[:])
                    vector.drain()
                    if G >= 1:
                        vector.wait_ge(trsem, NTRS * (G - 1) + NTRS)
                    vector.tensor_tensor(
                        pn[:], psb[:],
                        rs[:].rearrange("q c -> q c ()").broadcast_to([128, KC, 64]),
                        op=OP.mult)
                    vector.drain()
                    vector.nop().then_inc(V3, 1)
                    base_tr = NTRS * G
                    if G >= 1:
                        vector.wait_ge(T5, G)
                    nn = 0
                    for c in range(KC):
                        for sub in range(2):
                            vector.wait_ge(trsem, base_tr + nn + 1)
                            vector.tensor_copy(
                                K2T[0:32, c, sub, :],
                                TR[(base_tr + nn) % 2][0:32, 0:128])
                            vector.drain()
                            vector.nop().then_inc(trev, 1)
                            nn += 1
                    vector.wait_ge(T3, G + 1)
                    if G >= 1:
                        vector.wait_ge(T4, G)
                    vector.tensor_tensor(
                        TtSB[:], Ttv[:, :, :, :],
                        p.tmask_sb[:].rearrange("q (s2 t) -> q () s2 t", t=32
                                                ).broadcast_to([128, KC, 2, 32]),
                        op=OP.mult)
                    vector.tensor_copy(ToffSB[:], B5tv[:])
                    vector.drain()
                    vector.nop().then_inc(V4, 1)
                    for c in range(KC):
                        vector.wait_ge(trsem, base_tr + nn + 1)
                        vector.tensor_copy(
                            pT[0:64, c, :], TR[(base_tr + nn) % 2][0:64, 0:128])
                        vector.drain()
                        vector.nop().then_inc(trev, 1)
                        nn += 1
                    vector.wait_ge(T4, G + 1)
                    vector.tensor_copy(att[:], B1v[:])
                    vector.drain()
                    vector.nop().then_inc(V5, 1)
                    vector.wait_ge(T5, G + 1)
                    vector.tensor_tensor(
                        S_all[:], S_all[:],
                        ct01[:].rearrange("q c -> q c ()").broadcast_to([128, KC, 64]),
                        op=OP.mult)
                    vector.drain()
                    vector.tensor_add(S_all[:], S_all[:], Dv[:])
                    vector.drain()
                    vector.nop().then_inc(V6, 1)
                    vector.wait_ge(ATr, G + 1)
                    if j == 0:
                        vector.wait_ge(xfl, (g + 1) * 16)
                    vector.tensor_add(xr[:, :, jo:jo + 64], B2v[:],
                                      xft[:, :, jo:jo + 64])
                    vector.drain()
                    vector.tensor_add(xr[:, :, jo:jo + 64],
                                      xr[:, :, jo:jo + 64], B3v[:])
                    vector.drain()
                    vector.nop().then_inc(V7, 1)
                    if j == NST - 1:
                        vector.wait_ge(LT1, g + 1)
                        vector.tensor_scalar_mul(m_sb, B[0][:], 1.0 / DM)
                        vector.drain()
                        vector.tensor_mul(msq, m_sb, m_sb)
                        vector.drain()
                        vector.wait_ge(LT2, g + 1)
                        vector.scalar_tensor_tensor(
                            v_sb, B[1][:], 1.0 / DM, msq,
                            op0=OP.mult, op1=OP.subtract)
                        vector.drain()
                        vector.nop().then_inc(LNe, 1)
                        vector.nop().then_inc(LNv, 1)
                        vector.wait_ge(LNs, g + 1)
                        vector.reciprocal(rstd, sd_sb)
                        vector.drain()
                        if g >= 1:
                            vector.wait_ge(x1sd[(g - 1) % 2],
                                           ((g - 1) // 2 + 1) * 16)
                        vector.tensor_tensor(
                            x1t[:], xr[:],
                            m_sb.rearrange("q t -> q () t"
                                           ).broadcast_to([128, KC, TT]),
                            op=OP.subtract)
                        vector.drain()
                        vector.tensor_tensor(
                            x1t[:], x1t[:],
                            rstd.rearrange("q t -> q () t"
                                           ).broadcast_to([128, KC, TT]),
                            op=OP.mult)
                        vector.drain()
                        vector.reduce_max(
                            p.maxacc[:, g:g + 1],
                            x1t[:].rearrange("q c t -> q (c t)"),
                            axis=AX, apply_absolute_value=True)
                        vector.drain()
                        vector.nop().then_inc(LNx, 1)
                vector.reduce_max(p.sig_sb[:, 0:1], p.maxacc[:, 0:NTT], axis=AX)
                vector.drain()

            @blk.scalar
            def _(scalar):
                for G in range(NG):
                    g, j = G // NST, G % NST
                    scalar.wait_ge(V2b, G + 1)
                    if G >= 1:
                        scalar.wait_ge(V3, G)
                    scalar.activation(psb[:], dsb[:], AF.Exp, scale=0.125).then_inc(S1e, 1)
                    scalar.drain()
                    if j == NST - 1:
                        scalar.wait_ge(V7, (g + 1) * NST)
                        if g >= 1:
                            scalar.wait_ge(LT2, g)
                            scalar.wait_ge(x1sd[(g - 1) % 2],
                                           ((g - 1) // 2 + 1) * 16)
                        scalar.activation(sq[:], xr[:], AF.Square).then_inc(LSq, 1)
                        scalar.drain()
                        scalar.wait_ge(LNv, g + 1)
                        scalar.activation(sd_sb, v_sb, AF.Sqrt, bias=p.eps_sb[:]).then_inc(LNs, 1)
                        scalar.drain()
        nc.clear_and_free_semaphores(slist)

def emit_p3(p):
    nc = p.nc
    jobs = [(p.wup, p.wupT_hi, p.wupT_lo, MC3, DM),
            (p.wgate, p.wgT_hi, p.wgT_lo, MC3, DM),
            (p.wdown, p.wdT_hi, p.wdT_lo, KC, I3)]
    chunks = []
    for ji, (w, whi, wlo, nmc, kk) in enumerate(jobs):
        for mc in range(nmc):
            chunks.append((ji, mc, kk))
    cumkc = [0]
    for (ji, mc, kk) in chunks:
        cumkc.append(cumkc[-1] + kk // 128)
    with ExitStack() as pc:
        sb = lambda nm, sh, d=F32: pc.enter_context(nc.sbuf_tensor(nm, sh, d))
        wt = [sb(f"p3_wt{i}", [128, I3]) for i in range(2)]
        z = sb("p3_z", [128, I3]); zi = sb("p3_zi", [128, I3], I32)
        zc = sb("p3_zc", [128, I3]); A = sb("p3_A", [128, I3])
        Bb = sb("p3_B", [128, I3]); wq = sb("p3_wq", [128, I3])
        am = sb("p3_am", [128, I3 // 32]); sc_ = sb("p3_s", [128, I3 // 32])
        hi = [sb(f"p3_hi{i}", [128, 24, 128], BF16) for i in range(2)]
        lo = [sb(f"p3_lo{i}", [128, 24, 128], BF16) for i in range(2)]
        ps = [pc.enter_context(nc.psum_tensor(f"p3_ps{i}", [128, 128], F32))
              for i in range(2)]
        slist = []
        def sem(nm):
            h = nc.alloc_semaphore(name=nm)
            slist.append(h)
            return h
        ld = [sem(f"p3_ld{i}") for i in range(2)]
        vq = sem("p3_vq")
        tr = sem("p3_tr")
        pe = sem("p3_pe")
        sd = [sem(f"p3_sd{i}") for i in range(2)]
        with nc.Block() as blk:
            @blk.sync
            def _(sync):
                for n, (ji, mc, kk) in enumerate(chunks):
                    w = jobs[ji][0]
                    if n >= 2:
                        sync.wait_ge(vq, n - 1)
                    sync.dma_start(out=wt[n % 2][:, 0:kk],
                                   in_=w[mc * 128:(mc + 1) * 128, :]
                                   ).then_inc(ld[n % 2], 16)
                    if n >= 1:
                        jj, mm, kk2 = chunks[n - 1]
                        whi, wlo = jobs[jj][1], jobs[jj][2]
                        nkc = kk2 // 128
                        sync.wait_ge(pe, cumkc[n])
                        for (stg, dst) in ((hi, whi), (lo, wlo)):
                            sync.dma_start(
                                out=dst[:, :, mm * 128:(mm + 1) * 128].rearrange(
                                    "k q m -> q k m"),
                                in_=stg[(n - 1) % 2][:, 0:nkc, :]
                                ).then_inc(sd[(n - 1) % 2], 16)
                n = len(chunks)
                jj, mm, kk2 = chunks[n - 1]
                whi, wlo = jobs[jj][1], jobs[jj][2]
                nkc = kk2 // 128
                sync.wait_ge(pe, cumkc[n])
                for (stg, dst) in ((hi, whi), (lo, wlo)):
                    sync.dma_start(
                        out=dst[:, :, mm * 128:(mm + 1) * 128].rearrange(
                            "k q m -> q k m"),
                        in_=stg[(n - 1) % 2][:, 0:nkc, :]
                        ).then_inc(sd[(n - 1) % 2], 16)
                h0 = sum(32 for i in range(0, len(chunks), 2))
                h1 = sum(32 for i in range(1, len(chunks), 2))
                sync.wait_ge(sd[0], h0)
                sync.wait_ge(sd[1], h1)

            @blk.tensor
            def _(tensor):
                ntr = 0
                for n, (ji, mc, kk) in enumerate(chunks):
                    tensor.wait_ge(vq, n + 1)
                    for kc in range(kk // 128):
                        if ntr >= 2:
                            tensor.wait_ge(pe, ntr - 1)
                        tensor.transpose(ps[ntr % 2][:],
                                         wq[:, kc * 128:(kc + 1) * 128],
                                         p.ident_sb[:]).then_inc(tr, 1)
                        ntr += 1

            @blk.vector
            def _(vector):
                ntr = 0
                for n, (ji, mc, kk) in enumerate(chunks):
                    vector.wait_ge(ld[n % 2], (n // 2 + 1) * 16)
                    nb = kk // 32
                    w3 = wt[n % 2][:, 0:kk].rearrange("q (g c) -> q g c", c=32)
                    vector.reduce_max(
                        am[:, 0:nb].rearrange("q g -> q g ()"), w3, axis=AX,
                        apply_absolute_value=True)
                    vector.drain()
                    vector.tensor_scalar(sc_[:, 0:nb], am[:, 0:nb], 1.0 / 7.0,
                                         1e-12, op0=OP.mult, op1=OP.max)
                    vector.drain()
                    vector.reciprocal(am[:, 0:nb], sc_[:, 0:nb])
                    vector.drain()
                    vector.tensor_tensor(
                        z[:, 0:kk].rearrange("q (g c) -> q g c", c=32), w3,
                        am[:, 0:nb].rearrange("q g -> q g ()"
                                              ).broadcast_to([128, nb, 32]),
                        op=OP.mult)
                    vector.drain()
                    vector.tensor_copy(zi[:, 0:kk], z[:, 0:kk])
                    vector.drain()
                    vector.tensor_copy(zc[:, 0:kk], zi[:, 0:kk])
                    vector.drain()
                    vector.scalar_tensor_tensor(A[:, 0:kk], zc[:, 0:kk], -0.5,
                                                z[:, 0:kk], op0=OP.add,
                                                op1=OP.is_le)
                    vector.scalar_tensor_tensor(Bb[:, 0:kk], zc[:, 0:kk], 0.5,
                                                z[:, 0:kk], op0=OP.add,
                                                op1=OP.is_le)
                    vector.drain()
                    vector.tensor_add(z[:, 0:kk], zc[:, 0:kk], A[:, 0:kk])
                    vector.drain()
                    vector.tensor_add(z[:, 0:kk], z[:, 0:kk], Bb[:, 0:kk])
                    vector.drain()
                    vector.tensor_scalar_add(z[:, 0:kk], z[:, 0:kk], -1.0)
                    vector.drain()
                    if n >= 1:
                        vector.wait_ge(tr, cumkc[n])
                    vector.tensor_tensor(
                        wq[:, 0:kk].rearrange("q (g c) -> q g c", c=32),
                        z[:, 0:kk].rearrange("q (g c) -> q g c", c=32),
                        sc_[:, 0:nb].rearrange("q g -> q g ()"
                                               ).broadcast_to([128, nb, 32]),
                        op=OP.mult)
                    vector.drain()
                    vector.nop().then_inc(vq, 1)
                    if n >= 2:
                        vector.wait_ge(sd[n % 2], (n // 2) * 32)
                    for kc in range(kk // 128):
                        vector.wait_ge(tr, ntr + 1)
                        vector.tensor_copy(hi[n % 2][:, kc, :], ps[ntr % 2][:])
                        vector.drain()
                        vector.scalar_tensor_tensor(
                            lo[n % 2][:, kc, :], ps[ntr % 2][:], 1.0,
                            hi[n % 2][:, kc, :], op0=OP.mult, op1=OP.subtract)
                        vector.drain()
                        vector.nop().then_inc(pe, 1)
                        ntr += 1
        nc.clear_and_free_semaphores(slist)

def _bcast_cols(p, tensor, vector, psum, src_cols, dst_cols, sem_pair):
    n = src_cols.stop - src_cols.start
    tensor.matmul(psum[:, 0:n], p.ones_sb[:], p.row_sb[:, src_cols],
                  start=True, stop=True).then_inc(sem_pair[0], 1)
    vector.wait_ge(sem_pair[0], sem_pair[1])
    vector.tensor_copy(p.sig_sb[:, dst_cols], psum[:, 0:n])
    vector.drain()


def emit_p4(p):
    nc = p.nc
    with ExitStack() as pc:
        sb = lambda nm, sh, d=F32: pc.enter_context(nc.sbuf_tensor(nm, sh, d))
        xqh = sb("p4_xqh", [128, KC, T], BF16)
        xql = sb("p4_xql", [128, KC, T], BF16)
        x1t = [sb(f"p4_x1t{i}", [128, KC, TT // 2]) for i in range(2)]
        z = sb("p4_z", [128, KC * TT // 2]); zi = sb("p4_zi", [128, KC * TT // 2], I32)
        zc = sb("p4_zc", [128, KC * TT // 2]); A = sb("p4_A", [128, KC * TT // 2])
        Bb = sb("p4_B", [128, KC * TT // 2])
        wch = [[sb(f"p4_wc{i}_{wp}", [128, KC, 128], BF16) for wp in range(2)] for i in range(2)]
        stg = [sb(f"p4_stg{i}", [128, TT]) for i in range(2)]
        tmx = sb("p4_tmx", [128, 1])
        ps = [pc.enter_context(nc.psum_tensor(f"p4_ps{i}", [128, TT], F32))
              for i in range(8)]
        slist = []
        def sem(nm):
            h = nc.alloc_semaphore(name=nm)
            slist.append(h)
            return h
        cc = sem("p4_cc"); ccc = sem("p4_ccc"); bc = sem("p4_bc")
        xld = [sem(f"p4_xld{i}") for i in range(2)]
        xq = sem("p4_xq")
        wld = [sem(f"p4_wld{i}") for i in range(2)]
        mmg = sem("p4_mmg")
        evg = sem("p4_evg")
        sdg = [sem(f"p4_sdg{i}") for i in range(2)]
        NHT = 2 * NTT
        HC = KC * TT // 2
        with nc.Block() as blk:
            @blk.gpsimd
            def _(g):
                g.wait_ge(bc, 2)
                g.dma_start(out=p.cc_in.ap(), in_=p.ccrow[:]).then_inc(cc, 16)
                g.wait_ge(cc, 16)
                g.collective_compute(
                    "AllReduce", OP.max,
                    replica_groups=[CC_GROUP or list(range(NCORES))],
                    ins=[p.cc_in.ap()], outs=[p.cc_out.ap()]).then_inc(ccc, 1)
                g.wait_ge(ccc, 1)
                g.dma_start(out=p.ccrow[:], in_=p.cc_out.ap()).then_inc(cc, 16)
                g.wait_ge(cc, 32)
                g.nop().then_inc(ccc, 1)

            @blk.tensor
            def _(tensor):
                tensor.transpose(ps[0][0:1, 0:128], p.sig_sb[:, 0:1],
                                 p.ident_sb[:]).then_inc(bc, 1)
                tensor.wait_ge(bc, 3)
                tensor.matmul(ps[1][:, 0:2], p.ones_sb[:], p.row_sb[:, 0:2],
                              start=True, stop=True).then_inc(bc, 1)
                tensor.wait_ge(xq, NHT)
                ng = 0
                for wsel in range(2):
                    for mc in range(MC3):
                        n = wsel * MC3 + mc
                        tensor.wait_ge(wld[n % 2], (n // 2 + 1) * 32)
                        for tg in range(8):
                            if ng >= 8:
                                tensor.wait_ge(evg, ng - 7)
                            k3 = 0
                            for (xs, wp) in ((xqh, 0), (xql, 0), (xqh, 1)):
                                for kc in range(KC):
                                    ins = tensor.matmul(
                                        ps[ng % 8][:],
                                        wch[n % 2][wp][:, kc, :],
                                        xs[:, kc, tg * TT:(tg + 1) * TT],
                                        start=(k3 == 0), stop=(k3 == 3 * KC - 1))
                                    k3 += 1
                            ins.then_inc(mmg, 1)
                            ng += 1

            @blk.vector
            def _(vector):
                vector.wait_ge(bc, 1)
                vector.tensor_copy(p.ccrow[0:1, 0:128], ps[0][0:1, 0:128])
                vector.drain()
                vector.nop().then_inc(bc, 1)
                vector.wait_ge(ccc, 2)
                vector.reduce_max(p.row_sb[0:1, 0:1], p.ccrow[0:1, 0:128], axis=AX)
                vector.drain()
                vector.tensor_scalar(p.row_sb[0:1, 0:1], p.row_sb[0:1, 0:1],
                                     1.0 / QMAX, 1e-12, op0=OP.mult, op1=OP.max)
                vector.drain()
                vector.reciprocal(p.row_sb[0:1, 1:2], p.row_sb[0:1, 0:1])
                vector.drain()
                vector.nop().then_inc(bc, 1)
                vector.wait_ge(bc, 4)
                vector.tensor_copy(p.sig_sb[:, 1:3], ps[1][:, 0:2])
                vector.drain()
                vector.nop().then_inc(bc, 1)
                for h in range(NHT):
                    vector.wait_ge(xld[h % 2], (h // 2 + 1) * 16)
                    vector.wait_ge(bc, 5)
                    xv = x1t[h % 2][:].rearrange("q c t -> q (c t)")
                    vector.tensor_scalar_mul(z[:], xv, p.sig_sb[:, 2:3])
                    vector.drain()
                    vector.tensor_copy(zi[:], z[:])
                    vector.drain()
                    vector.tensor_copy(zc[:], zi[:])
                    vector.drain()
                    vector.scalar_tensor_tensor(A[:], zc[:], -0.5, z[:],
                                                op0=OP.add, op1=OP.is_le)
                    vector.scalar_tensor_tensor(Bb[:], zc[:], 0.5, z[:],
                                                op0=OP.add, op1=OP.is_le)
                    vector.drain()
                    vector.tensor_add(z[:], zc[:], A[:])
                    vector.drain()
                    vector.tensor_add(z[:], z[:], Bb[:])
                    vector.drain()
                    vector.tensor_scalar(zc[:], z[:], -1.0, p.sig_sb[:, 1:2],
                                         op0=OP.add, op1=OP.mult)
                    vector.drain()
                    ho = (h // 2) * TT + (h % 2) * (TT // 2)
                    dsth = xqh[:, :, ho:ho + TT // 2]
                    dstl = xql[:, :, ho:ho + TT // 2]
                    zc3 = zc[:].rearrange("q (c t) -> q c t", c=KC)
                    vector.tensor_copy(dsth, zc3)
                    vector.drain()
                    vector.scalar_tensor_tensor(dstl, zc3, 1.0, dsth,
                                                op0=OP.mult, op1=OP.subtract)
                    vector.drain()
                    vector.nop().then_inc(xq, 1)
                ng = 0
                for wsel in range(2):
                    mxcol = 8 + wsel
                    for mc in range(MC3):
                        for tg in range(8):
                            vector.wait_ge(mmg, ng + 1)
                            if ng >= 2:
                                vector.wait_ge(sdg[ng % 2], (ng // 2) * 16)
                            vector.tensor_copy(stg[ng % 2][:], ps[ng % 8][:])
                            vector.drain()
                            vector.reduce_max(tmx[:], stg[ng % 2][:], axis=AX,
                                              apply_absolute_value=True)
                            vector.drain()
                            vector.tensor_max(p.maxacc[:, mxcol:mxcol + 1],
                                              p.maxacc[:, mxcol:mxcol + 1],
                                              tmx[:])
                            vector.drain()
                            vector.nop().then_inc(evg, 1)
                            ng += 1

            @blk.sync
            def _(sync):
                for h in range(NHT):
                    if h >= 2:
                        sync.wait_ge(xq, h - 1)
                    base = h * (TT // 2)
                    sync.dma_start(
                        out=x1t[h % 2][:],
                        in_=p.x1[:, :, base:base + TT // 2].rearrange(
                            "c q t -> q c t")).then_inc(xld[h % 2], 16)
                ng = 0
                for wsel in range(2):
                    hi_s, lo_s = ((p.wupT_hi, p.wupT_lo),
                                  (p.wgT_hi, p.wgT_lo))[wsel]
                    dstscr = (p.upraw, p.gateraw)[wsel]
                    for mc in range(MC3):
                        n = wsel * MC3 + mc
                        if n >= 2:
                            sync.wait_ge(mmg, (n - 1) * 8)
                        for wp, srcw in ((0, hi_s), (1, lo_s)):
                            sync.dma_start(
                                out=wch[n % 2][wp][:],
                                in_=srcw[:, :, mc * 128:(mc + 1) * 128].rearrange(
                                    "k q m -> q k m")).then_inc(wld[n % 2], 16)
                        for tg in range(8):
                            sync.wait_ge(evg, ng + 1)
                            sync.dma_start(
                                out=dstscr[mc, :, tg * TT:(tg + 1) * TT],
                                in_=stg[ng % 2][:]).then_inc(sdg[ng % 2], 16)
                            ng += 1
                sync.wait_ge(sdg[0], 2 * MC3 * 8 // 2 * 16)
                sync.wait_ge(sdg[1], 2 * MC3 * 8 // 2 * 16)
        nc.clear_and_free_semaphores(slist)

def _cc_parts(p, nfold, dst2=None, prescale=1.0):
    K = 3 if dst2 is not None else 2
    def gp_fn(g, cc, ccc, bc):
        g.wait_ge(bc, 2)
        g.dma_start(out=p.cc_in.ap(), in_=p.ccrow[:]).then_inc(cc, 16)
        g.wait_ge(cc, 16)
        g.collective_compute(
            "AllReduce", OP.max,
            replica_groups=[CC_GROUP or list(range(NCORES))],
            ins=[p.cc_in.ap()], outs=[p.cc_out.ap()]).then_inc(ccc, 1)
        g.wait_ge(ccc, 1)
        g.dma_start(out=p.ccrow[:], in_=p.cc_out.ap()).then_inc(cc, 16)
        g.wait_ge(cc, 32)
        g.nop().then_inc(ccc, 1)

    def tensor_fn(tensor, psum, mcols, bc):
        tensor.transpose(psum[0:nfold, 0:128],
                         p.maxacc[:, mcols[0]:mcols[0] + nfold],
                         p.ident_sb[:]).then_inc(bc, 1)
        tensor.wait_ge(bc, 3)
        tensor.matmul(psum[:, 0:K * nfold], p.ones_sb[:],
                      p.row_sb[:, 4:4 + K * nfold],
                      start=True, stop=True).then_inc(bc, 1)

    def vector_fn(vector, psum, dst, cc, ccc, bc):
        vector.wait_ge(bc, 1)
        vector.tensor_copy(p.ccrow[0:nfold, 0:128], psum[0:nfold, 0:128])
        vector.drain()
        if nfold == 1:
            vector.memset(p.ccrow[1:2, 0:128], 0.0) if False else None
        vector.nop().then_inc(bc, 1)
        vector.wait_ge(ccc, 2)
        vector.reduce_max(p.row_sb[0:nfold, 0:1], p.ccrow[0:nfold, 0:128],
                          axis=AX)
        vector.drain()
        vector.tensor_scalar(p.row_sb[0:nfold, 0:1], p.row_sb[0:nfold, 0:1],
                             prescale / QMAX, 1e-12, op0=OP.mult, op1=OP.max)
        vector.drain()
        vector.reciprocal(p.row_sb[0:nfold, 1:2], p.row_sb[0:nfold, 0:1])
        vector.drain()
        if prescale != 1.0:
            vector.tensor_scalar_mul(p.row_sb[0:nfold, 1:2],
                                     p.row_sb[0:nfold, 1:2], prescale)
            vector.drain()
        if dst2 is not None:
            vector.tensor_scalar_mul(p.row_sb[0:nfold, 2:3],
                                     p.row_sb[0:nfold, 0:1], -32769.0)
            vector.drain()
        vector.tensor_tensor(
            p.row_sb[0:nfold, 4:4 + nfold],
            p.row_sb[0:nfold, 0:1].broadcast_to([nfold, nfold]),
            p.ident_sb[0:nfold, 0:nfold], op=OP.mult)
        vector.tensor_tensor(
            p.row_sb[0:nfold, 4 + nfold:4 + 2 * nfold],
            p.row_sb[0:nfold, 1:2].broadcast_to([nfold, nfold]),
            p.ident_sb[0:nfold, 0:nfold], op=OP.mult)
        if dst2 is not None:
            vector.tensor_tensor(
                p.row_sb[0:nfold, 4 + 2 * nfold:4 + 3 * nfold],
                p.row_sb[0:nfold, 2:3].broadcast_to([nfold, nfold]),
                p.ident_sb[0:nfold, 0:nfold], op=OP.mult)
        vector.drain()
        vector.nop().then_inc(bc, 1)
        vector.wait_ge(bc, 4)
        vector.tensor_copy(p.sig_sb[:, dst:dst + 2 * nfold],
                           psum[:, 0:2 * nfold])
        if dst2 is not None:
            vector.tensor_copy(p.sig_sb[:, dst2:dst2 + nfold],
                               psum[:, 2 * nfold:3 * nfold])
        vector.drain()
        vector.nop().then_inc(bc, 1)

    return gp_fn, tensor_fn, vector_fn


def _ladder(vector, z, zi, zc, A, Bb, n):
    vector.tensor_copy(zi[:, 0:n], z[:, 0:n])
    vector.drain()
    vector.tensor_copy(zc[:, 0:n], zi[:, 0:n])
    vector.drain()
    vector.scalar_tensor_tensor(A[:, 0:n], zc[:, 0:n], -0.5, z[:, 0:n],
                                op0=OP.add, op1=OP.is_le)
    vector.scalar_tensor_tensor(Bb[:, 0:n], zc[:, 0:n], 0.5, z[:, 0:n],
                                op0=OP.add, op1=OP.is_le)
    vector.drain()
    vector.tensor_add(z[:, 0:n], zc[:, 0:n], A[:, 0:n])
    vector.drain()
    vector.tensor_add(z[:, 0:n], z[:, 0:n], Bb[:, 0:n])
    vector.drain()


def emit_p5(p):
    nc = p.nc
    HN = 4 * MC3
    HC = T // 4
    gp_fn, t_fn, v_fn = _cc_parts(p, 2, dst2=11)
    with ExitStack() as pc:
        sb = lambda nm, sh, d=F32: pc.enter_context(nc.sbuf_tensor(nm, sh, d))
        upb = [sb(f"p5_up{i}", [128, HC]) for i in range(2)]
        gtb = [sb(f"p5_gt{i}", [128, HC]) for i in range(2)]
        yub = [sb(f"p5_yu{i}", [128, HC]) for i in range(2)]
        ygb = [sb(f"p5_yg{i}", [128, HC]) for i in range(2)]
        zi = sb("p5_zi", [128, HC], I32)
        zcu = [sb(f"p5_zcu{i}", [128, HC]) for i in range(2)]
        Au = [sb(f"p5_Au{i}", [128, HC]) for i in range(2)]
        zcg = [sb(f"p5_zcg{i}", [128, HC]) for i in range(2)]
        Ag = [sb(f"p5_Ag{i}", [128, HC]) for i in range(2)]
        lup = [sb(f"p5_lup{i}", [128, HC]) for i in range(2)]
        lgp = [sb(f"p5_lgp{i}", [128, HC]) for i in range(2)]
        gqb = [sb(f"p5_gq{i}", [128, HC]) for i in range(2)]
        sgb = sb("p5_sg", [128, HC])
        yfb = [sb(f"p5_yf{i}", [128, HC]) for i in range(2)]
        zcf = [sb(f"p5_zcf{i}", [128, HC]) for i in range(2)]
        Af = [sb(f"p5_Af{i}", [128, HC]) for i in range(2)]
        lfp = [sb(f"p5_lfp{i}", [128, HC]) for i in range(2)]
        t1b = sb("p5_t1", [128, HC])
        mb = [sb(f"p5_m{i}", [128, HC]) for i in range(2)]
        tmx = sb("p5_tmx", [128, 1])
        ps = pc.enter_context(nc.psum_tensor("p5_ps", [128, TT], F32))
        slist = []
        def sem(nm):
            h = nc.alloc_semaphore(name=nm)
            slist.append(h)
            return h
        cc = sem("p5_cc"); ccc = sem("p5_ccc"); bc = sem("p5_bc")
        ld = [sem(f"p5_ld{i}") for i in range(2)]
        aq1 = sem("p5_aq1"); aq2 = sem("p5_aq2")
        vq1 = sem("p5_vq1"); vq2 = sem("p5_vq2"); vq3 = sem("p5_vq3")
        pq1 = sem("p5_pq1"); pq2 = sem("p5_pq2")
        su = [sem(f"p5_su{i}") for i in range(2)]
        sm = [sem(f"p5_sm{i}") for i in range(2)]
        def addr(h):
            return h // 4, (h % 4) * HC
        with nc.Block() as blk:
            @blk.tensor
            def _(tensor):
                t_fn(tensor, ps, [8, 9], bc)

            @blk.scalar
            def _(scalar):
                scalar.wait_ge(bc, 5)
                for r in range(HN + 1):
                    if r < HN:
                        scalar.wait_ge(ld[r % 2], (r // 2 + 1) * 32)
                        if r >= 2:
                            scalar.wait_ge(vq1, r - 1)
                        scalar.activation(yub[r % 2][:], upb[r % 2][:],
                                          AF.Identity, bias=p.b32k[:, 0:1],
                                          scale=p.sig_sb[:, 5:6])
                        scalar.activation(ygb[r % 2][:], gtb[r % 2][:],
                                          AF.Identity, bias=p.b32k[:, 0:1],
                                          scale=p.sig_sb[:, 6:7]).then_inc(aq1, 1)
                    k = r - 1
                    if 0 <= k < HN:
                        scalar.wait_ge(pq1, r)
                        if r >= 3:
                            scalar.wait_ge(vq3, r - 2)
                            scalar.wait_ge(vq2, r - 2)
                        scalar.activation(gqb[k % 2][:], lgp[k % 2][:],
                                          AF.Identity, bias=p.sig_sb[:, 12:13],
                                          scale=p.sig_sb[:, 4:5])
                        scalar.drain()
                        scalar.activation(sgb[:], gqb[k % 2][:], AF.Sigmoid)
                        scalar.drain()
                        scalar.activation(yfb[k % 2][:], sgb[:], AF.Identity,
                                          bias=p.b32k[:, 0:1],
                                          scale=65536.0).then_inc(aq2, 1)

            @blk.vector
            def _(vector):
                v_fn(vector, ps, 3, cc, ccc, bc)
                for r in range(HN + 2):
                    if r < HN:
                        vector.wait_ge(aq1, r + 1)
                        if r >= 2:
                            vector.wait_ge(pq1, r - 1)
                        vector.tensor_copy(zi[:], yub[r % 2][:])
                        vector.drain()
                        vector.tensor_copy(zcu[r % 2][:], zi[:])
                        vector.drain()
                        vector.scalar_tensor_tensor(
                            Au[r % 2][:], zcu[r % 2][:], 0.0, yub[r % 2][:],
                            op0=OP.add, op1=OP.is_le)
                        vector.drain()
                        vector.tensor_copy(zi[:], ygb[r % 2][:])
                        vector.drain()
                        vector.tensor_copy(zcg[r % 2][:], zi[:])
                        vector.drain()
                        vector.scalar_tensor_tensor(
                            Ag[r % 2][:], zcg[r % 2][:], 0.0, ygb[r % 2][:],
                            op0=OP.add, op1=OP.is_le).then_inc(vq1, 1)
                        vector.drain()
                    k = r - 1
                    if 0 <= k < HN:
                        vector.wait_ge(aq2, r)
                        if r >= 3:
                            vector.wait_ge(pq2, r - 2)
                        vector.tensor_copy(zi[:], yfb[k % 2][:])
                        vector.drain()
                        vector.tensor_copy(zcf[k % 2][:], zi[:])
                        vector.drain()
                        vector.scalar_tensor_tensor(
                            Af[k % 2][:], zcf[k % 2][:], 0.0, yfb[k % 2][:],
                            op0=OP.add, op1=OP.is_le).then_inc(vq2, 1)
                        vector.drain()
                    k = r - 2
                    if 0 <= k < HN:
                        vector.wait_ge(pq2, r - 1)
                        if r >= 4:
                            vector.wait_ge(sm[k % 2], (k // 2) * 16)
                        vector.tensor_scalar_min(t1b[:], lfp[k % 2][:], 98304.0)
                        vector.drain()
                        vector.scalar_tensor_tensor(
                            mb[k % 2][:], t1b[:], -32769.0, gqb[k % 2][:],
                            op0=OP.add, op1=OP.mult)
                        vector.drain()
                        vector.reduce_max(tmx[:], mb[k % 2][:], axis=AX,
                                          apply_absolute_value=True)
                        vector.drain()
                        vector.tensor_max(p.maxacc[:, 10:11],
                                          p.maxacc[:, 10:11],
                                          tmx[:]).then_inc(vq3, 1)
                        vector.drain()

            @blk.gpsimd
            def _(g):
                gp_fn(g, cc, ccc, bc)
                for r in range(HN + 1):
                    if r < HN:
                        g.wait_ge(vq1, r + 1)
                        if r >= 2:
                            g.wait_ge(su[r % 2], (r // 2) * 16)
                            g.wait_ge(aq2, r - 1)
                        g.tensor_tensor(lup[r % 2][:], zcu[r % 2][:],
                                        Au[r % 2][:], op=OP.add)
                        g.tensor_tensor(lgp[r % 2][:], zcg[r % 2][:],
                                        Ag[r % 2][:], op=OP.add).then_inc(pq1, 1)
                    k = r - 1
                    if 0 <= k < HN:
                        g.wait_ge(vq2, r)
                        if r >= 3:
                            g.wait_ge(vq3, r - 2)
                        g.tensor_tensor(lfp[k % 2][:], zcf[k % 2][:],
                                        Af[k % 2][:], op=OP.add).then_inc(pq2, 1)

            @blk.sync
            def _(sync):
                for r in range(HN + 2):
                    if r < HN:
                        if r >= 2:
                            sync.wait_ge(aq1, r - 1)
                        mc, co = addr(r)
                        sync.dma_start(out=upb[r % 2][:],
                                       in_=p.upraw[mc, :, co:co + HC]
                                       ).then_inc(ld[r % 2], 16)
                        sync.dma_start(out=gtb[r % 2][:],
                                       in_=p.gateraw[mc, :, co:co + HC]
                                       ).then_inc(ld[r % 2], 16)
                    k = r - 1
                    if 0 <= k < HN:
                        sync.wait_ge(pq1, r)
                        mc, co = addr(k)
                        sync.dma_start(out=p.upq[mc, :, co:co + HC],
                                       in_=lup[k % 2][:]).then_inc(su[k % 2], 16)
                    k = r - 2
                    if 0 <= k < HN:
                        sync.wait_ge(vq3, r - 1)
                        mc, co = addr(k)
                        sync.dma_start(out=p.g2raw[mc, :, co:co + HC],
                                       in_=mb[k % 2][:]).then_inc(sm[k % 2], 16)
                sync.wait_ge(su[0], HN // 2 * 16)
                sync.wait_ge(su[1], HN // 2 * 16)
                sync.wait_ge(sm[0], HN // 2 * 16)
                sync.wait_ge(sm[1], HN // 2 * 16)
        nc.clear_and_free_semaphores(slist)

def emit_p6(p):
    nc = p.nc
    HN = 4 * MC3
    HC = T // 4
    gp_fn, t_fn, v_fn = _cc_parts(p, 1, dst2=13, prescale=2.0 ** -16)
    with ExitStack() as pc:
        sb = lambda nm, sh, d=F32: pc.enter_context(nc.sbuf_tensor(nm, sh, d))
        g2b = [sb(f"p6_g2{i}", [128, HC]) for i in range(2)]
        uqb = [sb(f"p6_uq{i}", [128, HC]) for i in range(2)]
        y2b = [sb(f"p6_y2{i}", [128, HC]) for i in range(2)]
        upd = [sb(f"p6_upd{i}", [128, HC]) for i in range(2)]
        zi = sb("p6_zi", [128, HC], I32)
        zc2 = [sb(f"p6_zc{i}", [128, HC]) for i in range(2)]
        A2 = [sb(f"p6_A{i}", [128, HC]) for i in range(2)]
        l2p = [sb(f"p6_l2{i}", [128, HC]) for i in range(2)]
        g2q = [sb(f"p6_g2q{i}", [128, HC]) for i in range(2)]
        ob = [sb(f"p6_o{i}", [128, HC]) for i in range(2)]
        tmx = sb("p6_tmx", [128, 1])
        ps = pc.enter_context(nc.psum_tensor("p6_ps", [128, TT], F32))
        slist = []
        def sem(nm):
            h = nc.alloc_semaphore(name=nm)
            slist.append(h)
            return h
        cc = sem("p6_cc"); ccc = sem("p6_ccc"); bc = sem("p6_bc")
        ld = [sem(f"p6_ld{i}") for i in range(2)]
        aq1 = sem("p6_aq1"); aq2 = sem("p6_aq2")
        vq1 = sem("p6_vq1"); vq3 = sem("p6_vq3")
        pq1 = sem("p6_pq1"); pq2 = sem("p6_pq2")
        so = [sem(f"p6_so{i}") for i in range(2)]
        def addr(h):
            return h // 4, (h % 4) * HC
        with nc.Block() as blk:
            @blk.tensor
            def _(tensor):
                t_fn(tensor, ps, [10], bc)

            @blk.scalar
            def _(scalar):
                scalar.wait_ge(bc, 5)
                for r in range(HN + 1):
                    if r < HN:
                        scalar.wait_ge(ld[r % 2], (r // 2 + 1) * 32)
                        if r >= 2:
                            scalar.wait_ge(vq1, r - 1)
                            scalar.wait_ge(pq2, r - 1)
                        scalar.activation(y2b[r % 2][:], g2b[r % 2][:],
                                          AF.Identity, bias=p.b32k[:, 0:1],
                                          scale=p.sig_sb[:, 8:9])
                        scalar.activation(upd[r % 2][:], uqb[r % 2][:],
                                          AF.Identity, bias=p.sig_sb[:, 11:12],
                                          scale=p.sig_sb[:, 3:4]).then_inc(aq1, 1)
                    k = r - 1
                    if 0 <= k < HN:
                        scalar.wait_ge(pq1, r)
                        if r >= 3:
                            scalar.wait_ge(pq2, r - 1)
                        scalar.activation(g2q[k % 2][:], l2p[k % 2][:],
                                          AF.Identity, bias=p.sig_sb[:, 13:14],
                                          scale=p.sig_sb[:, 7:8]).then_inc(aq2, 1)

            @blk.vector
            def _(vector):
                v_fn(vector, ps, 7, cc, ccc, bc)
                for r in range(HN + 2):
                    if r < HN:
                        vector.wait_ge(aq1, r + 1)
                        if r >= 2:
                            vector.wait_ge(pq1, r - 1)
                        vector.tensor_copy(zi[:], y2b[r % 2][:])
                        vector.drain()
                        vector.tensor_copy(zc2[r % 2][:], zi[:])
                        vector.drain()
                        vector.scalar_tensor_tensor(
                            A2[r % 2][:], zc2[r % 2][:], 0.0, y2b[r % 2][:],
                            op0=OP.add, op1=OP.is_le).then_inc(vq1, 1)
                        vector.drain()
                    k = r - 2
                    if 0 <= k < HN:
                        vector.wait_ge(pq2, r - 1)
                        vector.reduce_max(tmx[:], ob[k % 2][:], axis=AX,
                                          apply_absolute_value=True)
                        vector.drain()
                        vector.tensor_max(p.maxacc[:, 11:12],
                                          p.maxacc[:, 11:12],
                                          tmx[:]).then_inc(vq3, 1)
                        vector.drain()

            @blk.gpsimd
            def _(g):
                gp_fn(g, cc, ccc, bc)
                for r in range(HN + 1):
                    if r < HN:
                        g.wait_ge(vq1, r + 1)
                        if r >= 2:
                            g.wait_ge(aq2, r - 1)
                        g.tensor_tensor(l2p[r % 2][:], zc2[r % 2][:],
                                        A2[r % 2][:], op=OP.add).then_inc(pq1, 1)
                    k = r - 1
                    if 0 <= k < HN:
                        g.wait_ge(aq2, r)
                        if r >= 3:
                            g.wait_ge(vq3, r - 2)
                            g.wait_ge(so[k % 2], (k // 2) * 16)
                        g.tensor_tensor(ob[k % 2][:], g2q[k % 2][:],
                                        upd[k % 2][:], op=OP.mult).then_inc(pq2, 1)

            @blk.sync
            def _(sync):
                for r in range(HN + 2):
                    if r < HN:
                        if r >= 2:
                            sync.wait_ge(aq1, r - 1)
                        mc, co = addr(r)
                        sync.dma_start(out=g2b[r % 2][:],
                                       in_=p.g2raw[mc, :, co:co + HC]
                                       ).then_inc(ld[r % 2], 16)
                        sync.dma_start(out=uqb[r % 2][:],
                                       in_=p.upq[mc, :, co:co + HC]
                                       ).then_inc(ld[r % 2], 16)
                    k = r - 2
                    if 0 <= k < HN:
                        sync.wait_ge(vq3, r - 1)
                        mc, co = addr(k)
                        sync.dma_start(out=p.oraw[mc, :, co:co + HC],
                                       in_=ob[k % 2][:]).then_inc(so[k % 2], 16)
                sync.wait_ge(so[0], HN // 2 * 16)
                sync.wait_ge(so[1], HN // 2 * 16)
        nc.clear_and_free_semaphores(slist)

def emit_p7(p):
    nc = p.nc
    gp_fn, t_fn, v_fn = _cc_parts(p, 1)
    T7 = 256
    NT7 = T // T7
    NQG = 6
    QW = 4 * T7
    with ExitStack() as pc:
        sb = lambda nm, sh, d=F32: pc.enter_context(nc.sbuf_tensor(nm, sh, d))
        orb = [sb(f"p7_or{i}", [128, MC3, T7]) for i in range(2)]
        oph = [sb(f"p7_oph{i}", [128, MC3, T7], BF16) for i in range(2)]
        opl = [sb(f"p7_opl{i}", [128, MC3, T7], BF16) for i in range(2)]
        z = sb("p7_z", [128, QW]); zi = sb("p7_zi", [128, QW], I32)
        zc = sb("p7_zc", [128, QW]); A = sb("p7_A", [128, QW])
        Bb = sb("p7_B", [128, QW])
        wdc = [[sb(f"p7_wd{i}_{wp}", [128, MC3, 128], BF16) for wp in range(2)]
               for i in range(2)]
        x1c = sb("p7_x1", [128, KC, T7])
        xr2 = sb("p7_xr2", [128, KC, T7])
        sq2 = sb("p7_sq2", [128, KC, T7])
        outst = [sb(f"p7_out{i}", [128, 2, DM]) for i in range(2)]
        ps = [pc.enter_context(nc.psum_tensor(f"p7_ps{i}", [128, T7], F32))
              for i in range(6)]
        m_sb = z[:, 0:T7]
        msq = z[:, T7:2 * T7]
        v_sb = zc[:, 0:T7]
        sd_sb = zc[:, T7:2 * T7]
        slist = []
        def sem(nm):
            h = nc.alloc_semaphore(name=nm)
            slist.append(h)
            return h
        cc = sem("p7_cc"); ccc = sem("p7_ccc"); bc = sem("p7_bc")
        old = [sem(f"p7_old{i}") for i in range(2)]
        x1l = sem("p7_x1l")
        wls = [sem(f"p7_wl{i}") for i in range(2)]
        qg = sem("p7_qg")
        mmg = sem("p7_mmg")
        evg = sem("p7_evg")
        LT1 = sem("p7_LT1"); LT2 = sem("p7_LT2"); LSq = sem("p7_LSq")
        LNv = sem("p7_LNv"); LNs = sem("p7_LNs"); LNx = sem("p7_LNx")
        otr = sem("p7_otr"); oev = sem("p7_oev")
        osd = [sem(f"p7_osd{i}") for i in range(2)]
        with nc.Block() as blk:
            @blk.gpsimd
            def _(g):
                gp_fn(g, cc, ccc, bc)

            @blk.tensor
            def _(tensor):
                t_fn(tensor, ps[2], [11], bc)
                ntr = 0
                for i in range(NT7):
                    tensor.wait_ge(qg, (i + 1) * NQG)
                    for mc in range(KC):
                        n = i * KC + mc
                        tensor.wait_ge(wls[n % 2], (n // 2 + 1) * 32)
                        if n >= 2:
                            tensor.wait_ge(evg, n - 1)
                        k3 = 0
                        for (xs, wp) in ((oph, 0), (opl, 0), (oph, 1)):
                            for kc in range(MC3):
                                ins = tensor.matmul(
                                    ps[n % 2][:],
                                    wdc[n % 2][wp][:, kc, :],
                                    xs[i % 2][:, kc, :],
                                    start=(k3 == 0), stop=(k3 == 3 * MC3 - 1))
                                k3 += 1
                        ins.then_inc(mmg, 1)
                    tensor.wait_ge(evg, (i + 1) * KC)
                    for kc in range(KC):
                        ins = tensor.matmul(ps[2][:], p.ones_sb[:],
                                            xr2[:, kc, :],
                                            start=(kc == 0), stop=(kc == KC - 1))
                    ins.then_inc(LT1, 1)
                    tensor.wait_ge(LSq, i + 1)
                    for kc in range(KC):
                        ins = tensor.matmul(ps[3][:], p.ones_sb[:],
                                            sq2[:, kc, :],
                                            start=(kc == 0), stop=(kc == KC - 1))
                    ins.then_inc(LT2, 1)
                    tensor.wait_ge(LNx, i + 1)
                    for s4 in range(2):
                        for kc in range(KC):
                            if ntr >= 2:
                                tensor.wait_ge(oev, ntr - 1)
                            tensor.transpose(
                                ps[4 + ntr % 2][0:128, 0:128],
                                xr2[:, kc, s4 * 128:(s4 + 1) * 128],
                                p.ident_sb[:]).then_inc(otr, 1)
                            ntr += 1

            @blk.scalar
            def _(scalar):
                for i in range(NT7):
                    scalar.wait_ge(evg, (i + 1) * KC)
                    if i >= 1:
                        scalar.wait_ge(LT2, i)
                        scalar.wait_ge(osd[(i - 1) % 2], ((i - 1) // 2 + 1) * 32)
                    scalar.activation(sq2[:], xr2[:], AF.Square).then_inc(LSq, 1)
                    scalar.drain()
                    scalar.wait_ge(LNv, i + 1)
                    scalar.activation(sd_sb, v_sb, AF.Sqrt, bias=p.eps_sb[:]).then_inc(LNs, 1)
                    scalar.drain()

            @blk.vector
            def _(vector):
                v_fn(vector, ps[2], 9, cc, ccc, bc)
                ntr = 0
                for i in range(NT7):
                    vector.wait_ge(old[i % 2], (i // 2 + 1) * 16)
                    for qv in range(NQG):
                        src = orb[i % 2][:, 4 * qv:4 * qv + 4, :].rearrange(
                            "q c t -> q (c t)")
                        vector.tensor_scalar_mul(z[:], src, p.sig_sb[:, 10:11])
                        vector.drain()
                        _ladder(vector, z, zi, zc, A, Bb, QW)
                        vector.tensor_scalar(zc[:], z[:], -1.0,
                                             p.sig_sb[:, 9:10],
                                             op0=OP.add, op1=OP.mult)
                        vector.drain()
                        zc3 = zc[:].rearrange("q (c t) -> q c t", c=4)
                        dh = oph[i % 2][:, 4 * qv:4 * qv + 4, :]
                        dl = opl[i % 2][:, 4 * qv:4 * qv + 4, :]
                        vector.tensor_copy(dh, zc3)
                        vector.drain()
                        vector.scalar_tensor_tensor(dl, zc3, 1.0, dh,
                                                    op0=OP.mult, op1=OP.subtract)
                        vector.drain()
                        vector.nop().then_inc(qg, 1)
                    vector.wait_ge(x1l, (i + 1) * 16)
                    for mc in range(KC):
                        n = i * KC + mc
                        vector.wait_ge(mmg, n + 1)
                        vector.tensor_add(xr2[:, mc, :], ps[n % 2][:],
                                          x1c[:, mc, :])
                        vector.drain()
                        vector.nop().then_inc(evg, 1)
                    vector.wait_ge(LT1, i + 1)
                    vector.tensor_scalar_mul(m_sb, ps[2][:], 1.0 / DM)
                    vector.drain()
                    vector.tensor_mul(msq, m_sb, m_sb)
                    vector.drain()
                    vector.wait_ge(LT2, i + 1)
                    vector.scalar_tensor_tensor(v_sb, ps[3][:], 1.0 / DM, msq,
                                                op0=OP.mult, op1=OP.subtract)
                    vector.drain()
                    vector.nop().then_inc(LNv, 1)
                    vector.wait_ge(LNs, i + 1)
                    vector.reciprocal(sd_sb, sd_sb)
                    vector.drain()
                    vector.tensor_tensor(
                        xr2[:], xr2[:],
                        m_sb.rearrange("q t -> q () t").broadcast_to([128, KC, T7]),
                        op=OP.subtract)
                    vector.drain()
                    vector.tensor_tensor(
                        xr2[:], xr2[:],
                        sd_sb.rearrange("q t -> q () t").broadcast_to([128, KC, T7]),
                        op=OP.mult)
                    vector.drain()
                    vector.nop().then_inc(LNx, 1)
                    if i >= 2:
                        vector.wait_ge(osd[i % 2], (i // 2) * 32)
                    for s4 in range(2):
                        for kc in range(KC):
                            vector.wait_ge(otr, ntr + 1)
                            vector.tensor_copy(
                                outst[i % 2][:, s4, kc * 128:(kc + 1) * 128],
                                ps[4 + ntr % 2][0:128, 0:128])
                            vector.drain()
                            vector.nop().then_inc(oev, 1)
                            ntr += 1

            @blk.sync
            def _(sync):
                for i in range(NT7):
                    if i >= 2:
                        sync.wait_ge(qg, (i - 1) * NQG)
                    sync.dma_start(
                        out=orb[i % 2][:],
                        in_=p.oraw[:, :, i * T7:(i + 1) * T7].rearrange(
                            "c q t -> q c t")).then_inc(old[i % 2], 16)
                    sync.wait_ge(evg, i * KC)
                    sync.dma_start(
                        out=x1c[:],
                        in_=p.x1[:, :, i * T7:(i + 1) * T7].rearrange(
                            "c q t -> q c t")).then_inc(x1l, 16)
                    for mc in range(KC):
                        n = i * KC + mc
                        if n >= 2:
                            sync.wait_ge(mmg, n - 1)
                        for wp, srcw in ((0, p.wdT_hi), (1, p.wdT_lo)):
                            sync.dma_start(
                                out=wdc[n % 2][wp][:],
                                in_=srcw[:, :, mc * 128:(mc + 1) * 128].rearrange(
                                    "k q m -> q k m")).then_inc(wls[n % 2], 16)
                    if i >= 1:
                        for s4 in range(2):
                            sync.wait_ge(oev, (i - 1) * 16 + (s4 + 1) * KC)
                            r0 = (i - 1) * T7 + s4 * 128
                            sync.dma_start(
                                out=p.out[r0:r0 + 128, :],
                                in_=outst[(i - 1) % 2][:, s4, :]
                                ).then_inc(osd[(i - 1) % 2], 16)
                for s4 in range(2):
                    sync.wait_ge(oev, (NT7 - 1) * 16 + (s4 + 1) * KC)
                    r0 = (NT7 - 1) * T7 + s4 * 128
                    sync.dma_start(out=p.out[r0:r0 + 128, :],
                                   in_=outst[(NT7 - 1) % 2][:, s4, :]
                                   ).then_inc(osd[(NT7 - 1) % 2], 16)
                sync.wait_ge(osd[0], NT7 // 2 * 32)
                sync.wait_ge(osd[1], NT7 // 2 * 32)
        nc.clear_and_free_semaphores(slist)

def emit_debug(p):
    nc = p.nc
    if not p.debug_taps:
        return
    srcs = dict(xf=p.xf, qf=p.qf, kf=p.kf, af=p.af, vt=p.vt, x1=p.x1,
                upraw=p.upraw, gateraw=p.gateraw, upq=p.upq, g2raw=p.g2raw,
                oraw=p.oraw)
    with ExitStack() as pc:
        st = nc.alloc_semaphore(name="dbg_st")
        with nc.Block() as blk:
            @blk.sync
            def _(sync):
                n = 0
                for name, ap in p.debug_taps.items():
                    sync.dma_start(out=ap, in_=srcs[name]).then_inc(st, 16)
                    n += 1
                sync.wait_ge(st, n * 16)


def make_consts():
    ident = np.eye(128, dtype=np.float32)
    ones = np.ones((128, 128), dtype=np.float32)
    cmask = np.zeros((128, 64), dtype=np.float32)
    for u in range(2):
        for t in range(64):
            cmask[u * 64 + t, t + 1:] = NEG
    tmask = np.zeros((128, 64), dtype=np.float32)
    for pr in range(128):
        sub_s, s_ = (pr % 64) // 32, pr % 32
        for col in range(64):
            sub_t, tau = col // 32, col % 32
            if sub_s == sub_t and s_ < tau:
                tmask[pr, col] = 1.0
    return ident, ones, cmask, tmask


def make_in_maps(x, W):
    ident, ones, cmask, tmask = make_consts()
    shared = dict(W)
    shared.update(ident=ident, ones=ones, cmask=cmask, tmask=tmask)
    in_maps = []
    for c in range(NCORES):
        m = dict(shared)
        m["x"] = np.ascontiguousarray(
            np.asarray(x[c * B_PER:(c + 1) * B_PER], dtype=np.float32
                       ).reshape(T, DM))
        in_maps.append(m)
    return in_maps


def kernel(L0_attn_x, L0_attn_Wq, L0_attn_bq, L0_attn_Wk, L0_attn_bk,
           L0_attn_Wv, L0_attn_bv, L0_attn_Wg1, L0_attn_bg1, L0_attn_Wg2,
           L0_attn_bg2, L0_ffn_w_gate, L0_ffn_w_up, L0_ffn_w_down):
    W = dict(
        Wq=np.ascontiguousarray(L0_attn_Wq), Wk=np.ascontiguousarray(L0_attn_Wk),
        Wv=np.ascontiguousarray(L0_attn_Wv),
        bq=np.ascontiguousarray(L0_attn_bq), bk=np.ascontiguousarray(L0_attn_bk),
        bv=np.ascontiguousarray(L0_attn_bv),
        Wg1=np.ascontiguousarray(L0_attn_Wg1), bg1=np.ascontiguousarray(L0_attn_bg1),
        Wg2=np.ascontiguousarray(L0_attn_Wg2), bg2=np.ascontiguousarray(L0_attn_bg2),
        wgate=np.ascontiguousarray(L0_ffn_w_gate), wup=np.ascontiguousarray(L0_ffn_w_up),
        wdown=np.ascontiguousarray(L0_ffn_w_down))
    nc, p = build()
    in_maps = make_in_maps(np.asarray(L0_attn_x), W)
    res = run_bass_kernel_spmd(nc, in_maps, core_ids=list(range(NCORES)))
    outs = [np.asarray(res.results[c]["out"]).reshape(B_PER, L, DM)
            for c in range(NCORES)]
    return np.concatenate(outs, axis=0)


# version 03
# speedup vs baseline: 1.0242x; 1.0081x over previous
import numpy as np
from contextlib import ExitStack

import concourse.bass as bass
import concourse.mybir as mybir
from concourse.bass_utils import run_bass_kernel_spmd

F32 = mybir.dt.float32
BF16 = mybir.dt.bfloat16
I32 = mybir.dt.int32
OP = mybir.AluOpType
AF = mybir.ActivationFunctionType
AX = mybir.AxisListType.X

NCORES = 8
B_PER = 2
L = 2048
DM = 1024
T = B_PER * L
TT = 512
NTT = T // TT
KC = DM // 128
H = 16
D = 64
C = 64
CS = 32
NCH = L // C
I3 = 3072
MC3 = I3 // 128
QMAX = 32767.0
NEG = -1.0e30
PHASES = 8
P2_STEPS = None
P2_SKIP = set()
CC_GROUP = None


class P:
    pass


def build(phases=PHASES, debug_taps=()):
    nc = bass.Bass("TRN2", target_bir_lowering=False)
    p = P()
    p.nc = nc
    p.phases = phases
    dt = nc.dram_tensor

    p.x = dt("x", [T, DM], F32, kind="ExternalInput").ap()
    p.Wq = dt("Wq", [DM, DM], F32, kind="ExternalInput").ap()
    p.Wk = dt("Wk", [DM, DM], F32, kind="ExternalInput").ap()
    p.Wv = dt("Wv", [DM, DM], F32, kind="ExternalInput").ap()
    p.bq = dt("bq", [DM], F32, kind="ExternalInput").ap()
    p.bk = dt("bk", [DM], F32, kind="ExternalInput").ap()
    p.bv = dt("bv", [DM], F32, kind="ExternalInput").ap()
    p.Wg1 = dt("Wg1", [DM, 16], F32, kind="ExternalInput").ap()
    p.bg1 = dt("bg1", [16], F32, kind="ExternalInput").ap()
    p.Wg2 = dt("Wg2", [16, DM], F32, kind="ExternalInput").ap()
    p.bg2 = dt("bg2", [DM], F32, kind="ExternalInput").ap()
    p.wgate = dt("wgate", [I3, DM], F32, kind="ExternalInput").ap()
    p.wup = dt("wup", [I3, DM], F32, kind="ExternalInput").ap()
    p.wdown = dt("wdown", [DM, I3], F32, kind="ExternalInput").ap()
    p.ident = dt("ident", [128, 128], F32, kind="ExternalInput").ap()
    p.ones = dt("ones", [128, 128], F32, kind="ExternalInput").ap()
    p.cmask = dt("cmask", [128, 64], F32, kind="ExternalInput").ap()
    p.tmask = dt("tmask", [128, 64], F32, kind="ExternalInput").ap()

    p.out = dt("out", [T, DM], F32, kind="ExternalOutput").ap()

    p.xf = dt("xf_scr", [KC, 128, T], F32).ap()
    p.qf = dt("qf_scr", [KC, 128, T], F32).ap()
    p.kf = dt("kf_scr", [KC, 128, T], F32).ap()
    p.af = dt("af_scr", [KC, 128, T], F32).ap()
    p.vt = dt("vt_scr", [T, DM], F32).ap()
    p.x1 = dt("x1_scr", [KC, 128, T], F32).ap()
    p.upraw = dt("upraw_scr", [MC3, 128, T], F32).ap()
    p.gateraw = dt("gateraw_scr", [MC3, 128, T], F32).ap()
    p.upq = dt("upq_scr", [MC3, 128, T], F32).ap()
    p.g2raw = dt("g2raw_scr", [MC3, 128, T], F32).ap()
    p.oraw = dt("oraw_scr", [MC3, 128, T], F32).ap()
    p.wupT_hi = dt("wupT_hi", [KC, 128, I3], BF16).ap()
    p.wupT_lo = dt("wupT_lo", [KC, 128, I3], BF16).ap()
    p.wgT_hi = dt("wgT_hi", [KC, 128, I3], BF16).ap()
    p.wgT_lo = dt("wgT_lo", [KC, 128, I3], BF16).ap()
    p.wdT_hi = dt("wdT_hi", [MC3, 128, DM], BF16).ap()
    p.wdT_lo = dt("wdT_lo", [MC3, 128, DM], BF16).ap()
    p.cc_in = dt("cc_in", [2, 128], F32)
    p.cc_out = dt("cc_out", [2, 128], F32)

    p.debug_taps = {}
    for name, shape in debug_taps:
        p.debug_taps[name] = dt("dbg_" + name, list(shape), F32,
                                kind="ExternalOutput").ap()

    with ExitStack() as ctx:
        p.semctx = ctx
        sbt = lambda nm, sh, d=F32: ctx.enter_context(nc.sbuf_tensor(nm, sh, d))
        p.ident_sb = sbt("ident_sb", [128, 128])
        p.ones_sb = sbt("ones_sb", [128, 128])
        p.cmask_sb = sbt("cmask_sb", [128, 64])
        p.tmask_sb = sbt("tmask_sb", [128, 64])
        p.bq_sb = sbt("bq_sb", [128, KC])
        p.bk_sb = sbt("bk_sb", [128, KC])
        p.bg1_sb = sbt("bg1_sb", [16, 1])
        p.bg2_sb = sbt("bg2_sb", [128, KC])
        p.bvrow = sbt("bvrow", [128, DM])
        p.maxacc = sbt("maxacc", [128, 16])
        p.sig_sb = sbt("sig_sb", [128, 16])
        p.row_sb = sbt("row_sb", [128, 16])
        p.ccrow = sbt("ccrow", [2, 128])
        p.eps_sb = sbt("eps_sb", [128, 1])
        p.b32k = sbt("b32k", [128, 1])

        emit_setup(p)
        if phases >= 1:
            emit_p0(p)
        if phases >= 2:
            emit_p1(p)
        if phases >= 3:
            emit_p2(p)
        if phases >= 4:
            emit_p3(p)
        if phases >= 5:
            emit_p4(p)
        if phases >= 6:
            emit_p5(p)
        if phases >= 7:
            emit_p6(p)
        if phases >= 8:
            emit_p7(p)
        emit_debug(p)
    return nc, p


def emit_setup(p):
    nc = p.nc
    with nc.semaphore("su_ld") as ld, nc.semaphore("su_ms") as ms, nc.Block() as blk:
        @blk.sync
        def _(sync):
            n = 0
            def d(out, in_):
                nonlocal n
                with nc.allow_non_contiguous_dma(reason="tiny one-time bias loads"):
                    sync.dma_start(out=out, in_=in_).then_inc(ld, 16)
                n += 1
            d(p.ident_sb[:], p.ident)
            d(p.ones_sb[:], p.ones)
            d(p.cmask_sb[:], p.cmask)
            d(p.tmask_sb[:], p.tmask)
            d(p.bq_sb[:], p.bq.rearrange("(c q) -> q c", q=128))
            d(p.bk_sb[:], p.bk.rearrange("(c q) -> q c", q=128))
            d(p.bg2_sb[:], p.bg2.rearrange("(c q) -> q c", q=128))
            d(p.bg1_sb[:], p.bg1.rearrange("g -> g ()"))
            sync.wait_ge(ms, 1)
            d(p.bvrow[0:1, :], p.bv.rearrange("f -> () f"))
            sync.wait_ge(ld, n * 16)

        @blk.vector
        def _(vector):
            vector.memset(p.bvrow[:], 0.0)
            vector.memset(p.maxacc[:], 0.0)
            vector.memset(p.row_sb[:], 0.0)
            vector.memset(p.ccrow[:], 0.0)
            vector.memset(p.eps_sb[:], 1e-5)
            vector.memset(p.b32k[:], 32768.5)
            vector.drain()
            vector.nop().then_inc(ms, 1)


def emit_p0(p):
    nc = p.nc
    with ExitStack() as pc:
        sb = lambda nm, sh, d=F32: pc.enter_context(nc.sbuf_tensor(nm, sh, d))
        xin = [sb(f"p0_xin{i}", [128, 4, DM]) for i in range(2)]
        xfo = [sb(f"p0_xfo{i}", [128, KC, TT]) for i in range(2)]
        ps = [pc.enter_context(nc.psum_tensor(f"p0_ps{i}", [128, 128], F32))
              for i in range(4)]
        slist = []
        def sem(nm):
            h = nc.alloc_semaphore(name=nm)
            slist.append(h)
            return h
        ld = [sem(f"p0_ld{i}") for i in range(2)]
        tr = sem("p0_tr")
        ev = sem("p0_ev")
        st = [sem(f"p0_st{i}") for i in range(2)]
        NTR = 32
        with nc.Block() as blk:
            @blk.sync
            def _(sync):
                def store(j):
                    sync.wait_ge(ev, (j + 1) * NTR)
                    sync.dma_start(
                        out=p.xf[:, :, j * TT:(j + 1) * TT].rearrange("c q t -> q c t"),
                        in_=xfo[j % 2][:]).then_inc(st[j % 2], 16)
                for i in range(NTT):
                    if i >= 2:
                        sync.wait_ge(tr, (i - 1) * NTR)
                    src = p.x[i * TT:(i + 1) * TT, :].rearrange(
                        "(s q) f -> q s f", q=128)
                    sync.dma_start(out=xin[i % 2][:], in_=src).then_inc(ld[i % 2], 16)
                    if i >= 1:
                        store(i - 1)
                store(NTT - 1)
                sync.wait_ge(st[0], NTT // 2 * 16)
                sync.wait_ge(st[1], NTT // 2 * 16)

            @blk.tensor
            def _(tensor):
                n = 0
                for i in range(NTT):
                    tensor.wait_ge(ld[i % 2], (i // 2 + 1) * 16)
                    for s in range(4):
                        for c in range(KC):
                            if n >= 4:
                                tensor.wait_ge(ev, n - 3)
                            tensor.transpose(
                                ps[n % 4][:], xin[i % 2][:, s, c * 128:(c + 1) * 128],
                                p.ident_sb[:]).then_inc(tr, 1)
                            n += 1

            @blk.vector
            def _(vector):
                n = 0
                for i in range(NTT):
                    if i >= 2:
                        vector.wait_ge(st[i % 2], (i // 2) * 16)
                    for s in range(4):
                        for c in range(KC):
                            vector.wait_ge(tr, n + 1)
                            vector.tensor_copy(
                                xfo[i % 2][:, c, s * 128:(s + 1) * 128],
                                ps[n % 4][:]).then_inc(ev, 1)
                            n += 1
        nc.clear_and_free_semaphores(slist)

def emit_p1(p):
    nc = p.nc
    with ExitStack() as pc:
        sb = lambda nm, sh, d=F32: pc.enter_context(nc.sbuf_tensor(nm, sh, d))
        wq_hi = sb("wq_hi", [128, KC, DM], BF16); wq_lo = sb("wq_lo", [128, KC, DM], BF16)
        wk_hi = sb("wk_hi", [128, KC, DM], BF16); wk_lo = sb("wk_lo", [128, KC, DM], BF16)
        wv_hi = sb("wv_hi", [128, KC, DM], BF16); wv_lo = sb("wv_lo", [128, KC, DM], BF16)
        wg1_hi = sb("wg1_hi", [128, KC, 16], BF16); wg1_lo = sb("wg1_lo", [128, KC, 16], BF16)
        wg2_sb = sb("wg2_sb", [16, DM])
        wtmp = [sb(f"p1_wtmp{i}", [128, DM]) for i in range(2)]
        xf_t = [sb(f"p1_xf{i}", [128, KC, TT]) for i in range(2)]
        xh = [sb(f"p1_xh{i}", [128, KC, TT], BF16) for i in range(2)]
        xl = [sb(f"p1_xl{i}", [128, KC, TT], BF16) for i in range(2)]
        qst = [sb(f"p1_qst{i}", [128, TT]) for i in range(2)]
        kst = [sb(f"p1_kst{i}", [128, TT]) for i in range(2)]
        vst = [sb(f"p1_vst{i}", [128, TT]) for i in range(2)]
        ast = [sb(f"p1_ast{i}", [128, TT]) for i in range(2)]
        g1b = sb("p1_g1b", [16, TT]); sg1 = sb("p1_sg1", [16, TT])
        g1s = [sb(f"p1_g1s{i}", [16, TT]) for i in range(2)]
        asig = sb("p1_asig", [128, TT])
        psq = pc.enter_context(nc.psum_tensor("p1_psq", [128, TT], F32))
        psk = pc.enter_context(nc.psum_tensor("p1_psk", [128, TT], F32))
        psv = pc.enter_context(nc.psum_tensor("p1_psv", [128, TT], F32))
        psg1 = pc.enter_context(nc.psum_tensor("p1_psg1", [16, TT], F32))
        psg2 = pc.enter_context(nc.psum_tensor("p1_psg2", [128, TT], F32))
        slist = []
        def sem(nm):
            h = nc.alloc_semaphore(name=nm)
            slist.append(h)
            return h
        wld = [sem(f"p1_wld{i}") for i in range(2)]
        wsp = sem("p1_wsp")
        xld = [sem(f"p1_xld{i}") for i in range(2)]
        xsp = sem("p1_xsp")
        qmm = sem("p1_qmm"); kmm = sem("p1_kmm"); vmm = sem("p1_vmm")
        g1mm = sem("p1_g1mm"); g2mm = sem("p1_g2mm")
        qev = sem("p1_qev"); kev = sem("p1_kev"); vev = sem("p1_vev")
        aev = sem("p1_aev")
        sc1 = sem("p1_sc1")
        g1sv = sem("p1_g1sv")
        scs = sem("p1_scs")
        qsd = [sem(f"p1_qsd{i}") for i in range(2)]
        ksd = [sem(f"p1_ksd{i}") for i in range(2)]
        vsd = [sem(f"p1_vsd{i}") for i in range(2)]
        asd = [sem(f"p1_asd{i}") for i in range(2)]

        NW = 3 * KC + 1
        with nc.Block() as blk:
            @blk.sync
            def _(sync):
                n = 0
                for (wsrc, _, _) in ((p.Wq, wq_hi, wq_lo), (p.Wk, wk_hi, wk_lo),
                                     (p.Wv, wv_hi, wv_lo)):
                    for kc in range(KC):
                        if n >= 2:
                            sync.wait_ge(wsp, n - 1)
                        sync.dma_start(out=wtmp[n % 2][:],
                                       in_=wsrc[kc * 128:(kc + 1) * 128, :]
                                       ).then_inc(wld[n % 2], 16)
                        n += 1
                if n >= 2:
                    sync.wait_ge(wsp, n - 1)
                with nc.allow_non_contiguous_dma(reason="tiny Wg1 one-time"):
                    sync.dma_start(out=wtmp[n % 2][:, 0:16 * KC].rearrange(
                                       "q (c g) -> q c g", g=16),
                                   in_=p.Wg1.rearrange("(c q) g -> q c g", q=128)
                                   ).then_inc(wld[n % 2], 16)
                n += 1
                sync.wait_ge(wsp, n - 1)
                sync.dma_start(out=wg2_sb[:], in_=p.Wg2).then_inc(wld[n % 2], 16)
                n += 1
                nq = nk = nv = na = 0
                def stores_for(j):
                    nonlocal nq, nk, nv, na
                    for mc in range(KC):
                        sync.wait_ge(qev, nq + 1)
                        sync.dma_start(out=p.qf[mc, :, j * TT:(j + 1) * TT],
                                       in_=qst[nq % 2][:]).then_inc(qsd[nq % 2], 16)
                        nq += 1
                        sync.wait_ge(kev, nk + 1)
                        sync.dma_start(out=p.kf[mc, :, j * TT:(j + 1) * TT],
                                       in_=kst[nk % 2][:]).then_inc(ksd[nk % 2], 16)
                        nk += 1
                        sync.wait_ge(vev, nv + 1)
                        tsub, half = mc % 4, mc // 4
                        r0 = j * TT + tsub * 128
                        sync.dma_start(out=p.vt[r0:r0 + 128,
                                                half * TT:(half + 1) * TT],
                                       in_=vst[nv % 2][:]).then_inc(vsd[nv % 2], 16)
                        nv += 1
                        sync.wait_ge(aev, na + 1)
                        sync.dma_start(out=p.af[mc, :, j * TT:(j + 1) * TT],
                                       in_=ast[na % 2][:]).then_inc(asd[na % 2], 16)
                        na += 1
                for i in range(NTT):
                    if i >= 1:
                        sync.wait_ge(xsp, i)
                    sync.dma_start(
                        out=xf_t[i % 2][:],
                        in_=p.xf[:, :, i * TT:(i + 1) * TT].rearrange("c q t -> q c t")
                        ).then_inc(xld[i % 2], 16)
                    if i >= 1:
                        stores_for(i - 1)
                stores_for(NTT - 1)
                for s2 in (qsd, ksd, vsd, asd):
                    sync.wait_ge(s2[0], NTT * KC // 2 * 16)
                    sync.wait_ge(s2[1], NTT * KC // 2 * 16)

            @blk.vector
            def _(vector):
                n = 0
                for (_, whi, wlo) in ((p.Wq, wq_hi, wq_lo), (p.Wk, wk_hi, wk_lo),
                                      (p.Wv, wv_hi, wv_lo)):
                    for kc in range(KC):
                        vector.wait_ge(wld[n % 2], (n // 2 + 1) * 16)
                        vector.tensor_copy(whi[:, kc, :], wtmp[n % 2][:])
                        vector.drain()
                        vector.tensor_sub(wlo[:, kc, :], wtmp[n % 2][:], whi[:, kc, :])
                        vector.drain()
                        vector.nop().then_inc(wsp, 1)
                        n += 1
                vector.wait_ge(wld[n % 2], (n // 2 + 1) * 16)
                wg1view = wtmp[n % 2][:, 0:16 * KC].rearrange("q (c g) -> q c g", g=16)
                vector.tensor_copy(wg1_hi[:], wg1view)
                vector.drain()
                vector.tensor_sub(wg1_lo[:], wg1view, wg1_hi[:])
                vector.drain()
                vector.nop().then_inc(wsp, 1)
                n += 1
                nq = nk = nv = na = 0
                for i in range(NTT):
                    vector.wait_ge(xld[i % 2], (i // 2 + 1) * 16)
                    if i >= 2:
                        vector.wait_ge(g2mm, (i - 1) * KC)
                    vector.tensor_copy(xh[i % 2][:], xf_t[i % 2][:])
                    vector.drain()
                    vector.tensor_sub(xl[i % 2][:], xf_t[i % 2][:], xh[i % 2][:])
                    vector.drain()
                    vector.nop().then_inc(xsp, 1)
                    vector.wait_ge(sc1, 2 * (i + 1))
                    vector.tensor_mul(g1s[i % 2][:], g1b[:], sg1[:])
                    vector.drain()
                    vector.nop().then_inc(g1sv, 1)
                    for mc in range(KC):
                        vector.wait_ge(qmm, nq + 1)
                        if nq >= 2:
                            vector.wait_ge(qsd[nq % 2], (nq // 2) * 16)
                        vector.tensor_scalar_add(qst[nq % 2][:], psq[:],
                                                 p.bq_sb[:, mc:mc + 1])
                        vector.drain()
                        vector.nop().then_inc(qev, 1)
                        nq += 1
                        vector.wait_ge(kmm, nk + 1)
                        if nk >= 2:
                            vector.wait_ge(ksd[nk % 2], (nk // 2) * 16)
                        vector.tensor_scalar_add(kst[nk % 2][:], psk[:],
                                                 p.bk_sb[:, mc:mc + 1])
                        vector.drain()
                        vector.nop().then_inc(kev, 1)
                        nk += 1
                        vector.wait_ge(vmm, nv + 1)
                        if nv >= 2:
                            vector.wait_ge(vsd[nv % 2], (nv // 2) * 16)
                        vector.tensor_copy(vst[nv % 2][:], psv[:])
                        vector.drain()
                        vector.nop().then_inc(vev, 1)
                        nv += 1
                        vector.wait_ge(scs, na + 1)
                        if na >= 2:
                            vector.wait_ge(asd[na % 2], (na // 2) * 16)
                        vector.tensor_scalar(ast[na % 2][:], asig[:], 0.9, 0.1,
                                             op0=OP.mult, op1=OP.add)
                        vector.drain()
                        vector.nop().then_inc(aev, 1)
                        na += 1

            @blk.scalar
            def _(scalar):
                for i in range(NTT):
                    scalar.wait_ge(g1mm, i + 1)
                    if i >= 1:
                        scalar.wait_ge(g1sv, i)
                    scalar.activation(g1b[:], psg1[:], AF.Identity,
                                      bias=p.bg1_sb[:, 0:1], scale=1.0).then_inc(sc1, 1)
                    scalar.drain()
                    scalar.activation(sg1[:], psg1[:], AF.Sigmoid,
                                      bias=p.bg1_sb[:, 0:1], scale=1.0).then_inc(sc1, 1)
                    scalar.drain()
                    for mc in range(KC):
                        nag = i * KC + mc
                        scalar.wait_ge(g2mm, nag + 1)
                        if nag >= 1:
                            scalar.wait_ge(aev, nag)
                        scalar.activation(asig[:], psg2[:], AF.Sigmoid,
                                          bias=p.bg2_sb[:, mc:mc + 1], scale=1.0).then_inc(scs, 1)
                        scalar.drain()

            @blk.tensor
            def _(tensor):
                tensor.wait_ge(wsp, NW)
                tensor.wait_ge(wld[1], 13 * 16)
                nq = nk = nv = na = 0
                for i in range(NTT):
                    bi = i % 2
                    tensor.wait_ge(xsp, i + 1)
                    if i >= 1:
                        tensor.wait_ge(sc1, 2 * i)
                    for n3, (xs, wk_) in enumerate(
                            ((xh, 0), (xl, 0), (xh, 1))):
                        w = (wg1_hi, wg1_lo)[wk_]
                        for kc in range(KC):
                            ins = tensor.matmul(psg1[:], w[:, kc, :],
                                                xs[bi][:, kc, :],
                                                start=(n3 == 0 and kc == 0),
                                                stop=(n3 == 2 and kc == KC - 1))
                    ins.then_inc(g1mm, 1)
                    for mc in range(KC):
                        tensor.wait_ge(qev, nq)
                        k3 = 0
                        for (xs, wpc) in ((xh, wq_hi), (xl, wq_hi), (xh, wq_lo)):
                            for kc in range(KC):
                                ins = tensor.matmul(
                                    psq[:], wpc[:, kc, mc * 128:(mc + 1) * 128],
                                    xs[bi][:, kc, :],
                                    start=(k3 == 0), stop=(k3 == 3 * KC - 1))
                                k3 += 1
                        ins.then_inc(qmm, 1)
                        nq += 1
                        tensor.wait_ge(kev, nk)
                        k3 = 0
                        for (xs, wpc) in ((xh, wk_hi), (xl, wk_hi), (xh, wk_lo)):
                            for kc in range(KC):
                                ins = tensor.matmul(
                                    psk[:], wpc[:, kc, mc * 128:(mc + 1) * 128],
                                    xs[bi][:, kc, :],
                                    start=(k3 == 0), stop=(k3 == 3 * KC - 1))
                                k3 += 1
                        ins.then_inc(kmm, 1)
                        nk += 1
                        tensor.wait_ge(vev, nv)
                        tsub, half = mc % 4, mc // 4
                        tensor.matmul(psv[:], p.ones_sb[:],
                                      p.bvrow[:, half * TT:(half + 1) * TT],
                                      start=True, stop=False)
                        k3 = 0
                        for (xs, wpc) in ((xh, wv_hi), (xl, wv_hi), (xh, wv_lo)):
                            for kc in range(KC):
                                ins = tensor.matmul(
                                    psv[:], xs[bi][:, kc, tsub * 128:(tsub + 1) * 128],
                                    wpc[:, kc, half * TT:(half + 1) * TT],
                                    start=False, stop=(k3 == 3 * KC - 1))
                                k3 += 1
                        ins.then_inc(vmm, 1)
                        nv += 1
                        tensor.wait_ge(g1sv, i + 1)
                        tensor.wait_ge(scs, na)
                        ins = tensor.matmul(psg2[:], wg2_sb[:, mc * 128:(mc + 1) * 128],
                                            g1s[bi][:], start=True, stop=True)
                        ins.then_inc(g2mm, 1)
                        na += 1
        nc.clear_and_free_semaphores(slist)

def emit_p2(p):
    nc = p.nc
    NST = 8
    NG = NTT * NST
    if P2_STEPS is not None:
        NG = P2_STEPS
    NTRS = 24
    with ExitStack() as pc:
        sb = lambda nm, sh, d=F32: pc.enter_context(nc.sbuf_tensor(nm, sh, d))
        qt = [sb(f"p2_qt{i}", [128, KC, TT]) for i in range(2)]
        kt = [sb(f"p2_kt{i}", [128, KC, TT]) for i in range(2)]
        at0 = sb("p2_at", [128, KC, TT])
        at = [at0, at0]
        xft = sb("p2_xft", [128, KC, TT])
        xr = sb("p2_xr", [128, KC, TT])
        x1t = sb("p2_x1t", [128, KC, TT])
        sq = x1t
        vch = [sb(f"p2_vch{i}", [128, DM]) for i in range(2)]
        vs1 = [sb(f"p2_vs1{i}", [32, DM]) for i in range(2)]
        S_all = sb("p2_S", [128, KC, 64])
        LY = sb("p2_LY", [128, KC, 2, 33])
        rc = sb("p2_rc", [128, KC, 2, 32])
        Qt_ = sb("p2_Qt", [128, KC, 2, 32])
        QtS = sb("p2_QtS", [128, KC, 2, 32])
        KtBD = sb("p2_KtBD", [128, KC, 128])
        K2BD = sb("p2_K2BD", [128, KC, 128])
        ToffSB = sb("p2_ToffSB", [128, KC, 32])
        K2_ = sb("p2_K2", [128, KC, 2, 32])
        ct01 = sb("p2_ct01", [128, KC])
        scb = sb("p2_scb", [128, KC, 64])
        mx = sb("p2_mx", [128, KC])
        dsb = sb("p2_dsb", [128, KC, 64])
        psb = sb("p2_psb", [128, KC, 64])
        sm = sb("p2_sm", [128, KC])
        rs = sb("p2_rs", [128, KC])
        pn = sb("p2_pn", [128, KC, 64])
        pT = sb("p2_pT", [64, KC, 128])
        K2T = sb("p2_K2T", [32, KC, 2, 128])
        TtSB = sb("p2_TtSB", [128, KC, 2, 32])
        att = sb("p2_att", [128, KC, 64])
        m_sb = pn[:].rearrange("q c e -> q (c e)")
        msq = dsb[:].rearrange("q c e -> q (c e)")
        v_sb = scb[:].rearrange("q c e -> q (c e)")
        sd_sb = att[:].rearrange("q c e -> q (c e)")
        rstd = sd_sb
        B = [pc.enter_context(nc.psum_tensor(f"p2_B{i}", [128, TT], F32))
             for i in range(8)]
        B0v = B[0][:].rearrange("q (c e) -> q c e", e=64)
        Ttv = B[0][:].rearrange("q (c s2 t) -> q c s2 t", s2=2, t=32)
        B1v = B[1][:].rearrange("q (c e) -> q c e", e=64)
        B2v = B[2][:].rearrange("q (c e) -> q c e", e=64)
        B3v = B[3][:].rearrange("q (c e) -> q c e", e=64)
        Dv = B[4][:].rearrange("q (c e) -> q c e", e=64)
        B5tv = B[5][:, 0:KC * 32].rearrange("q (c t) -> q c t", t=32)
        TR = [B[6], B[7]]

        slist = []
        def sem(nm):
            h = nc.alloc_semaphore(name=nm)
            slist.append(h)
            return h
        qld = [sem(f"p2_qld{i}") for i in range(2)]
        xfl = sem("p2_xfl")
        vld = [sem(f"p2_vld{i}") for i in range(2)]
        x1sd = [sem(f"p2_x1sd{i}") for i in range(2)]
        V1 = sem("p2_V1"); V2 = sem("p2_V2"); V2b = sem("p2_V2b")
        V3 = sem("p2_V3"); V4 = sem("p2_V4"); V5 = sem("p2_V5")
        V6 = sem("p2_V6"); V7 = sem("p2_V7")
        S1e = sem("p2_S1e")
        T1 = sem("p2_T1"); T3 = sem("p2_T3"); T4 = sem("p2_T4"); T5 = sem("p2_T5")
        ATr = sem("p2_ATr")
        trsem = sem("p2_trsem"); trev = sem("p2_trev")
        LSq = sem("p2_LSq"); LT1 = sem("p2_LT1"); LT2 = sem("p2_LT2")
        LNx = sem("p2_LNx"); LNe = sem("p2_LNe"); LNv = sem("p2_LNv")
        LNs = sem("p2_LNs")

        def v6req(G):
            return G + 1 + (1 if G >= NG // 2 else 0)

        with nc.Block() as blk:
            @blk.sync
            def _(sync):
                for G in range(NG):
                    g, j = G // NST, G % NST
                    if j == 0:
                        if g >= 2:
                            sync.wait_ge(T3, (g - 1) * NST)
                            sync.wait_ge(V1, (g - 1) * NST)
                        base = g * TT
                        for (dst, src) in ((qt, p.qf), (kt, p.kf)):
                            sync.dma_start(
                                out=dst[g % 2][:],
                                in_=src[:, :, base:base + TT].rearrange(
                                    "c q t -> q c t")).then_inc(qld[g % 2], 16)
                        if g >= 1:
                            sync.wait_ge(V1, g * NST)
                        sync.dma_start(
                            out=at[g % 2][:],
                            in_=p.af[:, :, base:base + TT].rearrange(
                                "c q t -> q c t")).then_inc(qld[g % 2], 16)
                        sync.wait_ge(V7, g * NST)
                        sync.dma_start(
                            out=xft[:],
                            in_=p.xf[:, :, base:base + TT].rearrange("c q t -> q c t")
                            ).then_inc(xfl, 16)
                    if G >= 2:
                        sync.wait_ge(T4, G - 1)
                    r0 = g * TT + j * 64
                    sync.dma_start(out=vch[G % 2][0:64, :],
                                   in_=p.vt[r0:r0 + 64, :]).then_inc(vld[G % 2], 16)
                    sync.dma_start(out=vch[G % 2][64:128, :],
                                   in_=p.vt[r0:r0 + 64, :]).then_inc(vld[G % 2], 16)
                    sync.dma_start(out=vs1[G % 2][0:32, :],
                                   in_=p.vt[r0 + 32:r0 + 64, :]).then_inc(vld[G % 2], 16)
                    if j == NST - 1 and g >= 1:
                        gs = g - 1
                        sync.wait_ge(LNx, gs + 1)
                        sync.dma_start(
                            out=p.x1[:, :, gs * TT:(gs + 1) * TT].rearrange(
                                "c q t -> q c t"),
                            in_=x1t[:]).then_inc(x1sd[gs % 2], 16)
                NT2 = NG // NST
                gs = NT2 - 1
                sync.wait_ge(LNx, NT2)
                sync.dma_start(
                    out=p.x1[:, :, gs * TT:(gs + 1) * TT].rearrange("c q t -> q c t"),
                    in_=x1t[:]).then_inc(x1sd[gs % 2], 16)
                sync.wait_ge(x1sd[0], (NT2 + 1) // 2 * 16)
                if NT2 >= 2:
                    sync.wait_ge(x1sd[1], NT2 // 2 * 16)

            @blk.tensor
            def _(tensor):
                ntr = 0

                def rot_transpose(in_ap):
                    nonlocal ntr
                    if ntr >= 2:
                        tensor.wait_ge(trev, ntr - 1)
                    nrows = in_ap.shape[-1]
                    tensor.transpose(TR[ntr % 2][0:nrows, 0:128], in_ap,
                                     p.ident_sb[:]).then_inc(trsem, 1)
                    ntr += 1

                for G in range(NG):
                    g, j = G // NST, G % NST
                    jo = j * 64
                    tensor.wait_ge(qld[g % 2], (g // 2 + 1) * 48)
                    tensor.wait_ge(V4, G)
                    tensor.wait_ge(LNe, g)
                    for c in range(KC):
                        for u in range(2):
                            ins = tensor.matmul(
                                B0v[64 * u:64 * u + 64, c, :],
                                qt[g % 2][64 * u:64 * u + 64, c, jo:jo + 64],
                                kt[g % 2][64 * u:64 * u + 64, c, jo:jo + 64],
                                start=True, stop=True)
                    ins.then_inc(T1, 1)
                    tensor.wait_ge(V1, G + 1)
                    for c in range(KC):
                        for sub in range(2):
                            rot_transpose(K2_[:, c, sub, :])
                    tensor.wait_ge(V2, G + 1)
                    tensor.wait_ge(V4, G)
                    for c in range(KC):
                        tensor.matmul(
                            Ttv[:, c, :, :], KtBD[:, c, :],
                            Qt_[:, c, :, :], start=True, stop=True)
                    for c in range(KC):
                        ins = tensor.matmul(
                            B5tv[:, c, :], K2BD[:, c, :],
                            Qt_[:, c, 1, :], start=True, stop=True)
                    ins.then_inc(T3, 1)
                    tensor.wait_ge(V6, v6req(G))
                    tensor.wait_ge(V7, G)
                    tensor.wait_ge(V4, G + 1)
                    tensor.wait_ge(vld[G % 2], (G // 2 + 1) * 48)
                    for c in range(KC):
                        for u in range(2):
                            tensor.matmul(
                                B3v[64 * u:64 * u + 64, c, :],
                                S_all[64 * u:64 * u + 64, c, :],
                                QtS[64 * u:64 * u + 64, c, :, :],
                                start=True, stop=False)
                            for st2 in range(2):
                                tensor.matmul(
                                    B3v[64 * u:64 * u + 64, c,
                                        32 * st2:32 * st2 + 32],
                                    vch[G % 2][64 * u:64 * u + 64,
                                               c * 128 + u * 64:c * 128 + u * 64 + 64],
                                    TtSB[64 * u:64 * u + 64, c, st2, :],
                                    start=False, stop=False)
                            tensor.matmul(
                                B3v[64 * u:64 * u + 64, c, 32:64],
                                vch[G % 2][64 * u:64 * u + 32,
                                           c * 128 + u * 64:c * 128 + u * 64 + 64],
                                ToffSB[64 * u:64 * u + 32, c, :],
                                start=False, stop=True)
                    tensor.wait_ge(trev, NTRS * G + 16)
                    tensor.wait_ge(V6, v6req(G))
                    for c in range(KC):
                        for u in range(2):
                            for sub in range(2):
                                vsrc = vch[G % 2] if sub == 0 else vs1[G % 2]
                                ins = tensor.matmul(
                                    Dv[64 * u:64 * u + 64, c, :],
                                    K2T[0:32, c, sub, u * 64:u * 64 + 64],
                                    vsrc[0:32,
                                         c * 128 + u * 64:c * 128 + u * 64 + 64],
                                    start=(sub == 0), stop=(sub == 1))
                    ins.then_inc(T5, 1)
                    tensor.wait_ge(V3, G + 1)
                    for c in range(KC):
                        rot_transpose(pn[:, c, :])
                    tensor.wait_ge(trev, NTRS * G + NTRS)
                    tensor.wait_ge(V5, G)
                    tensor.wait_ge(LNe, g)
                    for c in range(KC):
                        for u in range(2):
                            ins = tensor.matmul(
                                B1v[64 * u:64 * u + 64, c, :],
                                pT[0:64, c, u * 64:u * 64 + 64],
                                vch[G % 2][0:64,
                                           c * 128 + u * 64:c * 128 + u * 64 + 64],
                                start=True, stop=True)
                    ins.then_inc(T4, 1)
                    tensor.wait_ge(V5, G + 1)
                    tensor.wait_ge(V7, G)
                    for c in range(KC):
                        for u in range(2):
                            idq = p.ident_sb[64 * u:64 * u + 64, 64 * u:64 * u + 64]
                            ins = tensor.matmul(
                                B2v[64 * u:64 * u + 64, c, :],
                                att[64 * u:64 * u + 64, c, :], idq,
                                start=True, stop=True)
                    ins.then_inc(ATr, 1)
                    if j == NST - 1:
                        tensor.wait_ge(V7, (g + 1) * NST)
                        tensor.wait_ge(V4, (g + 1) * NST)
                        for kc in range(KC):
                            ins = tensor.matmul(B[0][:], p.ones_sb[:],
                                                xr[:, kc, :],
                                                start=(kc == 0), stop=(kc == KC - 1))
                        ins.then_inc(LT1, 1)
                        tensor.wait_ge(LSq, g + 1)
                        tensor.wait_ge(V5, (g + 1) * NST)
                        for kc in range(KC):
                            ins = tensor.matmul(B[1][:], p.ones_sb[:],
                                                sq[:, kc, :],
                                                start=(kc == 0), stop=(kc == KC - 1))
                        ins.then_inc(LT2, 1)

            @blk.vector
            def _(vector):
                vector.memset(LY[:, :, :, 0:1], 1.0)
                vector.memset(KtBD[:], 0.0)
                vector.memset(K2BD[:], 0.0)
                vector.drain()
                for G in range(NG):
                    g, j = G // NST, G % NST
                    jo = j * 64
                    b = G // (NG // 2)
                    if G % (NG // 2) == 0 and NG > NST // 2:
                        if b == 1:
                            vector.wait_ge(T4, NG // 2)
                        vector.memset(S_all[:], 0.0)
                        vector.drain()
                        vector.nop().then_inc(V6, 1)
                    vector.wait_ge(qld[g % 2], (g // 2 + 1) * 48)
                    vector.wait_ge(T4, G)
                    atj = at[g % 2][:, :, jo:jo + 64].rearrange(
                        "q c (s2 t) -> q c s2 t", t=32)
                    qtj = qt[g % 2][:, :, jo:jo + 64].rearrange(
                        "q c (s2 t) -> q c s2 t", t=32)
                    ktj = kt[g % 2][:, :, jo:jo + 64].rearrange(
                        "q c (s2 t) -> q c s2 t", t=32)
                    for c in range(KC):
                        for sub in range(2):
                            vector.tensor_tensor_scan(
                                LY[:, c, sub, 1:33], atj[:, c, sub, :],
                                atj[:, c, sub, :], 1.0, op0=OP.mult, op1=OP.bypass)
                    vector.drain()
                    vector.reciprocal(rc[:], LY[:, :, :, 1:33])
                    vector.drain()
                    vector.tensor_mul(Qt_[:], qtj, LY[:, :, :, 0:32])
                    vector.tensor_mul(
                        KtBD[0:64, :, 0:64].rearrange("q c (s2 t) -> q c s2 t", t=32),
                        ktj[0:64, :, :, :], rc[0:64, :, :, :])
                    vector.tensor_mul(
                        KtBD[64:128, :, 64:128].rearrange("q c (s2 t) -> q c s2 t", t=32),
                        ktj[64:128, :, :, :], rc[64:128, :, :, :])
                    vector.drain()
                    vector.tensor_tensor(
                        rc[:], rc[:],
                        LY[:, :, :, 32:33].broadcast_to([128, KC, 2, 32]), op=OP.mult)
                    vector.drain()
                    vector.tensor_mul(
                        K2BD[0:64, :, 0:32], ktj[0:64, :, 0, :], rc[0:64, :, 0, :])
                    vector.tensor_mul(
                        K2BD[64:128, :, 64:96], ktj[64:128, :, 0, :],
                        rc[64:128, :, 0, :])
                    vector.drain()
                    vector.tensor_tensor(
                        rc[:, :, 0, :], rc[:, :, 0, :],
                        LY[:, :, 1, 32:33].broadcast_to([128, KC, 32]), op=OP.mult)
                    vector.drain()
                    vector.tensor_mul(K2_[:], ktj, rc[:])
                    vector.tensor_mul(ct01[:], LY[:, :, 0, 32], LY[:, :, 1, 32])
                    vector.tensor_copy(QtS[:, :, 0, :], Qt_[:, :, 0, :])
                    vector.drain()
                    vector.tensor_tensor(
                        QtS[:, :, 1, :], Qt_[:, :, 1, :],
                        LY[:, :, 0, 32:33].broadcast_to([128, KC, 32]), op=OP.mult)
                    vector.drain()
                    vector.nop().then_inc(V1, 1)
                    vector.wait_ge(T1, G + 1)
                    vector.tensor_tensor(
                        scb[:], B0v[:],
                        p.cmask_sb[:].rearrange("q s -> q () s"
                                                ).broadcast_to([128, KC, 64]),
                        op=OP.add)
                    vector.drain()
                    vector.nop().then_inc(V2, 1)
                    vector.reduce_max(
                        mx[:].rearrange("q c -> q c ()"), scb[:], axis=AX)
                    vector.drain()
                    vector.tensor_tensor(
                        dsb[:], scb[:],
                        mx[:].rearrange("q c -> q c ()").broadcast_to([128, KC, 64]),
                        op=OP.subtract)
                    vector.drain()
                    vector.nop().then_inc(V2b, 1)
                    vector.wait_ge(S1e, G + 1)
                    vector.reduce_sum(
                        sm[:].rearrange("q c -> q c ()"), psb[:], axis=AX)
                    vector.drain()
                    vector.reciprocal(rs[:], sm[:])
                    vector.drain()
                    if G >= 1:
                        vector.wait_ge(trsem, NTRS * (G - 1) + NTRS)
                    vector.tensor_tensor(
                        pn[:], psb[:],
                        rs[:].rearrange("q c -> q c ()").broadcast_to([128, KC, 64]),
                        op=OP.mult)
                    vector.drain()
                    vector.nop().then_inc(V3, 1)
                    base_tr = NTRS * G
                    if G >= 1:
                        vector.wait_ge(T5, G)
                    nn = 0
                    for c in range(KC):
                        for sub in range(2):
                            vector.wait_ge(trsem, base_tr + nn + 1)
                            vector.tensor_copy(
                                K2T[0:32, c, sub, :],
                                TR[(base_tr + nn) % 2][0:32, 0:128])
                            vector.drain()
                            vector.nop().then_inc(trev, 1)
                            nn += 1
                    vector.wait_ge(T3, G + 1)
                    if G >= 1:
                        vector.wait_ge(T4, G)
                    vector.tensor_tensor(
                        TtSB[:], Ttv[:, :, :, :],
                        p.tmask_sb[:].rearrange("q (s2 t) -> q () s2 t", t=32
                                                ).broadcast_to([128, KC, 2, 32]),
                        op=OP.mult)
                    vector.tensor_copy(ToffSB[:], B5tv[:])
                    vector.drain()
                    vector.nop().then_inc(V4, 1)
                    for c in range(KC):
                        vector.wait_ge(trsem, base_tr + nn + 1)
                        vector.tensor_copy(
                            pT[0:64, c, :], TR[(base_tr + nn) % 2][0:64, 0:128])
                        vector.drain()
                        vector.nop().then_inc(trev, 1)
                        nn += 1
                    vector.wait_ge(T4, G + 1)
                    vector.tensor_copy(att[:], B1v[:])
                    vector.drain()
                    vector.nop().then_inc(V5, 1)
                    vector.wait_ge(T5, G + 1)
                    vector.tensor_tensor(
                        S_all[:], S_all[:],
                        ct01[:].rearrange("q c -> q c ()").broadcast_to([128, KC, 64]),
                        op=OP.mult)
                    vector.drain()
                    vector.tensor_add(S_all[:], S_all[:], Dv[:])
                    vector.drain()
                    vector.nop().then_inc(V6, 1)
                    vector.wait_ge(ATr, G + 1)
                    if j == 0:
                        vector.wait_ge(xfl, (g + 1) * 16)
                    vector.tensor_add(xr[:, :, jo:jo + 64], B2v[:],
                                      xft[:, :, jo:jo + 64])
                    vector.drain()
                    vector.tensor_add(xr[:, :, jo:jo + 64],
                                      xr[:, :, jo:jo + 64], B3v[:])
                    vector.drain()
                    vector.nop().then_inc(V7, 1)
                    if j == NST - 1:
                        vector.wait_ge(LT1, g + 1)
                        vector.tensor_scalar_mul(m_sb, B[0][:], 1.0 / DM)
                        vector.drain()
                        vector.tensor_mul(msq, m_sb, m_sb)
                        vector.drain()
                        vector.wait_ge(LT2, g + 1)
                        vector.scalar_tensor_tensor(
                            v_sb, B[1][:], 1.0 / DM, msq,
                            op0=OP.mult, op1=OP.subtract)
                        vector.drain()
                        vector.nop().then_inc(LNe, 1)
                        vector.nop().then_inc(LNv, 1)
                        vector.wait_ge(LNs, g + 1)
                        vector.reciprocal(rstd, sd_sb)
                        vector.drain()
                        if g >= 1:
                            vector.wait_ge(x1sd[(g - 1) % 2],
                                           ((g - 1) // 2 + 1) * 16)
                        vector.tensor_tensor(
                            x1t[:], xr[:],
                            m_sb.rearrange("q t -> q () t"
                                           ).broadcast_to([128, KC, TT]),
                            op=OP.subtract)
                        vector.drain()
                        vector.tensor_tensor(
                            x1t[:], x1t[:],
                            rstd.rearrange("q t -> q () t"
                                           ).broadcast_to([128, KC, TT]),
                            op=OP.mult)
                        vector.drain()
                        vector.reduce_max(
                            p.maxacc[:, g:g + 1],
                            x1t[:].rearrange("q c t -> q (c t)"),
                            axis=AX, apply_absolute_value=True)
                        vector.drain()
                        vector.nop().then_inc(LNx, 1)
                vector.reduce_max(p.sig_sb[:, 0:1], p.maxacc[:, 0:NTT], axis=AX)
                vector.drain()

            @blk.scalar
            def _(scalar):
                for G in range(NG):
                    g, j = G // NST, G % NST
                    scalar.wait_ge(V2b, G + 1)
                    if G >= 1:
                        scalar.wait_ge(V3, G)
                    scalar.activation(psb[:], dsb[:], AF.Exp, scale=0.125).then_inc(S1e, 1)
                    scalar.drain()
                    if j == NST - 1:
                        scalar.wait_ge(V7, (g + 1) * NST)
                        if g >= 1:
                            scalar.wait_ge(LT2, g)
                            scalar.wait_ge(x1sd[(g - 1) % 2],
                                           ((g - 1) // 2 + 1) * 16)
                        scalar.activation(sq[:], xr[:], AF.Square).then_inc(LSq, 1)
                        scalar.drain()
                        scalar.wait_ge(LNv, g + 1)
                        scalar.activation(sd_sb, v_sb, AF.Sqrt, bias=p.eps_sb[:]).then_inc(LNs, 1)
                        scalar.drain()
        nc.clear_and_free_semaphores(slist)

def emit_p3(p):
    nc = p.nc
    jobs = [(p.wup, p.wupT_hi, p.wupT_lo, MC3, DM),
            (p.wgate, p.wgT_hi, p.wgT_lo, MC3, DM),
            (p.wdown, p.wdT_hi, p.wdT_lo, KC, I3)]
    chunks = []
    for ji, (w, whi, wlo, nmc, kk) in enumerate(jobs):
        for mc in range(nmc):
            chunks.append((ji, mc, kk))
    cumkc = [0]
    for (ji, mc, kk) in chunks:
        cumkc.append(cumkc[-1] + kk // 128)
    with ExitStack() as pc:
        sb = lambda nm, sh, d=F32: pc.enter_context(nc.sbuf_tensor(nm, sh, d))
        wt = [sb(f"p3_wt{i}", [128, I3]) for i in range(2)]
        z = sb("p3_z", [128, I3]); zi = sb("p3_zi", [128, I3], I32)
        zc = sb("p3_zc", [128, I3]); A = sb("p3_A", [128, I3])
        Bb = sb("p3_B", [128, I3]); wq = sb("p3_wq", [128, I3])
        am = sb("p3_am", [128, I3 // 32]); sc_ = sb("p3_s", [128, I3 // 32])
        hi = [sb(f"p3_hi{i}", [128, 24, 128], BF16) for i in range(2)]
        lo = [sb(f"p3_lo{i}", [128, 24, 128], BF16) for i in range(2)]
        ps = [pc.enter_context(nc.psum_tensor(f"p3_ps{i}", [128, 128], F32))
              for i in range(2)]
        slist = []
        def sem(nm):
            h = nc.alloc_semaphore(name=nm)
            slist.append(h)
            return h
        ld = [sem(f"p3_ld{i}") for i in range(2)]
        vq = sem("p3_vq")
        tr = sem("p3_tr")
        pe = sem("p3_pe")
        sd = [sem(f"p3_sd{i}") for i in range(2)]
        with nc.Block() as blk:
            @blk.sync
            def _(sync):
                for n, (ji, mc, kk) in enumerate(chunks):
                    w = jobs[ji][0]
                    if n >= 2:
                        sync.wait_ge(vq, n - 1)
                    sync.dma_start(out=wt[n % 2][:, 0:kk],
                                   in_=w[mc * 128:(mc + 1) * 128, :]
                                   ).then_inc(ld[n % 2], 16)
                    if n >= 1:
                        jj, mm, kk2 = chunks[n - 1]
                        whi, wlo = jobs[jj][1], jobs[jj][2]
                        nkc = kk2 // 128
                        sync.wait_ge(pe, cumkc[n])
                        for (stg, dst) in ((hi, whi), (lo, wlo)):
                            sync.dma_start(
                                out=dst[:, :, mm * 128:(mm + 1) * 128].rearrange(
                                    "k q m -> q k m"),
                                in_=stg[(n - 1) % 2][:, 0:nkc, :]
                                ).then_inc(sd[(n - 1) % 2], 16)
                n = len(chunks)
                jj, mm, kk2 = chunks[n - 1]
                whi, wlo = jobs[jj][1], jobs[jj][2]
                nkc = kk2 // 128
                sync.wait_ge(pe, cumkc[n])
                for (stg, dst) in ((hi, whi), (lo, wlo)):
                    sync.dma_start(
                        out=dst[:, :, mm * 128:(mm + 1) * 128].rearrange(
                            "k q m -> q k m"),
                        in_=stg[(n - 1) % 2][:, 0:nkc, :]
                        ).then_inc(sd[(n - 1) % 2], 16)
                h0 = sum(32 for i in range(0, len(chunks), 2))
                h1 = sum(32 for i in range(1, len(chunks), 2))
                sync.wait_ge(sd[0], h0)
                sync.wait_ge(sd[1], h1)

            @blk.tensor
            def _(tensor):
                ntr = 0
                for n, (ji, mc, kk) in enumerate(chunks):
                    tensor.wait_ge(vq, n + 1)
                    for kc in range(kk // 128):
                        if ntr >= 2:
                            tensor.wait_ge(pe, ntr - 1)
                        tensor.transpose(ps[ntr % 2][:],
                                         wq[:, kc * 128:(kc + 1) * 128],
                                         p.ident_sb[:]).then_inc(tr, 1)
                        ntr += 1

            @blk.vector
            def _(vector):
                ntr = 0
                for n, (ji, mc, kk) in enumerate(chunks):
                    vector.wait_ge(ld[n % 2], (n // 2 + 1) * 16)
                    nb = kk // 32
                    w3 = wt[n % 2][:, 0:kk].rearrange("q (g c) -> q g c", c=32)
                    vector.reduce_max(
                        am[:, 0:nb].rearrange("q g -> q g ()"), w3, axis=AX,
                        apply_absolute_value=True)
                    vector.drain()
                    vector.tensor_scalar(sc_[:, 0:nb], am[:, 0:nb], 1.0 / 7.0,
                                         1e-12, op0=OP.mult, op1=OP.max)
                    vector.drain()
                    vector.reciprocal(am[:, 0:nb], sc_[:, 0:nb])
                    vector.drain()
                    vector.tensor_tensor(
                        z[:, 0:kk].rearrange("q (g c) -> q g c", c=32), w3,
                        am[:, 0:nb].rearrange("q g -> q g ()"
                                              ).broadcast_to([128, nb, 32]),
                        op=OP.mult)
                    vector.drain()
                    vector.tensor_copy(zi[:, 0:kk], z[:, 0:kk])
                    vector.drain()
                    vector.tensor_copy(zc[:, 0:kk], zi[:, 0:kk])
                    vector.drain()
                    vector.scalar_tensor_tensor(A[:, 0:kk], zc[:, 0:kk], -0.5,
                                                z[:, 0:kk], op0=OP.add,
                                                op1=OP.is_le)
                    vector.scalar_tensor_tensor(Bb[:, 0:kk], zc[:, 0:kk], 0.5,
                                                z[:, 0:kk], op0=OP.add,
                                                op1=OP.is_le)
                    vector.drain()
                    vector.tensor_add(z[:, 0:kk], zc[:, 0:kk], A[:, 0:kk])
                    vector.drain()
                    vector.tensor_add(z[:, 0:kk], z[:, 0:kk], Bb[:, 0:kk])
                    vector.drain()
                    vector.tensor_scalar_add(z[:, 0:kk], z[:, 0:kk], -1.0)
                    vector.drain()
                    if n >= 1:
                        vector.wait_ge(tr, cumkc[n])
                    vector.tensor_tensor(
                        wq[:, 0:kk].rearrange("q (g c) -> q g c", c=32),
                        z[:, 0:kk].rearrange("q (g c) -> q g c", c=32),
                        sc_[:, 0:nb].rearrange("q g -> q g ()"
                                               ).broadcast_to([128, nb, 32]),
                        op=OP.mult)
                    vector.drain()
                    vector.nop().then_inc(vq, 1)
                    if n >= 2:
                        vector.wait_ge(sd[n % 2], (n // 2) * 32)
                    for kc in range(kk // 128):
                        vector.wait_ge(tr, ntr + 1)
                        vector.tensor_copy(hi[n % 2][:, kc, :], ps[ntr % 2][:])
                        vector.drain()
                        vector.scalar_tensor_tensor(
                            lo[n % 2][:, kc, :], ps[ntr % 2][:], 1.0,
                            hi[n % 2][:, kc, :], op0=OP.mult, op1=OP.subtract)
                        vector.drain()
                        vector.nop().then_inc(pe, 1)
                        ntr += 1
        nc.clear_and_free_semaphores(slist)

def _bcast_cols(p, tensor, vector, psum, src_cols, dst_cols, sem_pair):
    n = src_cols.stop - src_cols.start
    tensor.matmul(psum[:, 0:n], p.ones_sb[:], p.row_sb[:, src_cols],
                  start=True, stop=True).then_inc(sem_pair[0], 1)
    vector.wait_ge(sem_pair[0], sem_pair[1])
    vector.tensor_copy(p.sig_sb[:, dst_cols], psum[:, 0:n])
    vector.drain()


def emit_p4(p):
    nc = p.nc
    with ExitStack() as pc:
        sb = lambda nm, sh, d=F32: pc.enter_context(nc.sbuf_tensor(nm, sh, d))
        xqh = sb("p4_xqh", [128, KC, T], BF16)
        xql = sb("p4_xql", [128, KC, T], BF16)
        x1t = [sb(f"p4_x1t{i}", [128, KC, TT // 2]) for i in range(2)]
        z = sb("p4_z", [128, KC * TT // 2]); zi = sb("p4_zi", [128, KC * TT // 2], I32)
        zc = sb("p4_zc", [128, KC * TT // 2]); A = sb("p4_A", [128, KC * TT // 2])
        Bb = sb("p4_B", [128, KC * TT // 2])
        wch = [[sb(f"p4_wc{i}_{wp}", [128, KC, 128], BF16) for wp in range(2)] for i in range(2)]
        stg = [sb(f"p4_stg{i}", [128, TT]) for i in range(2)]
        tmx = sb("p4_tmx", [128, 1])
        ps = [pc.enter_context(nc.psum_tensor(f"p4_ps{i}", [128, TT], F32))
              for i in range(8)]
        slist = []
        def sem(nm):
            h = nc.alloc_semaphore(name=nm)
            slist.append(h)
            return h
        cc = sem("p4_cc"); ccc = sem("p4_ccc"); bc = sem("p4_bc")
        xld = [sem(f"p4_xld{i}") for i in range(2)]
        xq = sem("p4_xq")
        wld = [sem(f"p4_wld{i}") for i in range(2)]
        mmg = sem("p4_mmg")
        evg = sem("p4_evg")
        sdg = [sem(f"p4_sdg{i}") for i in range(2)]
        NHT = 2 * NTT
        HC = KC * TT // 2
        with nc.Block() as blk:
            @blk.gpsimd
            def _(g):
                g.wait_ge(bc, 2)
                g.dma_start(out=p.cc_in.ap(), in_=p.ccrow[:]).then_inc(cc, 16)
                g.wait_ge(cc, 16)
                g.collective_compute(
                    "AllReduce", OP.max,
                    replica_groups=[CC_GROUP or list(range(NCORES))],
                    ins=[p.cc_in.ap()], outs=[p.cc_out.ap()]).then_inc(ccc, 1)
                g.wait_ge(ccc, 1)
                g.dma_start(out=p.ccrow[:], in_=p.cc_out.ap()).then_inc(cc, 16)
                g.wait_ge(cc, 32)
                g.nop().then_inc(ccc, 1)

            @blk.tensor
            def _(tensor):
                tensor.transpose(ps[0][0:1, 0:128], p.sig_sb[:, 0:1],
                                 p.ident_sb[:]).then_inc(bc, 1)
                tensor.wait_ge(bc, 3)
                tensor.matmul(ps[1][:, 0:2], p.ones_sb[:], p.row_sb[:, 0:2],
                              start=True, stop=True).then_inc(bc, 1)
                tensor.wait_ge(xq, NHT)
                ng = 0
                for wsel in range(2):
                    for mc in range(MC3):
                        n = wsel * MC3 + mc
                        tensor.wait_ge(wld[n % 2], (n // 2 + 1) * 32)
                        for tg in range(8):
                            if ng >= 8:
                                tensor.wait_ge(evg, ng - 7)
                            k3 = 0
                            for (xs, wp) in ((xqh, 0), (xql, 0), (xqh, 1)):
                                for kc in range(KC):
                                    ins = tensor.matmul(
                                        ps[ng % 8][:],
                                        wch[n % 2][wp][:, kc, :],
                                        xs[:, kc, tg * TT:(tg + 1) * TT],
                                        start=(k3 == 0), stop=(k3 == 3 * KC - 1))
                                    k3 += 1
                            ins.then_inc(mmg, 1)
                            ng += 1

            @blk.vector
            def _(vector):
                vector.wait_ge(bc, 1)
                vector.tensor_copy(p.ccrow[0:1, 0:128], ps[0][0:1, 0:128])
                vector.drain()
                vector.nop().then_inc(bc, 1)
                vector.wait_ge(ccc, 2)
                vector.reduce_max(p.row_sb[0:1, 0:1], p.ccrow[0:1, 0:128], axis=AX)
                vector.drain()
                vector.tensor_scalar(p.row_sb[0:1, 0:1], p.row_sb[0:1, 0:1],
                                     1.0 / QMAX, 1e-12, op0=OP.mult, op1=OP.max)
                vector.drain()
                vector.reciprocal(p.row_sb[0:1, 1:2], p.row_sb[0:1, 0:1])
                vector.drain()
                vector.nop().then_inc(bc, 1)
                vector.wait_ge(bc, 4)
                vector.tensor_copy(p.sig_sb[:, 1:3], ps[1][:, 0:2])
                vector.drain()
                vector.nop().then_inc(bc, 1)
                for h in range(NHT):
                    vector.wait_ge(xld[h % 2], (h // 2 + 1) * 16)
                    vector.wait_ge(bc, 5)
                    xv = x1t[h % 2][:].rearrange("q c t -> q (c t)")
                    vector.tensor_scalar_mul(z[:], xv, p.sig_sb[:, 2:3])
                    vector.drain()
                    vector.tensor_copy(zi[:], z[:])
                    vector.drain()
                    vector.tensor_copy(zc[:], zi[:])
                    vector.drain()
                    vector.scalar_tensor_tensor(A[:], zc[:], -0.5, z[:],
                                                op0=OP.add, op1=OP.is_le)
                    vector.scalar_tensor_tensor(Bb[:], zc[:], 0.5, z[:],
                                                op0=OP.add, op1=OP.is_le)
                    vector.drain()
                    vector.tensor_add(z[:], zc[:], A[:])
                    vector.drain()
                    vector.tensor_add(z[:], z[:], Bb[:])
                    vector.drain()
                    vector.tensor_scalar(zc[:], z[:], -1.0, p.sig_sb[:, 1:2],
                                         op0=OP.add, op1=OP.mult)
                    vector.drain()
                    ho = (h // 2) * TT + (h % 2) * (TT // 2)
                    dsth = xqh[:, :, ho:ho + TT // 2]
                    dstl = xql[:, :, ho:ho + TT // 2]
                    zc3 = zc[:].rearrange("q (c t) -> q c t", c=KC)
                    vector.tensor_copy(dsth, zc3)
                    vector.drain()
                    vector.scalar_tensor_tensor(dstl, zc3, 1.0, dsth,
                                                op0=OP.mult, op1=OP.subtract)
                    vector.drain()
                    vector.nop().then_inc(xq, 1)
                ng = 0
                for wsel in range(2):
                    mxcol = 8 + wsel
                    for mc in range(MC3):
                        for tg in range(8):
                            vector.wait_ge(mmg, ng + 1)
                            if ng >= 2:
                                vector.wait_ge(sdg[ng % 2], (ng // 2) * 16)
                            vector.tensor_copy(stg[ng % 2][:], ps[ng % 8][:])
                            vector.drain()
                            vector.reduce_max(tmx[:], stg[ng % 2][:], axis=AX,
                                              apply_absolute_value=True)
                            vector.drain()
                            vector.tensor_max(p.maxacc[:, mxcol:mxcol + 1],
                                              p.maxacc[:, mxcol:mxcol + 1],
                                              tmx[:])
                            vector.drain()
                            vector.nop().then_inc(evg, 1)
                            ng += 1

            @blk.sync
            def _(sync):
                for h in range(NHT):
                    if h >= 2:
                        sync.wait_ge(xq, h - 1)
                    base = h * (TT // 2)
                    sync.dma_start(
                        out=x1t[h % 2][:],
                        in_=p.x1[:, :, base:base + TT // 2].rearrange(
                            "c q t -> q c t")).then_inc(xld[h % 2], 16)
                ng = 0
                for wsel in range(2):
                    hi_s, lo_s = ((p.wupT_hi, p.wupT_lo),
                                  (p.wgT_hi, p.wgT_lo))[wsel]
                    dstscr = (p.upraw, p.gateraw)[wsel]
                    for mc in range(MC3):
                        n = wsel * MC3 + mc
                        if n >= 2:
                            sync.wait_ge(mmg, (n - 1) * 8)
                        for wp, srcw in ((0, hi_s), (1, lo_s)):
                            sync.dma_start(
                                out=wch[n % 2][wp][:],
                                in_=srcw[:, :, mc * 128:(mc + 1) * 128].rearrange(
                                    "k q m -> q k m")).then_inc(wld[n % 2], 16)
                        for tg in range(8):
                            sync.wait_ge(evg, ng + 1)
                            sync.dma_start(
                                out=dstscr[mc, :, tg * TT:(tg + 1) * TT],
                                in_=stg[ng % 2][:]).then_inc(sdg[ng % 2], 16)
                            ng += 1
                sync.wait_ge(sdg[0], 2 * MC3 * 8 // 2 * 16)
                sync.wait_ge(sdg[1], 2 * MC3 * 8 // 2 * 16)
        nc.clear_and_free_semaphores(slist)

def _cc_parts(p, nfold, dst2=None, prescale=1.0):
    K = 3 if dst2 is not None else 2
    def gp_fn(g, cc, ccc, bc):
        g.wait_ge(bc, 2)
        g.dma_start(out=p.cc_in.ap(), in_=p.ccrow[:]).then_inc(cc, 16)
        g.wait_ge(cc, 16)
        g.collective_compute(
            "AllReduce", OP.max,
            replica_groups=[CC_GROUP or list(range(NCORES))],
            ins=[p.cc_in.ap()], outs=[p.cc_out.ap()]).then_inc(ccc, 1)
        g.wait_ge(ccc, 1)
        g.dma_start(out=p.ccrow[:], in_=p.cc_out.ap()).then_inc(cc, 16)
        g.wait_ge(cc, 32)
        g.nop().then_inc(ccc, 1)

    def tensor_fn(tensor, psum, mcols, bc):
        tensor.transpose(psum[0:nfold, 0:128],
                         p.maxacc[:, mcols[0]:mcols[0] + nfold],
                         p.ident_sb[:]).then_inc(bc, 1)
        tensor.wait_ge(bc, 3)
        tensor.matmul(psum[:, 0:K * nfold], p.ones_sb[:],
                      p.row_sb[:, 4:4 + K * nfold],
                      start=True, stop=True).then_inc(bc, 1)

    def vector_fn(vector, psum, dst, cc, ccc, bc):
        vector.wait_ge(bc, 1)
        vector.tensor_copy(p.ccrow[0:nfold, 0:128], psum[0:nfold, 0:128])
        vector.drain()
        if nfold == 1:
            vector.memset(p.ccrow[1:2, 0:128], 0.0) if False else None
        vector.nop().then_inc(bc, 1)
        vector.wait_ge(ccc, 2)
        vector.reduce_max(p.row_sb[0:nfold, 0:1], p.ccrow[0:nfold, 0:128],
                          axis=AX)
        vector.drain()
        vector.tensor_scalar(p.row_sb[0:nfold, 0:1], p.row_sb[0:nfold, 0:1],
                             prescale / QMAX, 1e-12, op0=OP.mult, op1=OP.max)
        vector.drain()
        vector.reciprocal(p.row_sb[0:nfold, 1:2], p.row_sb[0:nfold, 0:1])
        vector.drain()
        if prescale != 1.0:
            vector.tensor_scalar_mul(p.row_sb[0:nfold, 1:2],
                                     p.row_sb[0:nfold, 1:2], prescale)
            vector.drain()
        if dst2 is not None:
            vector.tensor_scalar_mul(p.row_sb[0:nfold, 2:3],
                                     p.row_sb[0:nfold, 0:1], -32769.0)
            vector.drain()
        vector.tensor_tensor(
            p.row_sb[0:nfold, 4:4 + nfold],
            p.row_sb[0:nfold, 0:1].broadcast_to([nfold, nfold]),
            p.ident_sb[0:nfold, 0:nfold], op=OP.mult)
        vector.tensor_tensor(
            p.row_sb[0:nfold, 4 + nfold:4 + 2 * nfold],
            p.row_sb[0:nfold, 1:2].broadcast_to([nfold, nfold]),
            p.ident_sb[0:nfold, 0:nfold], op=OP.mult)
        if dst2 is not None:
            vector.tensor_tensor(
                p.row_sb[0:nfold, 4 + 2 * nfold:4 + 3 * nfold],
                p.row_sb[0:nfold, 2:3].broadcast_to([nfold, nfold]),
                p.ident_sb[0:nfold, 0:nfold], op=OP.mult)
        vector.drain()
        vector.nop().then_inc(bc, 1)
        vector.wait_ge(bc, 4)
        vector.tensor_copy(p.sig_sb[:, dst:dst + 2 * nfold],
                           psum[:, 0:2 * nfold])
        if dst2 is not None:
            vector.tensor_copy(p.sig_sb[:, dst2:dst2 + nfold],
                               psum[:, 2 * nfold:3 * nfold])
        vector.drain()
        vector.nop().then_inc(bc, 1)

    return gp_fn, tensor_fn, vector_fn


def _ladder(vector, z, zi, zc, A, Bb, n):
    vector.tensor_copy(zi[:, 0:n], z[:, 0:n])
    vector.drain()
    vector.tensor_copy(zc[:, 0:n], zi[:, 0:n])
    vector.drain()
    vector.scalar_tensor_tensor(A[:, 0:n], zc[:, 0:n], -0.5, z[:, 0:n],
                                op0=OP.add, op1=OP.is_le)
    vector.scalar_tensor_tensor(Bb[:, 0:n], zc[:, 0:n], 0.5, z[:, 0:n],
                                op0=OP.add, op1=OP.is_le)
    vector.drain()
    vector.tensor_add(z[:, 0:n], zc[:, 0:n], A[:, 0:n])
    vector.drain()
    vector.tensor_add(z[:, 0:n], z[:, 0:n], Bb[:, 0:n])
    vector.drain()


def emit_p5(p):
    nc = p.nc
    HN = 4 * MC3
    HC = T // 4
    gp_fn, t_fn, v_fn = _cc_parts(p, 2, dst2=11)
    with ExitStack() as pc:
        sb = lambda nm, sh, d=F32: pc.enter_context(nc.sbuf_tensor(nm, sh, d))
        upb = [sb(f"p5_up{i}", [128, HC]) for i in range(2)]
        gtb = [sb(f"p5_gt{i}", [128, HC]) for i in range(2)]
        yub = [sb(f"p5_yu{i}", [128, HC]) for i in range(2)]
        ygb = [sb(f"p5_yg{i}", [128, HC]) for i in range(2)]
        ziu = [sb(f"p5_ziu{i}", [128, HC], I32) for i in range(2)]
        zig = [sb(f"p5_zig{i}", [128, HC], I32) for i in range(2)]
        zif = [sb(f"p5_zif{i}", [128, HC], I32) for i in range(2)]
        zcu = [sb(f"p5_zcu{i}", [128, HC]) for i in range(2)]
        Au = [sb(f"p5_Au{i}", [128, HC]) for i in range(2)]
        zcg = [sb(f"p5_zcg{i}", [128, HC]) for i in range(2)]
        Ag = [sb(f"p5_Ag{i}", [128, HC]) for i in range(2)]
        lup = [sb(f"p5_lup{i}", [128, HC]) for i in range(2)]
        lgp = [sb(f"p5_lgp{i}", [128, HC]) for i in range(2)]
        gqb = [sb(f"p5_gq{i}", [128, HC]) for i in range(2)]
        sgb = sb("p5_sg", [128, HC])
        yfb = [sb(f"p5_yf{i}", [128, HC]) for i in range(2)]
        zcf = [sb(f"p5_zcf{i}", [128, HC]) for i in range(2)]
        Af = [sb(f"p5_Af{i}", [128, HC]) for i in range(2)]
        lfp = [sb(f"p5_lfp{i}", [128, HC]) for i in range(2)]
        t1b = sb("p5_t1", [128, HC])
        mb = [sb(f"p5_m{i}", [128, HC]) for i in range(2)]
        tmx = sb("p5_tmx", [128, 1])
        ps = pc.enter_context(nc.psum_tensor("p5_ps", [128, TT], F32))
        slist = []
        def sem(nm):
            h = nc.alloc_semaphore(name=nm)
            slist.append(h)
            return h
        cc = sem("p5_cc"); ccc = sem("p5_ccc"); bc = sem("p5_bc")
        ld = [sem(f"p5_ld{i}") for i in range(2)]
        aq1 = sem("p5_aq1"); aq2 = sem("p5_aq2")
        vqa = sem("p5_vqa")
        pq1 = sem("p5_pq1"); pq2 = sem("p5_pq2")
        su = [sem(f"p5_su{i}") for i in range(2)]
        sm = [sem(f"p5_sm{i}") for i in range(2)]
        def addr(h):
            return h // 4, (h % 4) * HC
        with nc.Block() as blk:
            @blk.tensor
            def _(tensor):
                t_fn(tensor, ps, [8, 9], bc)

            @blk.scalar
            def _(scalar):
                scalar.wait_ge(bc, 5)
                for r in range(HN + 1):
                    if r < HN:
                        scalar.wait_ge(ld[r % 2], (r // 2 + 1) * 32)
                        if r >= 2:
                            scalar.wait_ge(vqa, r - 1)
                        scalar.activation(yub[r % 2][:], upb[r % 2][:],
                                          AF.Identity, bias=p.b32k[:, 0:1],
                                          scale=p.sig_sb[:, 5:6])
                        scalar.activation(ziu[r % 2][:], upb[r % 2][:],
                                          AF.Identity, bias=p.b32k[:, 0:1],
                                          scale=p.sig_sb[:, 5:6])
                        scalar.activation(ygb[r % 2][:], gtb[r % 2][:],
                                          AF.Identity, bias=p.b32k[:, 0:1],
                                          scale=p.sig_sb[:, 6:7])
                        scalar.activation(zig[r % 2][:], gtb[r % 2][:],
                                          AF.Identity, bias=p.b32k[:, 0:1],
                                          scale=p.sig_sb[:, 6:7]).then_inc(aq1, 1)
                    k = r - 1
                    if 0 <= k < HN:
                        scalar.wait_ge(pq1, r)
                        if r >= 2:
                            scalar.wait_ge(vqa, r)
                        scalar.activation(sgb[:], lgp[k % 2][:], AF.Sigmoid,
                                          bias=p.sig_sb[:, 12:13],
                                          scale=p.sig_sb[:, 4:5])
                        scalar.activation(gqb[k % 2][:], lgp[k % 2][:],
                                          AF.Identity, bias=p.sig_sb[:, 12:13],
                                          scale=p.sig_sb[:, 4:5])
                        scalar.drain()
                        scalar.activation(yfb[k % 2][:], sgb[:], AF.Identity,
                                          bias=p.b32k[:, 0:1], scale=65536.0)
                        scalar.activation(zif[k % 2][:], sgb[:], AF.Identity,
                                          bias=p.b32k[:, 0:1],
                                          scale=65536.0).then_inc(aq2, 1)

            @blk.vector
            def _(vector):
                v_fn(vector, ps, 3, cc, ccc, bc)
                for r in range(HN + 2):
                    g1 = r < HN
                    k2 = r - 1
                    g2 = 0 <= k2 < HN
                    k3 = r - 2
                    g3 = 0 <= k3 < HN
                    if g1:
                        vector.wait_ge(aq1, r + 1)
                        if r >= 2:
                            vector.wait_ge(pq1, r - 1)
                    if g2:
                        vector.wait_ge(aq2, r)
                        if r >= 3:
                            vector.wait_ge(pq2, r - 2)
                    if g3:
                        vector.wait_ge(pq2, r - 1)
                        if r >= 4:
                            vector.wait_ge(sm[k3 % 2], (k3 // 2) * 16)
                    if g1:
                        vector.tensor_copy(zcu[r % 2][:], ziu[r % 2][:])
                        vector.tensor_copy(zcg[r % 2][:], zig[r % 2][:])
                    if g2:
                        vector.tensor_copy(zcf[k2 % 2][:], zif[k2 % 2][:])
                    if g3:
                        vector.tensor_scalar_min(t1b[:], lfp[k3 % 2][:],
                                                 98304.0)
                    vector.drain()
                    ins = None
                    if g1:
                        vector.scalar_tensor_tensor(
                            Au[r % 2][:], zcu[r % 2][:], 0.0, yub[r % 2][:],
                            op0=OP.add, op1=OP.is_le)
                        ins = vector.scalar_tensor_tensor(
                            Ag[r % 2][:], zcg[r % 2][:], 0.0, ygb[r % 2][:],
                            op0=OP.add, op1=OP.is_le)
                    if g2:
                        ins = vector.scalar_tensor_tensor(
                            Af[k2 % 2][:], zcf[k2 % 2][:], 0.0, yfb[k2 % 2][:],
                            op0=OP.add, op1=OP.is_le)
                    if g3:
                        ins = vector.scalar_tensor_tensor(
                            mb[k3 % 2][:], t1b[:], -32769.0, gqb[k3 % 2][:],
                            op0=OP.add, op1=OP.mult)
                    if ins is not None:
                        ins.then_inc(vqa, 1)
                    vector.drain()
                    if g3:
                        vector.reduce_max(tmx[:], mb[k3 % 2][:], axis=AX,
                                          apply_absolute_value=True)
                        vector.drain()
                        vector.tensor_max(p.maxacc[:, 10:11],
                                          p.maxacc[:, 10:11], tmx[:])
                        vector.drain()

            @blk.gpsimd
            def _(g):
                gp_fn(g, cc, ccc, bc)
                for r in range(HN + 1):
                    if r < HN:
                        g.wait_ge(vqa, r + 1)
                        if r >= 2:
                            g.wait_ge(su[r % 2], (r // 2) * 16)
                            g.wait_ge(aq2, r - 1)
                        g.tensor_tensor(lup[r % 2][:], zcu[r % 2][:],
                                        Au[r % 2][:], op=OP.add)
                        g.tensor_tensor(lgp[r % 2][:], zcg[r % 2][:],
                                        Ag[r % 2][:], op=OP.add).then_inc(pq1, 1)
                    k = r - 1
                    if 0 <= k < HN:
                        g.wait_ge(vqa, min(r + 1, HN + 2))
                        g.tensor_tensor(lfp[k % 2][:], zcf[k % 2][:],
                                        Af[k % 2][:], op=OP.add).then_inc(pq2, 1)

            @blk.sync
            def _(sync):
                for r in range(HN + 2):
                    if r < HN:
                        if r >= 2:
                            sync.wait_ge(aq1, r - 1)
                        mc, co = addr(r)
                        sync.dma_start(out=upb[r % 2][:],
                                       in_=p.upraw[mc, :, co:co + HC]
                                       ).then_inc(ld[r % 2], 16)
                        sync.dma_start(out=gtb[r % 2][:],
                                       in_=p.gateraw[mc, :, co:co + HC]
                                       ).then_inc(ld[r % 2], 16)
                    k = r - 1
                    if 0 <= k < HN:
                        sync.wait_ge(pq1, r)
                        mc, co = addr(k)
                        sync.dma_start(out=p.upq[mc, :, co:co + HC],
                                       in_=lup[k % 2][:]).then_inc(su[k % 2], 16)
                    k = r - 2
                    if 0 <= k < HN:
                        sync.wait_ge(vqa, r + 1)
                        mc, co = addr(k)
                        sync.dma_start(out=p.g2raw[mc, :, co:co + HC],
                                       in_=mb[k % 2][:]).then_inc(sm[k % 2], 16)
                sync.wait_ge(su[0], HN // 2 * 16)
                sync.wait_ge(su[1], HN // 2 * 16)
                sync.wait_ge(sm[0], HN // 2 * 16)
                sync.wait_ge(sm[1], HN // 2 * 16)
        nc.clear_and_free_semaphores(slist)


def emit_p6(p):
    nc = p.nc
    HN = 4 * MC3
    HC = T // 4
    gp_fn, t_fn, v_fn = _cc_parts(p, 1, dst2=13, prescale=2.0 ** -16)
    with ExitStack() as pc:
        sb = lambda nm, sh, d=F32: pc.enter_context(nc.sbuf_tensor(nm, sh, d))
        g2b = [sb(f"p6_g2{i}", [128, HC]) for i in range(2)]
        uqb = [sb(f"p6_uq{i}", [128, HC]) for i in range(2)]
        y2b = [sb(f"p6_y2{i}", [128, HC]) for i in range(2)]
        zi2 = [sb(f"p6_zi{i}", [128, HC], I32) for i in range(2)]
        upd = [sb(f"p6_upd{i}", [128, HC]) for i in range(2)]
        zc2 = [sb(f"p6_zc{i}", [128, HC]) for i in range(2)]
        A2 = [sb(f"p6_A{i}", [128, HC]) for i in range(2)]
        l2p = [sb(f"p6_l2{i}", [128, HC]) for i in range(2)]
        g2q = [sb(f"p6_g2q{i}", [128, HC]) for i in range(2)]
        ob = [sb(f"p6_o{i}", [128, HC]) for i in range(2)]
        tmx = sb("p6_tmx", [128, 1])
        ps = pc.enter_context(nc.psum_tensor("p6_ps", [128, TT], F32))
        slist = []
        def sem(nm):
            h = nc.alloc_semaphore(name=nm)
            slist.append(h)
            return h
        cc = sem("p6_cc"); ccc = sem("p6_ccc"); bc = sem("p6_bc")
        ld = [sem(f"p6_ld{i}") for i in range(2)]
        aq1 = sem("p6_aq1"); aq2 = sem("p6_aq2")
        vqa = sem("p6_vqa")
        pq1 = sem("p6_pq1"); pq2 = sem("p6_pq2")
        so = [sem(f"p6_so{i}") for i in range(2)]
        def addr(h):
            return h // 4, (h % 4) * HC
        with nc.Block() as blk:
            @blk.tensor
            def _(tensor):
                t_fn(tensor, ps, [10], bc)

            @blk.scalar
            def _(scalar):
                scalar.wait_ge(bc, 5)
                for r in range(HN + 1):
                    if r < HN:
                        scalar.wait_ge(ld[r % 2], (r // 2 + 1) * 32)
                        if r >= 2:
                            scalar.wait_ge(vqa, r - 1)
                            scalar.wait_ge(pq2, r - 1)
                        scalar.activation(y2b[r % 2][:], g2b[r % 2][:],
                                          AF.Identity, bias=p.b32k[:, 0:1],
                                          scale=p.sig_sb[:, 8:9])
                        scalar.activation(zi2[r % 2][:], g2b[r % 2][:],
                                          AF.Identity, bias=p.b32k[:, 0:1],
                                          scale=p.sig_sb[:, 8:9])
                        scalar.activation(upd[r % 2][:], uqb[r % 2][:],
                                          AF.Identity, bias=p.sig_sb[:, 11:12],
                                          scale=p.sig_sb[:, 3:4]).then_inc(aq1, 1)
                    k = r - 1
                    if 0 <= k < HN:
                        scalar.wait_ge(pq1, r)
                        if r >= 2:
                            scalar.wait_ge(pq2, r - 1)
                        scalar.activation(g2q[k % 2][:], l2p[k % 2][:],
                                          AF.Identity, bias=p.sig_sb[:, 13:14],
                                          scale=p.sig_sb[:, 7:8]).then_inc(aq2, 1)

            @blk.vector
            def _(vector):
                v_fn(vector, ps, 7, cc, ccc, bc)
                for r in range(HN + 2):
                    g1 = r < HN
                    k3 = r - 2
                    g3 = 0 <= k3 < HN
                    if g1:
                        vector.wait_ge(aq1, r + 1)
                        if r >= 2:
                            vector.wait_ge(pq1, r - 1)
                    if g3:
                        vector.wait_ge(pq2, r - 1)
                    if g1:
                        vector.tensor_copy(zc2[r % 2][:], zi2[r % 2][:])
                    if g3:
                        vector.reduce_max(tmx[:], ob[k3 % 2][:], axis=AX,
                                          apply_absolute_value=True)
                    vector.drain()
                    ins = None
                    if g1:
                        ins = vector.scalar_tensor_tensor(
                            A2[r % 2][:], zc2[r % 2][:], 0.0, y2b[r % 2][:],
                            op0=OP.add, op1=OP.is_le)
                    if g3:
                        ins = vector.tensor_max(p.maxacc[:, 11:12],
                                                p.maxacc[:, 11:12], tmx[:])
                    if ins is not None:
                        ins.then_inc(vqa, 1)
                    vector.drain()

            @blk.gpsimd
            def _(g):
                gp_fn(g, cc, ccc, bc)
                for r in range(HN + 1):
                    if r < HN:
                        g.wait_ge(vqa, r + 1)
                        if r >= 2:
                            g.wait_ge(aq2, r - 1)
                        g.tensor_tensor(l2p[r % 2][:], zc2[r % 2][:],
                                        A2[r % 2][:], op=OP.add).then_inc(pq1, 1)
                    k = r - 1
                    if 0 <= k < HN:
                        g.wait_ge(aq2, r)
                        if r >= 3:
                            g.wait_ge(vqa, r)
                            g.wait_ge(so[k % 2], (k // 2) * 16)
                        g.tensor_tensor(ob[k % 2][:], g2q[k % 2][:],
                                        upd[k % 2][:], op=OP.mult).then_inc(pq2, 1)

            @blk.sync
            def _(sync):
                for r in range(HN + 1):
                    if r < HN:
                        if r >= 2:
                            sync.wait_ge(aq1, r - 1)
                        mc, co = addr(r)
                        sync.dma_start(out=g2b[r % 2][:],
                                       in_=p.g2raw[mc, :, co:co + HC]
                                       ).then_inc(ld[r % 2], 16)
                        sync.dma_start(out=uqb[r % 2][:],
                                       in_=p.upq[mc, :, co:co + HC]
                                       ).then_inc(ld[r % 2], 16)
                    k = r - 1
                    if 0 <= k < HN:
                        sync.wait_ge(pq2, r)
                        mc, co = addr(k)
                        sync.dma_start(out=p.oraw[mc, :, co:co + HC],
                                       in_=ob[k % 2][:]).then_inc(so[k % 2], 16)
                sync.wait_ge(so[0], HN // 2 * 16)
                sync.wait_ge(so[1], HN // 2 * 16)
        nc.clear_and_free_semaphores(slist)

def emit_p7(p):
    nc = p.nc
    gp_fn, t_fn, v_fn = _cc_parts(p, 1)
    T7 = 256
    NT7 = T // T7
    NQG = 6
    QW = 4 * T7
    with ExitStack() as pc:
        sb = lambda nm, sh, d=F32: pc.enter_context(nc.sbuf_tensor(nm, sh, d))
        orb = [sb(f"p7_or{i}", [128, MC3, T7]) for i in range(2)]
        oph = [sb(f"p7_oph{i}", [128, MC3, T7], BF16) for i in range(2)]
        opl = [sb(f"p7_opl{i}", [128, MC3, T7], BF16) for i in range(2)]
        z = sb("p7_z", [128, QW]); zi = sb("p7_zi", [128, QW], I32)
        zc = sb("p7_zc", [128, QW]); A = sb("p7_A", [128, QW])
        Bb = sb("p7_B", [128, QW])
        wdc = [[sb(f"p7_wd{i}_{wp}", [128, MC3, 128], BF16) for wp in range(2)]
               for i in range(2)]
        x1c = sb("p7_x1", [128, KC, T7])
        xr2 = sb("p7_xr2", [128, KC, T7])
        sq2 = sb("p7_sq2", [128, KC, T7])
        outst = [sb(f"p7_out{i}", [128, 2, DM]) for i in range(2)]
        ps = [pc.enter_context(nc.psum_tensor(f"p7_ps{i}", [128, T7], F32))
              for i in range(6)]
        m_sb = z[:, 0:T7]
        msq = z[:, T7:2 * T7]
        v_sb = zc[:, 0:T7]
        sd_sb = zc[:, T7:2 * T7]
        slist = []
        def sem(nm):
            h = nc.alloc_semaphore(name=nm)
            slist.append(h)
            return h
        cc = sem("p7_cc"); ccc = sem("p7_ccc"); bc = sem("p7_bc")
        old = [sem(f"p7_old{i}") for i in range(2)]
        x1l = sem("p7_x1l")
        wls = [sem(f"p7_wl{i}") for i in range(2)]
        qg = sem("p7_qg")
        mmg = sem("p7_mmg")
        evg = sem("p7_evg")
        LT1 = sem("p7_LT1"); LT2 = sem("p7_LT2"); LSq = sem("p7_LSq")
        LNv = sem("p7_LNv"); LNs = sem("p7_LNs"); LNx = sem("p7_LNx")
        otr = sem("p7_otr"); oev = sem("p7_oev")
        osd = [sem(f"p7_osd{i}") for i in range(2)]
        with nc.Block() as blk:
            @blk.gpsimd
            def _(g):
                gp_fn(g, cc, ccc, bc)

            @blk.tensor
            def _(tensor):
                t_fn(tensor, ps[2], [11], bc)
                ntr = 0
                for i in range(NT7):
                    tensor.wait_ge(qg, (i + 1) * NQG)
                    for mc in range(KC):
                        n = i * KC + mc
                        tensor.wait_ge(wls[n % 2], (n // 2 + 1) * 32)
                        if n >= 2:
                            tensor.wait_ge(evg, n - 1)
                        k3 = 0
                        for (xs, wp) in ((oph, 0), (opl, 0), (oph, 1)):
                            for kc in range(MC3):
                                ins = tensor.matmul(
                                    ps[n % 2][:],
                                    wdc[n % 2][wp][:, kc, :],
                                    xs[i % 2][:, kc, :],
                                    start=(k3 == 0), stop=(k3 == 3 * MC3 - 1))
                                k3 += 1
                        ins.then_inc(mmg, 1)
                    tensor.wait_ge(evg, (i + 1) * KC)
                    for kc in range(KC):
                        ins = tensor.matmul(ps[2][:], p.ones_sb[:],
                                            xr2[:, kc, :],
                                            start=(kc == 0), stop=(kc == KC - 1))
                    ins.then_inc(LT1, 1)
                    tensor.wait_ge(LSq, i + 1)
                    for kc in range(KC):
                        ins = tensor.matmul(ps[3][:], p.ones_sb[:],
                                            sq2[:, kc, :],
                                            start=(kc == 0), stop=(kc == KC - 1))
                    ins.then_inc(LT2, 1)
                    tensor.wait_ge(LNx, i + 1)
                    for s4 in range(2):
                        for kc in range(KC):
                            if ntr >= 2:
                                tensor.wait_ge(oev, ntr - 1)
                            tensor.transpose(
                                ps[4 + ntr % 2][0:128, 0:128],
                                xr2[:, kc, s4 * 128:(s4 + 1) * 128],
                                p.ident_sb[:]).then_inc(otr, 1)
                            ntr += 1

            @blk.scalar
            def _(scalar):
                for i in range(NT7):
                    scalar.wait_ge(evg, (i + 1) * KC)
                    if i >= 1:
                        scalar.wait_ge(LT2, i)
                        scalar.wait_ge(osd[(i - 1) % 2], ((i - 1) // 2 + 1) * 32)
                    scalar.activation(sq2[:], xr2[:], AF.Square).then_inc(LSq, 1)
                    scalar.drain()
                    scalar.wait_ge(LNv, i + 1)
                    scalar.activation(sd_sb, v_sb, AF.Sqrt, bias=p.eps_sb[:]).then_inc(LNs, 1)
                    scalar.drain()

            @blk.vector
            def _(vector):
                v_fn(vector, ps[2], 9, cc, ccc, bc)
                ntr = 0
                for i in range(NT7):
                    vector.wait_ge(old[i % 2], (i // 2 + 1) * 16)
                    for qv in range(NQG):
                        src = orb[i % 2][:, 4 * qv:4 * qv + 4, :].rearrange(
                            "q c t -> q (c t)")
                        vector.tensor_scalar_mul(z[:], src, p.sig_sb[:, 10:11])
                        vector.drain()
                        _ladder(vector, z, zi, zc, A, Bb, QW)
                        vector.tensor_scalar(zc[:], z[:], -1.0,
                                             p.sig_sb[:, 9:10],
                                             op0=OP.add, op1=OP.mult)
                        vector.drain()
                        zc3 = zc[:].rearrange("q (c t) -> q c t", c=4)
                        dh = oph[i % 2][:, 4 * qv:4 * qv + 4, :]
                        dl = opl[i % 2][:, 4 * qv:4 * qv + 4, :]
                        vector.tensor_copy(dh, zc3)
                        vector.drain()
                        vector.scalar_tensor_tensor(dl, zc3, 1.0, dh,
                                                    op0=OP.mult, op1=OP.subtract)
                        vector.drain()
                        vector.nop().then_inc(qg, 1)
                    vector.wait_ge(x1l, (i + 1) * 16)
                    for mc in range(KC):
                        n = i * KC + mc
                        vector.wait_ge(mmg, n + 1)
                        vector.tensor_add(xr2[:, mc, :], ps[n % 2][:],
                                          x1c[:, mc, :])
                        vector.drain()
                        vector.nop().then_inc(evg, 1)
                    vector.wait_ge(LT1, i + 1)
                    vector.tensor_scalar_mul(m_sb, ps[2][:], 1.0 / DM)
                    vector.drain()
                    vector.tensor_mul(msq, m_sb, m_sb)
                    vector.drain()
                    vector.wait_ge(LT2, i + 1)
                    vector.scalar_tensor_tensor(v_sb, ps[3][:], 1.0 / DM, msq,
                                                op0=OP.mult, op1=OP.subtract)
                    vector.drain()
                    vector.nop().then_inc(LNv, 1)
                    vector.wait_ge(LNs, i + 1)
                    vector.reciprocal(sd_sb, sd_sb)
                    vector.drain()
                    vector.tensor_tensor(
                        xr2[:], xr2[:],
                        m_sb.rearrange("q t -> q () t").broadcast_to([128, KC, T7]),
                        op=OP.subtract)
                    vector.drain()
                    vector.tensor_tensor(
                        xr2[:], xr2[:],
                        sd_sb.rearrange("q t -> q () t").broadcast_to([128, KC, T7]),
                        op=OP.mult)
                    vector.drain()
                    vector.nop().then_inc(LNx, 1)
                    if i >= 2:
                        vector.wait_ge(osd[i % 2], (i // 2) * 32)
                    for s4 in range(2):
                        for kc in range(KC):
                            vector.wait_ge(otr, ntr + 1)
                            vector.tensor_copy(
                                outst[i % 2][:, s4, kc * 128:(kc + 1) * 128],
                                ps[4 + ntr % 2][0:128, 0:128])
                            vector.drain()
                            vector.nop().then_inc(oev, 1)
                            ntr += 1

            @blk.sync
            def _(sync):
                for i in range(NT7):
                    if i >= 2:
                        sync.wait_ge(qg, (i - 1) * NQG)
                    sync.dma_start(
                        out=orb[i % 2][:],
                        in_=p.oraw[:, :, i * T7:(i + 1) * T7].rearrange(
                            "c q t -> q c t")).then_inc(old[i % 2], 16)
                    sync.wait_ge(evg, i * KC)
                    sync.dma_start(
                        out=x1c[:],
                        in_=p.x1[:, :, i * T7:(i + 1) * T7].rearrange(
                            "c q t -> q c t")).then_inc(x1l, 16)
                    for mc in range(KC):
                        n = i * KC + mc
                        if n >= 2:
                            sync.wait_ge(mmg, n - 1)
                        for wp, srcw in ((0, p.wdT_hi), (1, p.wdT_lo)):
                            sync.dma_start(
                                out=wdc[n % 2][wp][:],
                                in_=srcw[:, :, mc * 128:(mc + 1) * 128].rearrange(
                                    "k q m -> q k m")).then_inc(wls[n % 2], 16)
                    if i >= 1:
                        for s4 in range(2):
                            sync.wait_ge(oev, (i - 1) * 16 + (s4 + 1) * KC)
                            r0 = (i - 1) * T7 + s4 * 128
                            sync.dma_start(
                                out=p.out[r0:r0 + 128, :],
                                in_=outst[(i - 1) % 2][:, s4, :]
                                ).then_inc(osd[(i - 1) % 2], 16)
                for s4 in range(2):
                    sync.wait_ge(oev, (NT7 - 1) * 16 + (s4 + 1) * KC)
                    r0 = (NT7 - 1) * T7 + s4 * 128
                    sync.dma_start(out=p.out[r0:r0 + 128, :],
                                   in_=outst[(NT7 - 1) % 2][:, s4, :]
                                   ).then_inc(osd[(NT7 - 1) % 2], 16)
                sync.wait_ge(osd[0], NT7 // 2 * 32)
                sync.wait_ge(osd[1], NT7 // 2 * 32)
        nc.clear_and_free_semaphores(slist)

def emit_debug(p):
    nc = p.nc
    if not p.debug_taps:
        return
    srcs = dict(xf=p.xf, qf=p.qf, kf=p.kf, af=p.af, vt=p.vt, x1=p.x1,
                upraw=p.upraw, gateraw=p.gateraw, upq=p.upq, g2raw=p.g2raw,
                oraw=p.oraw)
    with ExitStack() as pc:
        st = nc.alloc_semaphore(name="dbg_st")
        with nc.Block() as blk:
            @blk.sync
            def _(sync):
                n = 0
                for name, ap in p.debug_taps.items():
                    sync.dma_start(out=ap, in_=srcs[name]).then_inc(st, 16)
                    n += 1
                sync.wait_ge(st, n * 16)


def make_consts():
    ident = np.eye(128, dtype=np.float32)
    ones = np.ones((128, 128), dtype=np.float32)
    cmask = np.zeros((128, 64), dtype=np.float32)
    for u in range(2):
        for t in range(64):
            cmask[u * 64 + t, t + 1:] = NEG
    tmask = np.zeros((128, 64), dtype=np.float32)
    for pr in range(128):
        sub_s, s_ = (pr % 64) // 32, pr % 32
        for col in range(64):
            sub_t, tau = col // 32, col % 32
            if sub_s == sub_t and s_ < tau:
                tmask[pr, col] = 1.0
    return ident, ones, cmask, tmask


def make_in_maps(x, W):
    ident, ones, cmask, tmask = make_consts()
    shared = dict(W)
    shared.update(ident=ident, ones=ones, cmask=cmask, tmask=tmask)
    in_maps = []
    for c in range(NCORES):
        m = dict(shared)
        m["x"] = np.ascontiguousarray(
            np.asarray(x[c * B_PER:(c + 1) * B_PER], dtype=np.float32
                       ).reshape(T, DM))
        in_maps.append(m)
    return in_maps


def kernel(L0_attn_x, L0_attn_Wq, L0_attn_bq, L0_attn_Wk, L0_attn_bk,
           L0_attn_Wv, L0_attn_bv, L0_attn_Wg1, L0_attn_bg1, L0_attn_Wg2,
           L0_attn_bg2, L0_ffn_w_gate, L0_ffn_w_up, L0_ffn_w_down):
    W = dict(
        Wq=np.ascontiguousarray(L0_attn_Wq), Wk=np.ascontiguousarray(L0_attn_Wk),
        Wv=np.ascontiguousarray(L0_attn_Wv),
        bq=np.ascontiguousarray(L0_attn_bq), bk=np.ascontiguousarray(L0_attn_bk),
        bv=np.ascontiguousarray(L0_attn_bv),
        Wg1=np.ascontiguousarray(L0_attn_Wg1), bg1=np.ascontiguousarray(L0_attn_bg1),
        Wg2=np.ascontiguousarray(L0_attn_Wg2), bg2=np.ascontiguousarray(L0_attn_bg2),
        wgate=np.ascontiguousarray(L0_ffn_w_gate), wup=np.ascontiguousarray(L0_ffn_w_up),
        wdown=np.ascontiguousarray(L0_ffn_w_down))
    nc, p = build()
    in_maps = make_in_maps(np.asarray(L0_attn_x), W)
    res = run_bass_kernel_spmd(nc, in_maps, core_ids=list(range(NCORES)))
    outs = [np.asarray(res.results[c]["out"]).reshape(B_PER, L, DM)
            for c in range(NCORES)]
    return np.concatenate(outs, axis=0)


# version 04
# speedup vs baseline: 1.1312x; 1.1045x over previous
import numpy as np
from contextlib import ExitStack

import concourse.bass as bass
import concourse.mybir as mybir
from concourse.bass_utils import run_bass_kernel_spmd

F32 = mybir.dt.float32
BF16 = mybir.dt.bfloat16
I32 = mybir.dt.int32
OP = mybir.AluOpType
AF = mybir.ActivationFunctionType
AX = mybir.AxisListType.X

NCORES = 8
B_PER = 2
L = 2048
DM = 1024
T = B_PER * L
TT = 512
NTT = T // TT
KC = DM // 128
H = 16
D = 64
C = 64
CS = 32
NCH = L // C
I3 = 3072
MC3 = I3 // 128
QMAX = 32767.0
NEG = -1.0e30
PHASES = 8
P2_STEPS = None
P2_SKIP = set()
CC_GROUP = None


class P:
    pass


def build(phases=PHASES, debug_taps=()):
    nc = bass.Bass("TRN2", target_bir_lowering=False)
    p = P()
    p.nc = nc
    p.phases = phases
    dt = nc.dram_tensor

    p.x = dt("x", [T, DM], F32, kind="ExternalInput").ap()
    p.Wq = dt("Wq", [DM, DM], F32, kind="ExternalInput").ap()
    p.Wk = dt("Wk", [DM, DM], F32, kind="ExternalInput").ap()
    p.Wv = dt("Wv", [DM, DM], F32, kind="ExternalInput").ap()
    p.bq = dt("bq", [DM], F32, kind="ExternalInput").ap()
    p.bk = dt("bk", [DM], F32, kind="ExternalInput").ap()
    p.bv = dt("bv", [DM], F32, kind="ExternalInput").ap()
    p.Wg1 = dt("Wg1", [DM, 16], F32, kind="ExternalInput").ap()
    p.bg1 = dt("bg1", [16], F32, kind="ExternalInput").ap()
    p.Wg2 = dt("Wg2", [16, DM], F32, kind="ExternalInput").ap()
    p.bg2 = dt("bg2", [DM], F32, kind="ExternalInput").ap()
    p.wgate = dt("wgate", [I3, DM], F32, kind="ExternalInput").ap()
    p.wup = dt("wup", [I3, DM], F32, kind="ExternalInput").ap()
    p.wdown = dt("wdown", [DM, I3], F32, kind="ExternalInput").ap()
    p.ident = dt("ident", [128, 128], F32, kind="ExternalInput").ap()
    p.ones = dt("ones", [128, 128], F32, kind="ExternalInput").ap()
    p.cmask = dt("cmask", [128, 64], F32, kind="ExternalInput").ap()
    p.tmask = dt("tmask", [128, 64], F32, kind="ExternalInput").ap()

    p.out = dt("out", [T, DM], F32, kind="ExternalOutput").ap()

    p.xf = dt("xf_scr", [KC, 128, T], F32).ap()
    p.qf = dt("qf_scr", [KC, 128, T], F32).ap()
    p.kf = dt("kf_scr", [KC, 128, T], F32).ap()
    p.af = dt("af_scr", [KC, 128, T], F32).ap()
    p.vt = dt("vt_scr", [T, DM], F32).ap()
    p.x1 = dt("x1_scr", [KC, 128, T], F32).ap()
    p.upraw = dt("upraw_scr", [MC3, 128, T], F32).ap()
    p.gateraw = dt("gateraw_scr", [MC3, 128, T], F32).ap()
    p.upq = dt("upq_scr", [MC3, 128, T], F32).ap()
    p.g2raw = dt("g2raw_scr", [MC3, 128, T], F32).ap()
    p.oraw = dt("oraw_scr", [MC3, 128, T], F32).ap()
    p.wupT_hi = dt("wupT_hi", [KC, 128, I3], BF16).ap()
    p.wupT_lo = dt("wupT_lo", [KC, 128, I3], BF16).ap()
    p.wgT_hi = dt("wgT_hi", [KC, 128, I3], BF16).ap()
    p.wgT_lo = dt("wgT_lo", [KC, 128, I3], BF16).ap()
    p.wdT_hi = dt("wdT_hi", [MC3, 128, DM], BF16).ap()
    p.wdT_lo = dt("wdT_lo", [MC3, 128, DM], BF16).ap()
    p.cc_in = dt("cc_in", [2, 128], F32)
    p.cc_out = dt("cc_out", [2, 128], F32)

    p.debug_taps = {}
    for name, shape in debug_taps:
        p.debug_taps[name] = dt("dbg_" + name, list(shape), F32,
                                kind="ExternalOutput").ap()

    with ExitStack() as ctx:
        p.semctx = ctx
        sbt = lambda nm, sh, d=F32: ctx.enter_context(nc.sbuf_tensor(nm, sh, d))
        p.ident_sb = sbt("ident_sb", [128, 128])
        p.ones_sb = sbt("ones_sb", [128, 128])
        p.cmask_sb = sbt("cmask_sb", [128, 64])
        p.tmask_sb = sbt("tmask_sb", [128, 64])
        p.bq_sb = sbt("bq_sb", [128, KC])
        p.bk_sb = sbt("bk_sb", [128, KC])
        p.bg1_sb = sbt("bg1_sb", [16, 1])
        p.bg2_sb = sbt("bg2_sb", [128, KC])
        p.bvrow = sbt("bvrow", [128, DM])
        p.maxacc = sbt("maxacc", [128, 16])
        p.sig_sb = sbt("sig_sb", [128, 16])
        p.row_sb = sbt("row_sb", [128, 16])
        p.ccrow = sbt("ccrow", [2, 128])
        p.eps_sb = sbt("eps_sb", [128, 1])
        p.b32k = sbt("b32k", [128, 1])

        emit_setup(p)
        if phases >= 1:
            emit_p0(p)
        if phases >= 2:
            emit_p1(p)
        if phases >= 3:
            emit_p2(p)
        if phases >= 4:
            emit_p3(p)
        if phases >= 5:
            emit_p4(p)
        if phases >= 6:
            emit_p5(p)
        if phases >= 7:
            emit_p6(p)
        if phases >= 8:
            emit_p7(p)
        emit_debug(p)
    return nc, p


def emit_setup(p):
    nc = p.nc
    with nc.semaphore("su_ld") as ld, nc.semaphore("su_ms") as ms, nc.Block() as blk:
        @blk.sync
        def _(sync):
            n = 0
            def d(out, in_):
                nonlocal n
                with nc.allow_non_contiguous_dma(reason="tiny one-time bias loads"):
                    sync.dma_start(out=out, in_=in_).then_inc(ld, 16)
                n += 1
            d(p.ident_sb[:], p.ident)
            d(p.ones_sb[:], p.ones)
            d(p.cmask_sb[:], p.cmask)
            d(p.tmask_sb[:], p.tmask)
            d(p.bq_sb[:], p.bq.rearrange("(c q) -> q c", q=128))
            d(p.bk_sb[:], p.bk.rearrange("(c q) -> q c", q=128))
            d(p.bg2_sb[:], p.bg2.rearrange("(c q) -> q c", q=128))
            d(p.bg1_sb[:], p.bg1.rearrange("g -> g ()"))
            sync.wait_ge(ms, 1)
            d(p.bvrow[0:1, :], p.bv.rearrange("f -> () f"))
            sync.wait_ge(ld, n * 16)

        @blk.vector
        def _(vector):
            vector.memset(p.bvrow[:], 0.0)
            vector.memset(p.maxacc[:], 0.0)
            vector.memset(p.row_sb[:], 0.0)
            vector.memset(p.ccrow[:], 0.0)
            vector.memset(p.eps_sb[:], 1e-5)
            vector.memset(p.b32k[:], 32768.5)
            vector.drain()
            vector.nop().then_inc(ms, 1)


def emit_p0(p):
    nc = p.nc
    with ExitStack() as pc:
        sb = lambda nm, sh, d=F32: pc.enter_context(nc.sbuf_tensor(nm, sh, d))
        xin = [sb(f"p0_xin{i}", [128, 4, DM]) for i in range(2)]
        xfo = [sb(f"p0_xfo{i}", [128, KC, TT]) for i in range(2)]
        ps = [pc.enter_context(nc.psum_tensor(f"p0_ps{i}", [128, 128], F32))
              for i in range(4)]
        slist = []
        def sem(nm):
            h = nc.alloc_semaphore(name=nm)
            slist.append(h)
            return h
        ld = [sem(f"p0_ld{i}") for i in range(2)]
        tr = sem("p0_tr")
        ev = sem("p0_ev")
        st = [sem(f"p0_st{i}") for i in range(2)]
        NTR = 32
        with nc.Block() as blk:
            @blk.sync
            def _(sync):
                def store(j):
                    sync.wait_ge(ev, (j + 1) * NTR)
                    sync.dma_start(
                        out=p.xf[:, :, j * TT:(j + 1) * TT].rearrange("c q t -> q c t"),
                        in_=xfo[j % 2][:]).then_inc(st[j % 2], 16)
                for i in range(NTT):
                    if i >= 2:
                        sync.wait_ge(tr, (i - 1) * NTR)
                    src = p.x[i * TT:(i + 1) * TT, :].rearrange(
                        "(s q) f -> q s f", q=128)
                    sync.dma_start(out=xin[i % 2][:], in_=src).then_inc(ld[i % 2], 16)
                    if i >= 1:
                        store(i - 1)
                store(NTT - 1)
                sync.wait_ge(st[0], NTT // 2 * 16)
                sync.wait_ge(st[1], NTT // 2 * 16)

            @blk.tensor
            def _(tensor):
                n = 0
                for i in range(NTT):
                    tensor.wait_ge(ld[i % 2], (i // 2 + 1) * 16)
                    for s in range(4):
                        for c in range(KC):
                            if n >= 4:
                                tensor.wait_ge(ev, n - 3)
                            tensor.transpose(
                                ps[n % 4][:], xin[i % 2][:, s, c * 128:(c + 1) * 128],
                                p.ident_sb[:]).then_inc(tr, 1)
                            n += 1

            @blk.vector
            def _(vector):
                n = 0
                for i in range(NTT):
                    if i >= 2:
                        vector.wait_ge(st[i % 2], (i // 2) * 16)
                    for s in range(4):
                        for c in range(KC):
                            vector.wait_ge(tr, n + 1)
                            vector.tensor_copy(
                                xfo[i % 2][:, c, s * 128:(s + 1) * 128],
                                ps[n % 4][:]).then_inc(ev, 1)
                            n += 1
        nc.clear_and_free_semaphores(slist)

def emit_p1(p):
    nc = p.nc
    with ExitStack() as pc:
        sb = lambda nm, sh, d=F32: pc.enter_context(nc.sbuf_tensor(nm, sh, d))
        wq_hi = sb("wq_hi", [128, KC, DM], BF16); wq_lo = sb("wq_lo", [128, KC, DM], BF16)
        wk_hi = sb("wk_hi", [128, KC, DM], BF16); wk_lo = sb("wk_lo", [128, KC, DM], BF16)
        wv_hi = sb("wv_hi", [128, KC, DM], BF16); wv_lo = sb("wv_lo", [128, KC, DM], BF16)
        wg1_hi = sb("wg1_hi", [128, KC, 16], BF16); wg1_lo = sb("wg1_lo", [128, KC, 16], BF16)
        wg2_sb = sb("wg2_sb", [16, DM])
        wtmp = [sb(f"p1_wtmp{i}", [128, DM]) for i in range(2)]
        xf_t = [sb(f"p1_xf{i}", [128, KC, TT]) for i in range(2)]
        xh = [sb(f"p1_xh{i}", [128, KC, TT], BF16) for i in range(2)]
        xl = [sb(f"p1_xl{i}", [128, KC, TT], BF16) for i in range(2)]
        qst = [sb(f"p1_qst{i}", [128, TT]) for i in range(2)]
        kst = [sb(f"p1_kst{i}", [128, TT]) for i in range(2)]
        vst = [sb(f"p1_vst{i}", [128, TT]) for i in range(2)]
        ast = [sb(f"p1_ast{i}", [128, TT]) for i in range(2)]
        g1b = sb("p1_g1b", [16, TT]); sg1 = sb("p1_sg1", [16, TT])
        g1s = [sb(f"p1_g1s{i}", [16, TT]) for i in range(2)]
        asig = sb("p1_asig", [128, TT])
        psq = pc.enter_context(nc.psum_tensor("p1_psq", [128, TT], F32))
        psk = pc.enter_context(nc.psum_tensor("p1_psk", [128, TT], F32))
        psv = pc.enter_context(nc.psum_tensor("p1_psv", [128, TT], F32))
        psg1 = pc.enter_context(nc.psum_tensor("p1_psg1", [16, TT], F32))
        psg2 = pc.enter_context(nc.psum_tensor("p1_psg2", [128, TT], F32))
        slist = []
        def sem(nm):
            h = nc.alloc_semaphore(name=nm)
            slist.append(h)
            return h
        wld = [sem(f"p1_wld{i}") for i in range(2)]
        wsp = sem("p1_wsp")
        xld = [sem(f"p1_xld{i}") for i in range(2)]
        xsp = sem("p1_xsp")
        qmm = sem("p1_qmm"); kmm = sem("p1_kmm"); vmm = sem("p1_vmm")
        g1mm = sem("p1_g1mm"); g2mm = sem("p1_g2mm")
        qev = sem("p1_qev"); kev = sem("p1_kev"); vev = sem("p1_vev")
        aev = sem("p1_aev")
        sc1 = sem("p1_sc1")
        g1sv = sem("p1_g1sv")
        scs = sem("p1_scs")
        qsd = [sem(f"p1_qsd{i}") for i in range(2)]
        ksd = [sem(f"p1_ksd{i}") for i in range(2)]
        vsd = [sem(f"p1_vsd{i}") for i in range(2)]
        asd = [sem(f"p1_asd{i}") for i in range(2)]

        NW = 3 * KC + 1
        with nc.Block() as blk:
            @blk.sync
            def _(sync):
                n = 0
                for (wsrc, _, _) in ((p.Wq, wq_hi, wq_lo), (p.Wk, wk_hi, wk_lo),
                                     (p.Wv, wv_hi, wv_lo)):
                    for kc in range(KC):
                        if n >= 2:
                            sync.wait_ge(wsp, n - 1)
                        sync.dma_start(out=wtmp[n % 2][:],
                                       in_=wsrc[kc * 128:(kc + 1) * 128, :]
                                       ).then_inc(wld[n % 2], 16)
                        n += 1
                if n >= 2:
                    sync.wait_ge(wsp, n - 1)
                with nc.allow_non_contiguous_dma(reason="tiny Wg1 one-time"):
                    sync.dma_start(out=wtmp[n % 2][:, 0:16 * KC].rearrange(
                                       "q (c g) -> q c g", g=16),
                                   in_=p.Wg1.rearrange("(c q) g -> q c g", q=128)
                                   ).then_inc(wld[n % 2], 16)
                n += 1
                sync.wait_ge(wsp, n - 1)
                sync.dma_start(out=wg2_sb[:], in_=p.Wg2).then_inc(wld[n % 2], 16)
                n += 1
                nq = nk = nv = na = 0
                def stores_for(j):
                    nonlocal nq, nk, nv, na
                    for mc in range(KC):
                        sync.wait_ge(qev, nq + 1)
                        sync.dma_start(out=p.qf[mc, :, j * TT:(j + 1) * TT],
                                       in_=qst[nq % 2][:]).then_inc(qsd[nq % 2], 16)
                        nq += 1
                        sync.wait_ge(kev, nk + 1)
                        sync.dma_start(out=p.kf[mc, :, j * TT:(j + 1) * TT],
                                       in_=kst[nk % 2][:]).then_inc(ksd[nk % 2], 16)
                        nk += 1
                        sync.wait_ge(vev, nv + 1)
                        tsub, half = mc % 4, mc // 4
                        r0 = j * TT + tsub * 128
                        sync.dma_start(out=p.vt[r0:r0 + 128,
                                                half * TT:(half + 1) * TT],
                                       in_=vst[nv % 2][:]).then_inc(vsd[nv % 2], 16)
                        nv += 1
                        sync.wait_ge(aev, na + 1)
                        sync.dma_start(out=p.af[mc, :, j * TT:(j + 1) * TT],
                                       in_=ast[na % 2][:]).then_inc(asd[na % 2], 16)
                        na += 1
                for i in range(NTT):
                    if i >= 1:
                        sync.wait_ge(xsp, i)
                    sync.dma_start(
                        out=xf_t[i % 2][:],
                        in_=p.xf[:, :, i * TT:(i + 1) * TT].rearrange("c q t -> q c t")
                        ).then_inc(xld[i % 2], 16)
                    if i >= 1:
                        stores_for(i - 1)
                stores_for(NTT - 1)
                for s2 in (qsd, ksd, vsd, asd):
                    sync.wait_ge(s2[0], NTT * KC // 2 * 16)
                    sync.wait_ge(s2[1], NTT * KC // 2 * 16)

            @blk.vector
            def _(vector):
                n = 0
                for (_, whi, wlo) in ((p.Wq, wq_hi, wq_lo), (p.Wk, wk_hi, wk_lo),
                                      (p.Wv, wv_hi, wv_lo)):
                    for kc in range(KC):
                        vector.wait_ge(wld[n % 2], (n // 2 + 1) * 16)
                        vector.tensor_copy(whi[:, kc, :], wtmp[n % 2][:])
                        vector.drain()
                        vector.tensor_sub(wlo[:, kc, :], wtmp[n % 2][:], whi[:, kc, :])
                        vector.drain()
                        vector.nop().then_inc(wsp, 1)
                        n += 1
                vector.wait_ge(wld[n % 2], (n // 2 + 1) * 16)
                wg1view = wtmp[n % 2][:, 0:16 * KC].rearrange("q (c g) -> q c g", g=16)
                vector.tensor_copy(wg1_hi[:], wg1view)
                vector.drain()
                vector.tensor_sub(wg1_lo[:], wg1view, wg1_hi[:])
                vector.drain()
                vector.nop().then_inc(wsp, 1)
                n += 1
                nq = nk = nv = na = 0
                for i in range(NTT):
                    vector.wait_ge(xld[i % 2], (i // 2 + 1) * 16)
                    if i >= 2:
                        vector.wait_ge(g2mm, (i - 1) * KC)
                    vector.tensor_copy(xh[i % 2][:], xf_t[i % 2][:])
                    vector.drain()
                    vector.tensor_sub(xl[i % 2][:], xf_t[i % 2][:], xh[i % 2][:])
                    vector.drain()
                    vector.nop().then_inc(xsp, 1)
                    vector.wait_ge(sc1, 2 * (i + 1))
                    vector.tensor_mul(g1s[i % 2][:], g1b[:], sg1[:])
                    vector.drain()
                    vector.nop().then_inc(g1sv, 1)
                    for mc in range(KC):
                        vector.wait_ge(qmm, nq + 1)
                        if nq >= 2:
                            vector.wait_ge(qsd[nq % 2], (nq // 2) * 16)
                        vector.tensor_scalar_add(qst[nq % 2][:], psq[:],
                                                 p.bq_sb[:, mc:mc + 1])
                        vector.drain()
                        vector.nop().then_inc(qev, 1)
                        nq += 1
                        vector.wait_ge(kmm, nk + 1)
                        if nk >= 2:
                            vector.wait_ge(ksd[nk % 2], (nk // 2) * 16)
                        vector.tensor_scalar_add(kst[nk % 2][:], psk[:],
                                                 p.bk_sb[:, mc:mc + 1])
                        vector.drain()
                        vector.nop().then_inc(kev, 1)
                        nk += 1
                        vector.wait_ge(vmm, nv + 1)
                        if nv >= 2:
                            vector.wait_ge(vsd[nv % 2], (nv // 2) * 16)
                        vector.tensor_copy(vst[nv % 2][:], psv[:])
                        vector.drain()
                        vector.nop().then_inc(vev, 1)
                        nv += 1
                        vector.wait_ge(scs, na + 1)
                        if na >= 2:
                            vector.wait_ge(asd[na % 2], (na // 2) * 16)
                        vector.tensor_scalar(ast[na % 2][:], asig[:], 0.9, 0.1,
                                             op0=OP.mult, op1=OP.add)
                        vector.drain()
                        vector.nop().then_inc(aev, 1)
                        na += 1

            @blk.scalar
            def _(scalar):
                for i in range(NTT):
                    scalar.wait_ge(g1mm, i + 1)
                    if i >= 1:
                        scalar.wait_ge(g1sv, i)
                    scalar.activation(g1b[:], psg1[:], AF.Identity,
                                      bias=p.bg1_sb[:, 0:1], scale=1.0).then_inc(sc1, 1)
                    scalar.drain()
                    scalar.activation(sg1[:], psg1[:], AF.Sigmoid,
                                      bias=p.bg1_sb[:, 0:1], scale=1.0).then_inc(sc1, 1)
                    scalar.drain()
                    for mc in range(KC):
                        nag = i * KC + mc
                        scalar.wait_ge(g2mm, nag + 1)
                        if nag >= 1:
                            scalar.wait_ge(aev, nag)
                        scalar.activation(asig[:], psg2[:], AF.Sigmoid,
                                          bias=p.bg2_sb[:, mc:mc + 1], scale=1.0).then_inc(scs, 1)
                        scalar.drain()

            @blk.tensor
            def _(tensor):
                tensor.wait_ge(wsp, NW)
                tensor.wait_ge(wld[1], 13 * 16)
                nq = nk = nv = na = 0
                for i in range(NTT):
                    bi = i % 2
                    tensor.wait_ge(xsp, i + 1)
                    if i >= 1:
                        tensor.wait_ge(sc1, 2 * i)
                    for n3, (xs, wk_) in enumerate(
                            ((xh, 0), (xl, 0), (xh, 1))):
                        w = (wg1_hi, wg1_lo)[wk_]
                        for kc in range(KC):
                            ins = tensor.matmul(psg1[:], w[:, kc, :],
                                                xs[bi][:, kc, :],
                                                start=(n3 == 0 and kc == 0),
                                                stop=(n3 == 2 and kc == KC - 1))
                    ins.then_inc(g1mm, 1)
                    for mc in range(KC):
                        tensor.wait_ge(qev, nq)
                        k3 = 0
                        for (xs, wpc) in ((xh, wq_hi), (xl, wq_hi), (xh, wq_lo)):
                            for kc in range(KC):
                                ins = tensor.matmul(
                                    psq[:], wpc[:, kc, mc * 128:(mc + 1) * 128],
                                    xs[bi][:, kc, :],
                                    start=(k3 == 0), stop=(k3 == 3 * KC - 1))
                                k3 += 1
                        ins.then_inc(qmm, 1)
                        nq += 1
                        tensor.wait_ge(kev, nk)
                        k3 = 0
                        for (xs, wpc) in ((xh, wk_hi), (xl, wk_hi), (xh, wk_lo)):
                            for kc in range(KC):
                                ins = tensor.matmul(
                                    psk[:], wpc[:, kc, mc * 128:(mc + 1) * 128],
                                    xs[bi][:, kc, :],
                                    start=(k3 == 0), stop=(k3 == 3 * KC - 1))
                                k3 += 1
                        ins.then_inc(kmm, 1)
                        nk += 1
                        tensor.wait_ge(vev, nv)
                        tsub, half = mc % 4, mc // 4
                        tensor.matmul(psv[:], p.ones_sb[:],
                                      p.bvrow[:, half * TT:(half + 1) * TT],
                                      start=True, stop=False)
                        k3 = 0
                        for (xs, wpc) in ((xh, wv_hi), (xl, wv_hi), (xh, wv_lo)):
                            for kc in range(KC):
                                ins = tensor.matmul(
                                    psv[:], xs[bi][:, kc, tsub * 128:(tsub + 1) * 128],
                                    wpc[:, kc, half * TT:(half + 1) * TT],
                                    start=False, stop=(k3 == 3 * KC - 1))
                                k3 += 1
                        ins.then_inc(vmm, 1)
                        nv += 1
                        tensor.wait_ge(g1sv, i + 1)
                        tensor.wait_ge(scs, na)
                        ins = tensor.matmul(psg2[:], wg2_sb[:, mc * 128:(mc + 1) * 128],
                                            g1s[bi][:], start=True, stop=True)
                        ins.then_inc(g2mm, 1)
                        na += 1
        nc.clear_and_free_semaphores(slist)

def emit_p2(p):
    nc = p.nc
    NST = 8
    NG = NTT * NST
    if P2_STEPS is not None:
        NG = P2_STEPS
    NTRS = 24
    with ExitStack() as pc:
        sb = lambda nm, sh, d=F32: pc.enter_context(nc.sbuf_tensor(nm, sh, d))
        qt = [sb(f"p2_qt{i}", [128, KC, TT]) for i in range(2)]
        kt = [sb(f"p2_kt{i}", [128, KC, TT]) for i in range(2)]
        at0 = sb("p2_at", [128, KC, TT])
        at = [at0, at0]
        xft = sb("p2_xft", [128, KC, TT])
        xr = sb("p2_xr", [128, KC, TT])
        x1t = sb("p2_x1t", [128, KC, TT])
        sq = x1t
        vch = [sb(f"p2_vch{i}", [128, DM]) for i in range(2)]
        vs1 = [sb(f"p2_vs1{i}", [32, DM]) for i in range(2)]
        S_all = sb("p2_S", [128, KC, 64])
        LY = sb("p2_LY", [128, KC, 2, 33])
        rc = sb("p2_rc", [128, KC, 2, 32])
        Qt_ = sb("p2_Qt", [128, KC, 2, 32])
        QtS = sb("p2_QtS", [128, KC, 2, 32])
        KtBD = sb("p2_KtBD", [128, KC, 128])
        K2BD = sb("p2_K2BD", [128, KC, 128])
        ToffSB = sb("p2_ToffSB", [128, KC, 32])
        K2_ = sb("p2_K2", [128, KC, 2, 32])
        ct01 = sb("p2_ct01", [128, KC])
        scb = sb("p2_scb", [128, KC, 64])
        mx = sb("p2_mx", [128, KC])
        dsb = sb("p2_dsb", [128, KC, 64])
        psb = sb("p2_psb", [128, KC, 64])
        sm = sb("p2_sm", [128, KC])
        rs = sb("p2_rs", [128, KC])
        pn = sb("p2_pn", [128, KC, 64])
        pT = sb("p2_pT", [64, KC, 128])
        K2T = sb("p2_K2T", [32, KC, 2, 128])
        TtSB = sb("p2_TtSB", [128, KC, 2, 32])
        att = sb("p2_att", [128, KC, 64])
        m_sb = pn[:].rearrange("q c e -> q (c e)")
        msq = dsb[:].rearrange("q c e -> q (c e)")
        v_sb = scb[:].rearrange("q c e -> q (c e)")
        sd_sb = att[:].rearrange("q c e -> q (c e)")
        rstd = sd_sb
        B = [pc.enter_context(nc.psum_tensor(f"p2_B{i}", [128, TT], F32))
             for i in range(8)]
        B0v = B[0][:].rearrange("q (c e) -> q c e", e=64)
        Ttv = B[0][:].rearrange("q (c s2 t) -> q c s2 t", s2=2, t=32)
        B1v = B[1][:].rearrange("q (c e) -> q c e", e=64)
        B2v = B[2][:].rearrange("q (c e) -> q c e", e=64)
        B3v = B[3][:].rearrange("q (c e) -> q c e", e=64)
        Dv = B[4][:].rearrange("q (c e) -> q c e", e=64)
        B5tv = B[5][:, 0:KC * 32].rearrange("q (c t) -> q c t", t=32)
        TR = [B[6], B[7]]

        slist = []
        def sem(nm):
            h = nc.alloc_semaphore(name=nm)
            slist.append(h)
            return h
        qld = [sem(f"p2_qld{i}") for i in range(2)]
        xfl = sem("p2_xfl")
        vld = [sem(f"p2_vld{i}") for i in range(2)]
        x1sd = [sem(f"p2_x1sd{i}") for i in range(2)]
        V1 = sem("p2_V1"); V2 = sem("p2_V2"); V2b = sem("p2_V2b")
        V3 = sem("p2_V3"); V4 = sem("p2_V4"); V5 = sem("p2_V5")
        V6 = sem("p2_V6"); V7 = sem("p2_V7")
        S1e = sem("p2_S1e")
        T1 = sem("p2_T1"); T3 = sem("p2_T3"); T4 = sem("p2_T4"); T5 = sem("p2_T5")
        ATr = sem("p2_ATr")
        trsem = sem("p2_trsem"); trev = sem("p2_trev")
        LSq = sem("p2_LSq"); LT1 = sem("p2_LT1"); LT2 = sem("p2_LT2")
        LNx = sem("p2_LNx"); LNe = sem("p2_LNe"); LNv = sem("p2_LNv")
        LNs = sem("p2_LNs")

        def v6req(G):
            return G + 1 + (1 if G >= NG // 2 else 0)

        with nc.Block() as blk:
            @blk.sync
            def _(sync):
                for G in range(NG):
                    g, j = G // NST, G % NST
                    if j == 0:
                        if g >= 2:
                            sync.wait_ge(T3, (g - 1) * NST)
                            sync.wait_ge(V1, (g - 1) * NST)
                        base = g * TT
                        for (dst, src) in ((qt, p.qf), (kt, p.kf)):
                            sync.dma_start(
                                out=dst[g % 2][:],
                                in_=src[:, :, base:base + TT].rearrange(
                                    "c q t -> q c t")).then_inc(qld[g % 2], 16)
                        if g >= 1:
                            sync.wait_ge(V1, g * NST)
                        sync.dma_start(
                            out=at[g % 2][:],
                            in_=p.af[:, :, base:base + TT].rearrange(
                                "c q t -> q c t")).then_inc(qld[g % 2], 16)
                        sync.wait_ge(V7, g * NST)
                        sync.dma_start(
                            out=xft[:],
                            in_=p.xf[:, :, base:base + TT].rearrange("c q t -> q c t")
                            ).then_inc(xfl, 16)
                    if G >= 2:
                        sync.wait_ge(T4, G - 1)
                    r0 = g * TT + j * 64
                    sync.dma_start(out=vch[G % 2][0:64, :],
                                   in_=p.vt[r0:r0 + 64, :]).then_inc(vld[G % 2], 16)
                    sync.dma_start(out=vch[G % 2][64:128, :],
                                   in_=p.vt[r0:r0 + 64, :]).then_inc(vld[G % 2], 16)
                    sync.dma_start(out=vs1[G % 2][0:32, :],
                                   in_=p.vt[r0 + 32:r0 + 64, :]).then_inc(vld[G % 2], 16)
                    if j == NST - 1 and g >= 1:
                        gs = g - 1
                        sync.wait_ge(LNx, gs + 1)
                        sync.dma_start(
                            out=p.x1[:, :, gs * TT:(gs + 1) * TT].rearrange(
                                "c q t -> q c t"),
                            in_=x1t[:]).then_inc(x1sd[gs % 2], 16)
                NT2 = NG // NST
                gs = NT2 - 1
                sync.wait_ge(LNx, NT2)
                sync.dma_start(
                    out=p.x1[:, :, gs * TT:(gs + 1) * TT].rearrange("c q t -> q c t"),
                    in_=x1t[:]).then_inc(x1sd[gs % 2], 16)
                sync.wait_ge(x1sd[0], (NT2 + 1) // 2 * 16)
                if NT2 >= 2:
                    sync.wait_ge(x1sd[1], NT2 // 2 * 16)

            @blk.tensor
            def _(tensor):
                ntr = 0

                def rot_transpose(in_ap):
                    nonlocal ntr
                    if ntr >= 2:
                        tensor.wait_ge(trev, ntr - 1)
                    nrows = in_ap.shape[-1]
                    tensor.transpose(TR[ntr % 2][0:nrows, 0:128], in_ap,
                                     p.ident_sb[:]).then_inc(trsem, 1)
                    ntr += 1

                for G in range(NG):
                    g, j = G // NST, G % NST
                    jo = j * 64
                    tensor.wait_ge(qld[g % 2], (g // 2 + 1) * 48)
                    tensor.wait_ge(V4, G)
                    tensor.wait_ge(LNe, g)
                    for c in range(KC):
                        for u in range(2):
                            ins = tensor.matmul(
                                B0v[64 * u:64 * u + 64, c, :],
                                qt[g % 2][64 * u:64 * u + 64, c, jo:jo + 64],
                                kt[g % 2][64 * u:64 * u + 64, c, jo:jo + 64],
                                start=True, stop=True)
                    ins.then_inc(T1, 1)
                    tensor.wait_ge(V1, G + 1)
                    for c in range(KC):
                        for sub in range(2):
                            rot_transpose(K2_[:, c, sub, :])
                    tensor.wait_ge(V2, G + 1)
                    tensor.wait_ge(V4, G)
                    for c in range(KC):
                        tensor.matmul(
                            Ttv[:, c, :, :], KtBD[:, c, :],
                            Qt_[:, c, :, :], start=True, stop=True)
                    for c in range(KC):
                        ins = tensor.matmul(
                            B5tv[:, c, :], K2BD[:, c, :],
                            Qt_[:, c, 1, :], start=True, stop=True)
                    ins.then_inc(T3, 1)
                    tensor.wait_ge(V6, v6req(G))
                    tensor.wait_ge(V7, G)
                    tensor.wait_ge(V4, G + 1)
                    tensor.wait_ge(vld[G % 2], (G // 2 + 1) * 48)
                    for c in range(KC):
                        for u in range(2):
                            tensor.matmul(
                                B3v[64 * u:64 * u + 64, c, :],
                                S_all[64 * u:64 * u + 64, c, :],
                                QtS[64 * u:64 * u + 64, c, :, :],
                                start=True, stop=False)
                            for st2 in range(2):
                                tensor.matmul(
                                    B3v[64 * u:64 * u + 64, c,
                                        32 * st2:32 * st2 + 32],
                                    vch[G % 2][64 * u:64 * u + 64,
                                               c * 128 + u * 64:c * 128 + u * 64 + 64],
                                    TtSB[64 * u:64 * u + 64, c, st2, :],
                                    start=False, stop=False)
                            tensor.matmul(
                                B3v[64 * u:64 * u + 64, c, 32:64],
                                vch[G % 2][64 * u:64 * u + 32,
                                           c * 128 + u * 64:c * 128 + u * 64 + 64],
                                ToffSB[64 * u:64 * u + 32, c, :],
                                start=False, stop=True)
                    tensor.wait_ge(trev, NTRS * G + 16)
                    tensor.wait_ge(V6, v6req(G))
                    for c in range(KC):
                        for u in range(2):
                            for sub in range(2):
                                vsrc = vch[G % 2] if sub == 0 else vs1[G % 2]
                                ins = tensor.matmul(
                                    Dv[64 * u:64 * u + 64, c, :],
                                    K2T[0:32, c, sub, u * 64:u * 64 + 64],
                                    vsrc[0:32,
                                         c * 128 + u * 64:c * 128 + u * 64 + 64],
                                    start=(sub == 0), stop=(sub == 1))
                    ins.then_inc(T5, 1)
                    tensor.wait_ge(V3, G + 1)
                    for c in range(KC):
                        rot_transpose(pn[:, c, :])
                    tensor.wait_ge(trev, NTRS * G + NTRS)
                    tensor.wait_ge(V5, G)
                    tensor.wait_ge(LNe, g)
                    for c in range(KC):
                        for u in range(2):
                            ins = tensor.matmul(
                                B1v[64 * u:64 * u + 64, c, :],
                                pT[0:64, c, u * 64:u * 64 + 64],
                                vch[G % 2][0:64,
                                           c * 128 + u * 64:c * 128 + u * 64 + 64],
                                start=True, stop=True)
                    ins.then_inc(T4, 1)
                    tensor.wait_ge(V5, G + 1)
                    tensor.wait_ge(V7, G)
                    for c in range(KC):
                        for u in range(2):
                            idq = p.ident_sb[64 * u:64 * u + 64, 64 * u:64 * u + 64]
                            ins = tensor.matmul(
                                B2v[64 * u:64 * u + 64, c, :],
                                att[64 * u:64 * u + 64, c, :], idq,
                                start=True, stop=True)
                    ins.then_inc(ATr, 1)
                    if j == NST - 1:
                        tensor.wait_ge(V7, (g + 1) * NST)
                        tensor.wait_ge(V4, (g + 1) * NST)
                        for kc in range(KC):
                            ins = tensor.matmul(B[0][:], p.ones_sb[:],
                                                xr[:, kc, :],
                                                start=(kc == 0), stop=(kc == KC - 1))
                        ins.then_inc(LT1, 1)
                        tensor.wait_ge(LSq, g + 1)
                        tensor.wait_ge(V5, (g + 1) * NST)
                        for kc in range(KC):
                            ins = tensor.matmul(B[1][:], p.ones_sb[:],
                                                sq[:, kc, :],
                                                start=(kc == 0), stop=(kc == KC - 1))
                        ins.then_inc(LT2, 1)

            @blk.vector
            def _(vector):
                vector.memset(LY[:, :, :, 0:1], 1.0)
                vector.memset(KtBD[:], 0.0)
                vector.memset(K2BD[:], 0.0)
                vector.drain()
                for G in range(NG):
                    g, j = G // NST, G % NST
                    jo = j * 64
                    b = G // (NG // 2)
                    if G % (NG // 2) == 0 and NG > NST // 2:
                        if b == 1:
                            vector.wait_ge(T4, NG // 2)
                        vector.memset(S_all[:], 0.0)
                        vector.drain()
                        vector.nop().then_inc(V6, 1)
                    vector.wait_ge(qld[g % 2], (g // 2 + 1) * 48)
                    vector.wait_ge(T4, G)
                    atj = at[g % 2][:, :, jo:jo + 64].rearrange(
                        "q c (s2 t) -> q c s2 t", t=32)
                    qtj = qt[g % 2][:, :, jo:jo + 64].rearrange(
                        "q c (s2 t) -> q c s2 t", t=32)
                    ktj = kt[g % 2][:, :, jo:jo + 64].rearrange(
                        "q c (s2 t) -> q c s2 t", t=32)
                    for c in range(KC):
                        for sub in range(2):
                            vector.tensor_tensor_scan(
                                LY[:, c, sub, 1:33], atj[:, c, sub, :],
                                atj[:, c, sub, :], 1.0, op0=OP.mult, op1=OP.bypass)
                    vector.drain()
                    vector.reciprocal(rc[:], LY[:, :, :, 1:33])
                    vector.drain()
                    vector.tensor_mul(Qt_[:], qtj, LY[:, :, :, 0:32])
                    vector.tensor_mul(
                        KtBD[0:64, :, 0:64].rearrange("q c (s2 t) -> q c s2 t", t=32),
                        ktj[0:64, :, :, :], rc[0:64, :, :, :])
                    vector.tensor_mul(
                        KtBD[64:128, :, 64:128].rearrange("q c (s2 t) -> q c s2 t", t=32),
                        ktj[64:128, :, :, :], rc[64:128, :, :, :])
                    vector.drain()
                    vector.tensor_tensor(
                        rc[:], rc[:],
                        LY[:, :, :, 32:33].broadcast_to([128, KC, 2, 32]), op=OP.mult)
                    vector.drain()
                    vector.tensor_mul(
                        K2BD[0:64, :, 0:32], ktj[0:64, :, 0, :], rc[0:64, :, 0, :])
                    vector.tensor_mul(
                        K2BD[64:128, :, 64:96], ktj[64:128, :, 0, :],
                        rc[64:128, :, 0, :])
                    vector.drain()
                    vector.tensor_tensor(
                        rc[:, :, 0, :], rc[:, :, 0, :],
                        LY[:, :, 1, 32:33].broadcast_to([128, KC, 32]), op=OP.mult)
                    vector.drain()
                    vector.tensor_mul(K2_[:], ktj, rc[:])
                    vector.tensor_mul(ct01[:], LY[:, :, 0, 32], LY[:, :, 1, 32])
                    vector.tensor_copy(QtS[:, :, 0, :], Qt_[:, :, 0, :])
                    vector.drain()
                    vector.tensor_tensor(
                        QtS[:, :, 1, :], Qt_[:, :, 1, :],
                        LY[:, :, 0, 32:33].broadcast_to([128, KC, 32]), op=OP.mult)
                    vector.drain()
                    vector.nop().then_inc(V1, 1)
                    vector.wait_ge(T1, G + 1)
                    vector.tensor_tensor(
                        scb[:], B0v[:],
                        p.cmask_sb[:].rearrange("q s -> q () s"
                                                ).broadcast_to([128, KC, 64]),
                        op=OP.add)
                    vector.drain()
                    vector.nop().then_inc(V2, 1)
                    vector.reduce_max(
                        mx[:].rearrange("q c -> q c ()"), scb[:], axis=AX)
                    vector.drain()
                    vector.tensor_tensor(
                        dsb[:], scb[:],
                        mx[:].rearrange("q c -> q c ()").broadcast_to([128, KC, 64]),
                        op=OP.subtract)
                    vector.drain()
                    vector.nop().then_inc(V2b, 1)
                    vector.wait_ge(S1e, G + 1)
                    vector.reduce_sum(
                        sm[:].rearrange("q c -> q c ()"), psb[:], axis=AX)
                    vector.drain()
                    vector.reciprocal(rs[:], sm[:])
                    vector.drain()
                    if G >= 1:
                        vector.wait_ge(trsem, NTRS * (G - 1) + NTRS)
                    vector.tensor_tensor(
                        pn[:], psb[:],
                        rs[:].rearrange("q c -> q c ()").broadcast_to([128, KC, 64]),
                        op=OP.mult)
                    vector.drain()
                    vector.nop().then_inc(V3, 1)
                    base_tr = NTRS * G
                    if G >= 1:
                        vector.wait_ge(T5, G)
                    nn = 0
                    for c in range(KC):
                        for sub in range(2):
                            vector.wait_ge(trsem, base_tr + nn + 1)
                            vector.tensor_copy(
                                K2T[0:32, c, sub, :],
                                TR[(base_tr + nn) % 2][0:32, 0:128])
                            vector.drain()
                            vector.nop().then_inc(trev, 1)
                            nn += 1
                    vector.wait_ge(T3, G + 1)
                    if G >= 1:
                        vector.wait_ge(T4, G)
                    vector.tensor_tensor(
                        TtSB[:], Ttv[:, :, :, :],
                        p.tmask_sb[:].rearrange("q (s2 t) -> q () s2 t", t=32
                                                ).broadcast_to([128, KC, 2, 32]),
                        op=OP.mult)
                    vector.tensor_copy(ToffSB[:], B5tv[:])
                    vector.drain()
                    vector.nop().then_inc(V4, 1)
                    for c in range(KC):
                        vector.wait_ge(trsem, base_tr + nn + 1)
                        vector.tensor_copy(
                            pT[0:64, c, :], TR[(base_tr + nn) % 2][0:64, 0:128])
                        vector.drain()
                        vector.nop().then_inc(trev, 1)
                        nn += 1
                    vector.wait_ge(T4, G + 1)
                    vector.tensor_copy(att[:], B1v[:])
                    vector.drain()
                    vector.nop().then_inc(V5, 1)
                    vector.wait_ge(T5, G + 1)
                    vector.tensor_tensor(
                        S_all[:], S_all[:],
                        ct01[:].rearrange("q c -> q c ()").broadcast_to([128, KC, 64]),
                        op=OP.mult)
                    vector.drain()
                    vector.tensor_add(S_all[:], S_all[:], Dv[:])
                    vector.drain()
                    vector.nop().then_inc(V6, 1)
                    vector.wait_ge(ATr, G + 1)
                    if j == 0:
                        vector.wait_ge(xfl, (g + 1) * 16)
                    vector.tensor_add(xr[:, :, jo:jo + 64], B2v[:],
                                      xft[:, :, jo:jo + 64])
                    vector.drain()
                    vector.tensor_add(xr[:, :, jo:jo + 64],
                                      xr[:, :, jo:jo + 64], B3v[:])
                    vector.drain()
                    vector.nop().then_inc(V7, 1)
                    if j == NST - 1:
                        vector.wait_ge(LT1, g + 1)
                        vector.tensor_scalar_mul(m_sb, B[0][:], 1.0 / DM)
                        vector.drain()
                        vector.tensor_mul(msq, m_sb, m_sb)
                        vector.drain()
                        vector.wait_ge(LT2, g + 1)
                        vector.scalar_tensor_tensor(
                            v_sb, B[1][:], 1.0 / DM, msq,
                            op0=OP.mult, op1=OP.subtract)
                        vector.drain()
                        vector.nop().then_inc(LNe, 1)
                        vector.nop().then_inc(LNv, 1)
                        vector.wait_ge(LNs, g + 1)
                        vector.reciprocal(rstd, sd_sb)
                        vector.drain()
                        if g >= 1:
                            vector.wait_ge(x1sd[(g - 1) % 2],
                                           ((g - 1) // 2 + 1) * 16)
                        vector.tensor_tensor(
                            x1t[:], xr[:],
                            m_sb.rearrange("q t -> q () t"
                                           ).broadcast_to([128, KC, TT]),
                            op=OP.subtract)
                        vector.drain()
                        vector.tensor_tensor(
                            x1t[:], x1t[:],
                            rstd.rearrange("q t -> q () t"
                                           ).broadcast_to([128, KC, TT]),
                            op=OP.mult)
                        vector.drain()
                        vector.reduce_max(
                            p.maxacc[:, g:g + 1],
                            x1t[:].rearrange("q c t -> q (c t)"),
                            axis=AX, apply_absolute_value=True)
                        vector.drain()
                        vector.nop().then_inc(LNx, 1)
                vector.reduce_max(p.sig_sb[:, 0:1], p.maxacc[:, 0:NTT], axis=AX)
                vector.drain()

            @blk.scalar
            def _(scalar):
                for G in range(NG):
                    g, j = G // NST, G % NST
                    scalar.wait_ge(V2b, G + 1)
                    if G >= 1:
                        scalar.wait_ge(V3, G)
                    scalar.activation(psb[:], dsb[:], AF.Exp, scale=0.125).then_inc(S1e, 1)
                    scalar.drain()
                    if j == NST - 1:
                        scalar.wait_ge(V7, (g + 1) * NST)
                        if g >= 1:
                            scalar.wait_ge(LT2, g)
                            scalar.wait_ge(x1sd[(g - 1) % 2],
                                           ((g - 1) // 2 + 1) * 16)
                        scalar.activation(sq[:], xr[:], AF.Square).then_inc(LSq, 1)
                        scalar.drain()
                        scalar.wait_ge(LNv, g + 1)
                        scalar.activation(sd_sb, v_sb, AF.Sqrt, bias=p.eps_sb[:]).then_inc(LNs, 1)
                        scalar.drain()
        nc.clear_and_free_semaphores(slist)

def emit_p3(p):
    nc = p.nc
    jobs = [(p.wup, p.wupT_hi, p.wupT_lo, MC3, DM),
            (p.wgate, p.wgT_hi, p.wgT_lo, MC3, DM),
            (p.wdown, p.wdT_hi, p.wdT_lo, KC, I3)]
    chunks = []
    for ji, (w, whi, wlo, nmc, kk) in enumerate(jobs):
        for mc in range(nmc):
            chunks.append((ji, mc, kk))
    cumkc = [0]
    for (ji, mc, kk) in chunks:
        cumkc.append(cumkc[-1] + kk // 128)
    with ExitStack() as pc:
        sb = lambda nm, sh, d=F32: pc.enter_context(nc.sbuf_tensor(nm, sh, d))
        wt = [sb(f"p3_wt{i}", [128, I3]) for i in range(2)]
        z = sb("p3_z", [128, I3]); zi = sb("p3_zi", [128, I3], I32)
        zc = sb("p3_zc", [128, I3]); A = sb("p3_A", [128, I3])
        Bb = sb("p3_B", [128, I3]); wq = sb("p3_wq", [128, I3])
        am = sb("p3_am", [128, I3 // 32]); sc_ = sb("p3_s", [128, I3 // 32])
        hi = [sb(f"p3_hi{i}", [128, 24, 128], BF16) for i in range(2)]
        lo = [sb(f"p3_lo{i}", [128, 24, 128], BF16) for i in range(2)]
        ps = [pc.enter_context(nc.psum_tensor(f"p3_ps{i}", [128, 128], F32))
              for i in range(2)]
        slist = []
        def sem(nm):
            h = nc.alloc_semaphore(name=nm)
            slist.append(h)
            return h
        ld = [sem(f"p3_ld{i}") for i in range(2)]
        vq = sem("p3_vq")
        tr = sem("p3_tr")
        pe = sem("p3_pe")
        sd = [sem(f"p3_sd{i}") for i in range(2)]
        with nc.Block() as blk:
            @blk.sync
            def _(sync):
                for n, (ji, mc, kk) in enumerate(chunks):
                    w = jobs[ji][0]
                    if n >= 2:
                        sync.wait_ge(vq, n - 1)
                    sync.dma_start(out=wt[n % 2][:, 0:kk],
                                   in_=w[mc * 128:(mc + 1) * 128, :]
                                   ).then_inc(ld[n % 2], 16)
                    if n >= 1:
                        jj, mm, kk2 = chunks[n - 1]
                        whi, wlo = jobs[jj][1], jobs[jj][2]
                        nkc = kk2 // 128
                        sync.wait_ge(pe, cumkc[n])
                        for (stg, dst) in ((hi, whi), (lo, wlo)):
                            sync.dma_start(
                                out=dst[:, :, mm * 128:(mm + 1) * 128].rearrange(
                                    "k q m -> q k m"),
                                in_=stg[(n - 1) % 2][:, 0:nkc, :]
                                ).then_inc(sd[(n - 1) % 2], 16)
                n = len(chunks)
                jj, mm, kk2 = chunks[n - 1]
                whi, wlo = jobs[jj][1], jobs[jj][2]
                nkc = kk2 // 128
                sync.wait_ge(pe, cumkc[n])
                for (stg, dst) in ((hi, whi), (lo, wlo)):
                    sync.dma_start(
                        out=dst[:, :, mm * 128:(mm + 1) * 128].rearrange(
                            "k q m -> q k m"),
                        in_=stg[(n - 1) % 2][:, 0:nkc, :]
                        ).then_inc(sd[(n - 1) % 2], 16)
                h0 = sum(32 for i in range(0, len(chunks), 2))
                h1 = sum(32 for i in range(1, len(chunks), 2))
                sync.wait_ge(sd[0], h0)
                sync.wait_ge(sd[1], h1)

            @blk.tensor
            def _(tensor):
                ntr = 0
                for n, (ji, mc, kk) in enumerate(chunks):
                    tensor.wait_ge(vq, n + 1)
                    for kc in range(kk // 128):
                        if ntr >= 2:
                            tensor.wait_ge(pe, ntr - 1)
                        tensor.transpose(ps[ntr % 2][:],
                                         wq[:, kc * 128:(kc + 1) * 128],
                                         p.ident_sb[:]).then_inc(tr, 1)
                        ntr += 1

            @blk.vector
            def _(vector):
                ntr = 0
                for n, (ji, mc, kk) in enumerate(chunks):
                    vector.wait_ge(ld[n % 2], (n // 2 + 1) * 16)
                    nb = kk // 32
                    w3 = wt[n % 2][:, 0:kk].rearrange("q (g c) -> q g c", c=32)
                    vector.reduce_max(
                        am[:, 0:nb].rearrange("q g -> q g ()"), w3, axis=AX,
                        apply_absolute_value=True)
                    vector.drain()
                    vector.tensor_scalar(sc_[:, 0:nb], am[:, 0:nb], 1.0 / 7.0,
                                         1e-12, op0=OP.mult, op1=OP.max)
                    vector.drain()
                    vector.reciprocal(am[:, 0:nb], sc_[:, 0:nb])
                    vector.drain()
                    vector.tensor_tensor(
                        z[:, 0:kk].rearrange("q (g c) -> q g c", c=32), w3,
                        am[:, 0:nb].rearrange("q g -> q g ()"
                                              ).broadcast_to([128, nb, 32]),
                        op=OP.mult)
                    vector.drain()
                    vector.tensor_copy(zi[:, 0:kk], z[:, 0:kk])
                    vector.drain()
                    vector.tensor_copy(zc[:, 0:kk], zi[:, 0:kk])
                    vector.drain()
                    vector.scalar_tensor_tensor(A[:, 0:kk], zc[:, 0:kk], -0.5,
                                                z[:, 0:kk], op0=OP.add,
                                                op1=OP.is_le)
                    vector.scalar_tensor_tensor(Bb[:, 0:kk], zc[:, 0:kk], 0.5,
                                                z[:, 0:kk], op0=OP.add,
                                                op1=OP.is_le)
                    vector.drain()
                    vector.tensor_add(z[:, 0:kk], zc[:, 0:kk], A[:, 0:kk])
                    vector.drain()
                    vector.tensor_add(z[:, 0:kk], z[:, 0:kk], Bb[:, 0:kk])
                    vector.drain()
                    vector.tensor_scalar_add(z[:, 0:kk], z[:, 0:kk], -1.0)
                    vector.drain()
                    if n >= 1:
                        vector.wait_ge(tr, cumkc[n])
                    vector.tensor_tensor(
                        wq[:, 0:kk].rearrange("q (g c) -> q g c", c=32),
                        z[:, 0:kk].rearrange("q (g c) -> q g c", c=32),
                        sc_[:, 0:nb].rearrange("q g -> q g ()"
                                               ).broadcast_to([128, nb, 32]),
                        op=OP.mult)
                    vector.drain()
                    vector.nop().then_inc(vq, 1)
                    if n >= 2:
                        vector.wait_ge(sd[n % 2], (n // 2) * 32)
                    for kc in range(kk // 128):
                        vector.wait_ge(tr, ntr + 1)
                        vector.tensor_copy(hi[n % 2][:, kc, :], ps[ntr % 2][:])
                        vector.drain()
                        vector.scalar_tensor_tensor(
                            lo[n % 2][:, kc, :], ps[ntr % 2][:], 1.0,
                            hi[n % 2][:, kc, :], op0=OP.mult, op1=OP.subtract)
                        vector.drain()
                        vector.nop().then_inc(pe, 1)
                        ntr += 1
        nc.clear_and_free_semaphores(slist)

def _bcast_cols(p, tensor, vector, psum, src_cols, dst_cols, sem_pair):
    n = src_cols.stop - src_cols.start
    tensor.matmul(psum[:, 0:n], p.ones_sb[:], p.row_sb[:, src_cols],
                  start=True, stop=True).then_inc(sem_pair[0], 1)
    vector.wait_ge(sem_pair[0], sem_pair[1])
    vector.tensor_copy(p.sig_sb[:, dst_cols], psum[:, 0:n])
    vector.drain()


def emit_p4(p):
    nc = p.nc
    with ExitStack() as pc:
        sb = lambda nm, sh, d=F32: pc.enter_context(nc.sbuf_tensor(nm, sh, d))
        xqh = sb("p4_xqh", [128, KC, T], BF16)
        xql = sb("p4_xql", [128, KC, T], BF16)
        x1t = [sb(f"p4_x1t{i}", [128, KC, TT // 2]) for i in range(2)]
        z = sb("p4_z", [128, KC * TT // 2]); zi = sb("p4_zi", [128, KC * TT // 2], I32)
        zc = sb("p4_zc", [128, KC * TT // 2]); A = sb("p4_A", [128, KC * TT // 2])
        Bb = sb("p4_B", [128, KC * TT // 2])
        wch = [[sb(f"p4_wc{i}_{wp}", [128, KC, 128], BF16) for wp in range(2)] for i in range(2)]
        stg = [sb(f"p4_stg{i}", [128, TT]) for i in range(2)]
        tmx = sb("p4_tmx", [128, 1])
        ps = [pc.enter_context(nc.psum_tensor(f"p4_ps{i}", [128, TT], F32))
              for i in range(8)]
        slist = []
        def sem(nm):
            h = nc.alloc_semaphore(name=nm)
            slist.append(h)
            return h
        cc = sem("p4_cc"); ccc = sem("p4_ccc"); bc = sem("p4_bc")
        xld = [sem(f"p4_xld{i}") for i in range(2)]
        xq = sem("p4_xq")
        wld = [sem(f"p4_wld{i}") for i in range(2)]
        mmg = sem("p4_mmg")
        evg = sem("p4_evg")
        sdg = [sem(f"p4_sdg{i}") for i in range(2)]
        NHT = 2 * NTT
        HC = KC * TT // 2
        with nc.Block() as blk:
            @blk.gpsimd
            def _(g):
                g.wait_ge(bc, 2)
                g.dma_start(out=p.cc_in.ap(), in_=p.ccrow[:]).then_inc(cc, 16)
                g.wait_ge(cc, 16)
                g.collective_compute(
                    "AllReduce", OP.max,
                    replica_groups=[CC_GROUP or list(range(NCORES))],
                    ins=[p.cc_in.ap()], outs=[p.cc_out.ap()]).then_inc(ccc, 1)
                g.wait_ge(ccc, 1)
                g.dma_start(out=p.ccrow[:], in_=p.cc_out.ap()).then_inc(cc, 16)
                g.wait_ge(cc, 32)
                g.nop().then_inc(ccc, 1)

            @blk.tensor
            def _(tensor):
                tensor.transpose(ps[0][0:1, 0:128], p.sig_sb[:, 0:1],
                                 p.ident_sb[:]).then_inc(bc, 1)
                tensor.wait_ge(bc, 3)
                tensor.matmul(ps[1][:, 0:2], p.ones_sb[:], p.row_sb[:, 0:2],
                              start=True, stop=True).then_inc(bc, 1)
                tensor.wait_ge(xq, NHT)
                ng = 0
                for wsel in range(2):
                    for mc in range(MC3):
                        n = wsel * MC3 + mc
                        tensor.wait_ge(wld[n % 2], (n // 2 + 1) * 32)
                        for tg in range(8):
                            if ng >= 8:
                                tensor.wait_ge(evg, ng - 7)
                            k3 = 0
                            for (xs, wp) in ((xqh, 0), (xql, 0), (xqh, 1)):
                                for kc in range(KC):
                                    ins = tensor.matmul(
                                        ps[ng % 8][:],
                                        wch[n % 2][wp][:, kc, :],
                                        xs[:, kc, tg * TT:(tg + 1) * TT],
                                        start=(k3 == 0), stop=(k3 == 3 * KC - 1))
                                    k3 += 1
                            ins.then_inc(mmg, 1)
                            ng += 1

            @blk.vector
            def _(vector):
                vector.wait_ge(bc, 1)
                vector.tensor_copy(p.ccrow[0:1, 0:128], ps[0][0:1, 0:128])
                vector.drain()
                vector.nop().then_inc(bc, 1)
                vector.wait_ge(ccc, 2)
                vector.reduce_max(p.row_sb[0:1, 0:1], p.ccrow[0:1, 0:128], axis=AX)
                vector.drain()
                vector.tensor_scalar(p.row_sb[0:1, 0:1], p.row_sb[0:1, 0:1],
                                     1.0 / QMAX, 1e-12, op0=OP.mult, op1=OP.max)
                vector.drain()
                vector.reciprocal(p.row_sb[0:1, 1:2], p.row_sb[0:1, 0:1])
                vector.drain()
                vector.nop().then_inc(bc, 1)
                vector.wait_ge(bc, 4)
                vector.tensor_copy(p.sig_sb[:, 1:3], ps[1][:, 0:2])
                vector.drain()
                vector.nop().then_inc(bc, 1)
                for h in range(NHT):
                    vector.wait_ge(xld[h % 2], (h // 2 + 1) * 16)
                    vector.wait_ge(bc, 5)
                    xv = x1t[h % 2][:].rearrange("q c t -> q (c t)")
                    vector.tensor_scalar_mul(z[:], xv, p.sig_sb[:, 2:3])
                    vector.drain()
                    vector.tensor_copy(zi[:], z[:])
                    vector.drain()
                    vector.tensor_copy(zc[:], zi[:])
                    vector.drain()
                    vector.scalar_tensor_tensor(A[:], zc[:], -0.5, z[:],
                                                op0=OP.add, op1=OP.is_le)
                    vector.scalar_tensor_tensor(Bb[:], zc[:], 0.5, z[:],
                                                op0=OP.add, op1=OP.is_le)
                    vector.drain()
                    vector.tensor_add(z[:], zc[:], A[:])
                    vector.drain()
                    vector.tensor_add(z[:], z[:], Bb[:])
                    vector.drain()
                    vector.tensor_scalar(zc[:], z[:], -1.0, p.sig_sb[:, 1:2],
                                         op0=OP.add, op1=OP.mult)
                    vector.drain()
                    ho = (h // 2) * TT + (h % 2) * (TT // 2)
                    dsth = xqh[:, :, ho:ho + TT // 2]
                    dstl = xql[:, :, ho:ho + TT // 2]
                    zc3 = zc[:].rearrange("q (c t) -> q c t", c=KC)
                    vector.tensor_copy(dsth, zc3)
                    vector.drain()
                    vector.scalar_tensor_tensor(dstl, zc3, 1.0, dsth,
                                                op0=OP.mult, op1=OP.subtract)
                    vector.drain()
                    vector.nop().then_inc(xq, 1)
                ng = 0
                for wsel in range(2):
                    mxcol = 8 + wsel
                    for mc in range(MC3):
                        for tg in range(8):
                            vector.wait_ge(mmg, ng + 1)
                            if ng >= 2:
                                vector.wait_ge(sdg[ng % 2], (ng // 2) * 16)
                            vector.tensor_copy(stg[ng % 2][:], ps[ng % 8][:])
                            vector.drain()
                            vector.reduce_max(tmx[:], stg[ng % 2][:], axis=AX,
                                              apply_absolute_value=True)
                            vector.drain()
                            vector.tensor_max(p.maxacc[:, mxcol:mxcol + 1],
                                              p.maxacc[:, mxcol:mxcol + 1],
                                              tmx[:])
                            vector.drain()
                            vector.nop().then_inc(evg, 1)
                            ng += 1

            @blk.sync
            def _(sync):
                for h in range(NHT):
                    if h >= 2:
                        sync.wait_ge(xq, h - 1)
                    base = h * (TT // 2)
                    sync.dma_start(
                        out=x1t[h % 2][:],
                        in_=p.x1[:, :, base:base + TT // 2].rearrange(
                            "c q t -> q c t")).then_inc(xld[h % 2], 16)
                ng = 0
                for wsel in range(2):
                    hi_s, lo_s = ((p.wupT_hi, p.wupT_lo),
                                  (p.wgT_hi, p.wgT_lo))[wsel]
                    dstscr = (p.upraw, p.gateraw)[wsel]
                    for mc in range(MC3):
                        n = wsel * MC3 + mc
                        if n >= 2:
                            sync.wait_ge(mmg, (n - 1) * 8)
                        for wp, srcw in ((0, hi_s), (1, lo_s)):
                            sync.dma_start(
                                out=wch[n % 2][wp][:],
                                in_=srcw[:, :, mc * 128:(mc + 1) * 128].rearrange(
                                    "k q m -> q k m")).then_inc(wld[n % 2], 16)
                        for tg in range(8):
                            sync.wait_ge(evg, ng + 1)
                            sync.dma_start(
                                out=dstscr[mc, :, tg * TT:(tg + 1) * TT],
                                in_=stg[ng % 2][:]).then_inc(sdg[ng % 2], 16)
                            ng += 1
                sync.wait_ge(sdg[0], 2 * MC3 * 8 // 2 * 16)
                sync.wait_ge(sdg[1], 2 * MC3 * 8 // 2 * 16)
        nc.clear_and_free_semaphores(slist)

def _cc_parts(p, nfold, dst2=None, prescale=1.0):
    K = 3 if dst2 is not None else 2
    def gp_fn(g, cc, ccc, bc):
        g.wait_ge(bc, 2)
        g.dma_start(out=p.cc_in.ap(), in_=p.ccrow[:]).then_inc(cc, 16)
        g.wait_ge(cc, 16)
        g.collective_compute(
            "AllReduce", OP.max,
            replica_groups=[CC_GROUP or list(range(NCORES))],
            ins=[p.cc_in.ap()], outs=[p.cc_out.ap()]).then_inc(ccc, 1)
        g.wait_ge(ccc, 1)
        g.dma_start(out=p.ccrow[:], in_=p.cc_out.ap()).then_inc(cc, 16)
        g.wait_ge(cc, 32)
        g.nop().then_inc(ccc, 1)

    def tensor_fn(tensor, psum, mcols, bc):
        tensor.transpose(psum[0:nfold, 0:128],
                         p.maxacc[:, mcols[0]:mcols[0] + nfold],
                         p.ident_sb[:]).then_inc(bc, 1)
        tensor.wait_ge(bc, 3)
        tensor.matmul(psum[:, 0:K * nfold], p.ones_sb[:],
                      p.row_sb[:, 4:4 + K * nfold],
                      start=True, stop=True).then_inc(bc, 1)

    def vector_fn(vector, psum, dst, cc, ccc, bc):
        vector.wait_ge(bc, 1)
        vector.tensor_copy(p.ccrow[0:nfold, 0:128], psum[0:nfold, 0:128])
        vector.drain()
        if nfold == 1:
            vector.memset(p.ccrow[1:2, 0:128], 0.0) if False else None
        vector.nop().then_inc(bc, 1)
        vector.wait_ge(ccc, 2)
        vector.reduce_max(p.row_sb[0:nfold, 0:1], p.ccrow[0:nfold, 0:128],
                          axis=AX)
        vector.drain()
        vector.tensor_scalar(p.row_sb[0:nfold, 0:1], p.row_sb[0:nfold, 0:1],
                             prescale / QMAX, 1e-12, op0=OP.mult, op1=OP.max)
        vector.drain()
        vector.reciprocal(p.row_sb[0:nfold, 1:2], p.row_sb[0:nfold, 0:1])
        vector.drain()
        if prescale != 1.0:
            vector.tensor_scalar_mul(p.row_sb[0:nfold, 1:2],
                                     p.row_sb[0:nfold, 1:2], prescale)
            vector.drain()
        if dst2 is not None:
            vector.tensor_scalar_mul(p.row_sb[0:nfold, 2:3],
                                     p.row_sb[0:nfold, 0:1], -32769.0)
            vector.drain()
        vector.tensor_tensor(
            p.row_sb[0:nfold, 4:4 + nfold],
            p.row_sb[0:nfold, 0:1].broadcast_to([nfold, nfold]),
            p.ident_sb[0:nfold, 0:nfold], op=OP.mult)
        vector.tensor_tensor(
            p.row_sb[0:nfold, 4 + nfold:4 + 2 * nfold],
            p.row_sb[0:nfold, 1:2].broadcast_to([nfold, nfold]),
            p.ident_sb[0:nfold, 0:nfold], op=OP.mult)
        if dst2 is not None:
            vector.tensor_tensor(
                p.row_sb[0:nfold, 4 + 2 * nfold:4 + 3 * nfold],
                p.row_sb[0:nfold, 2:3].broadcast_to([nfold, nfold]),
                p.ident_sb[0:nfold, 0:nfold], op=OP.mult)
        vector.drain()
        vector.nop().then_inc(bc, 1)
        vector.wait_ge(bc, 4)
        vector.tensor_copy(p.sig_sb[:, dst:dst + 2 * nfold],
                           psum[:, 0:2 * nfold])
        if dst2 is not None:
            vector.tensor_copy(p.sig_sb[:, dst2:dst2 + nfold],
                               psum[:, 2 * nfold:3 * nfold])
        vector.drain()
        vector.nop().then_inc(bc, 1)

    return gp_fn, tensor_fn, vector_fn


def _ladder(vector, z, zi, zc, A, Bb, n):
    vector.tensor_copy(zi[:, 0:n], z[:, 0:n])
    vector.drain()
    vector.tensor_copy(zc[:, 0:n], zi[:, 0:n])
    vector.drain()
    vector.scalar_tensor_tensor(A[:, 0:n], zc[:, 0:n], -0.5, z[:, 0:n],
                                op0=OP.add, op1=OP.is_le)
    vector.scalar_tensor_tensor(Bb[:, 0:n], zc[:, 0:n], 0.5, z[:, 0:n],
                                op0=OP.add, op1=OP.is_le)
    vector.drain()
    vector.tensor_add(z[:, 0:n], zc[:, 0:n], A[:, 0:n])
    vector.drain()
    vector.tensor_add(z[:, 0:n], z[:, 0:n], Bb[:, 0:n])
    vector.drain()


def emit_p5(p):
    nc = p.nc
    HN = 4 * MC3
    HC = T // 4
    gp_fn, t_fn, v_fn = _cc_parts(p, 2, dst2=11)
    with ExitStack() as pc:
        sb = lambda nm, sh, d=F32: pc.enter_context(nc.sbuf_tensor(nm, sh, d))
        upb = [sb(f"p5_up{i}", [128, HC]) for i in range(2)]
        gtb = [sb(f"p5_gt{i}", [128, HC]) for i in range(2)]
        yub = [sb(f"p5_yu{i}", [128, HC]) for i in range(2)]
        ygb = [sb(f"p5_yg{i}", [128, HC]) for i in range(2)]
        ziu = [sb(f"p5_ziu{i}", [128, HC], I32) for i in range(2)]
        zig = [sb(f"p5_zig{i}", [128, HC], I32) for i in range(2)]
        zcu = [sb(f"p5_zcu{i}", [128, HC]) for i in range(2)]
        Au = [sb(f"p5_Au{i}", [128, HC]) for i in range(2)]
        zcg = [sb(f"p5_zcg{i}", [128, HC]) for i in range(2)]
        Ag = [sb(f"p5_Ag{i}", [128, HC]) for i in range(2)]
        lup = [sb(f"p5_lup{i}", [128, HC]) for i in range(2)]
        lgp = [sb(f"p5_lgp{i}", [128, HC]) for i in range(2)]
        gqb = [sb(f"p5_gq{i}", [128, HC]) for i in range(4)]
        sgb = sb("p5_sg", [128, HC])
        yfb = [sb(f"p5_yf{i}", [128, HC]) for i in range(3)]
        zif = [sb(f"p5_zif{i}", [128, HC], I32) for i in range(3)]
        zcf = [sb(f"p5_zcf{i}", [128, HC]) for i in range(3)]
        Af = [sb(f"p5_Af{i}", [128, HC]) for i in range(3)]
        lfp = [sb(f"p5_lfp{i}", [128, HC]) for i in range(3)]
        t1b = sb("p5_t1", [128, HC])
        mb = [sb(f"p5_m{i}", [128, HC]) for i in range(2)]
        tmx = sb("p5_tmx", [128, 1])
        ps = pc.enter_context(nc.psum_tensor("p5_ps", [128, TT], F32))
        slist = []
        def sem(nm):
            h = nc.alloc_semaphore(name=nm)
            slist.append(h)
            return h
        cc = sem("p5_cc"); ccc = sem("p5_ccc"); bc = sem("p5_bc")
        ld = [sem(f"p5_ld{i}") for i in range(2)]
        aq1 = sem("p5_aq1"); aq2 = sem("p5_aq2")
        vqa = sem("p5_vqa")
        pq1 = sem("p5_pq1"); pq2 = sem("p5_pq2")
        su = [sem(f"p5_su{i}") for i in range(2)]
        sm = [sem(f"p5_sm{i}") for i in range(2)]
        def addr(h):
            return h // 4, (h % 4) * HC
        with nc.Block() as blk:
            @blk.tensor
            def _(tensor):
                t_fn(tensor, ps, [8, 9], bc)

            @blk.scalar
            def _(scalar):
                scalar.wait_ge(bc, 5)
                for r in range(HN + 2):
                    if r < HN:
                        scalar.wait_ge(ld[r % 2], (r // 2 + 1) * 32)
                        if r >= 2:
                            scalar.wait_ge(vqa, r - 1)
                        scalar.activation(yub[r % 2][:], upb[r % 2][:],
                                          AF.Identity, bias=p.b32k[:, 0:1],
                                          scale=p.sig_sb[:, 5:6])
                        scalar.activation(ziu[r % 2][:], upb[r % 2][:],
                                          AF.Identity, bias=p.b32k[:, 0:1],
                                          scale=p.sig_sb[:, 5:6])
                        scalar.activation(ygb[r % 2][:], gtb[r % 2][:],
                                          AF.Identity, bias=p.b32k[:, 0:1],
                                          scale=p.sig_sb[:, 6:7])
                        scalar.activation(zig[r % 2][:], gtb[r % 2][:],
                                          AF.Identity, bias=p.b32k[:, 0:1],
                                          scale=p.sig_sb[:, 6:7]).then_inc(aq1, 1)
                    k = r - 2
                    if 0 <= k < HN:
                        scalar.wait_ge(pq1, r - 1)
                        if r >= 4:
                            scalar.wait_ge(vqa, r)
                        scalar.activation(sgb[:], lgp[k % 2][:], AF.Sigmoid,
                                          bias=p.sig_sb[:, 12:13],
                                          scale=p.sig_sb[:, 4:5])
                        scalar.activation(gqb[k % 4][:], lgp[k % 2][:],
                                          AF.Identity, bias=p.sig_sb[:, 12:13],
                                          scale=p.sig_sb[:, 4:5])
                        scalar.drain()
                        scalar.activation(yfb[k % 3][:], sgb[:], AF.Identity,
                                          bias=p.b32k[:, 0:1], scale=65536.0)
                        scalar.activation(zif[k % 3][:], sgb[:], AF.Identity,
                                          bias=p.b32k[:, 0:1],
                                          scale=65536.0).then_inc(aq2, 1)
                        scalar.drain()

            @blk.vector
            def _(vector):
                v_fn(vector, ps, 3, cc, ccc, bc)
                for r in range(HN + 5):
                    g1 = r < HN
                    k2 = r - 3
                    g2 = 0 <= k2 < HN
                    k3 = r - 5
                    g3 = 0 <= k3 < HN
                    if g1:
                        vector.wait_ge(aq1, r + 1)
                        if r >= 2:
                            vector.wait_ge(pq1, r - 1)
                    if g2:
                        vector.wait_ge(aq2, r - 2)
                        if k2 >= 3:
                            vector.wait_ge(pq2, k2 - 2)
                    if g3:
                        vector.wait_ge(pq2, k3 + 1)
                        if k3 >= 2:
                            vector.wait_ge(sm[k3 % 2], (k3 // 2) * 16)
                    if g1:
                        vector.tensor_copy(zcu[r % 2][:], ziu[r % 2][:])
                        vector.tensor_copy(zcg[r % 2][:], zig[r % 2][:])
                    if g2:
                        vector.tensor_copy(zcf[k2 % 3][:], zif[k2 % 3][:])
                    if g3:
                        vector.tensor_scalar_min(t1b[:], lfp[k3 % 3][:],
                                                 98304.0)
                    vector.drain()
                    ins = None
                    if g1:
                        vector.scalar_tensor_tensor(
                            Au[r % 2][:], zcu[r % 2][:], 0.0, yub[r % 2][:],
                            op0=OP.add, op1=OP.is_le)
                        ins = vector.scalar_tensor_tensor(
                            Ag[r % 2][:], zcg[r % 2][:], 0.0, ygb[r % 2][:],
                            op0=OP.add, op1=OP.is_le)
                    if g2:
                        ins = vector.scalar_tensor_tensor(
                            Af[k2 % 3][:], zcf[k2 % 3][:], 0.0, yfb[k2 % 3][:],
                            op0=OP.add, op1=OP.is_le)
                    if g3:
                        ins = vector.scalar_tensor_tensor(
                            mb[k3 % 2][:], t1b[:], -32769.0, gqb[k3 % 4][:],
                            op0=OP.add, op1=OP.mult)
                    ins.then_inc(vqa, 1)
                    vector.drain()
                    if g3:
                        vector.reduce_max(tmx[:], mb[k3 % 2][:], axis=AX,
                                          apply_absolute_value=True)
                        vector.drain()
                        vector.tensor_max(p.maxacc[:, 10:11],
                                          p.maxacc[:, 10:11], tmx[:])
                        vector.drain()

            @blk.gpsimd
            def _(g):
                gp_fn(g, cc, ccc, bc)
                for r in range(1, HN + 5):
                    k = r - 1
                    if 0 <= k < HN:
                        g.wait_ge(vqa, r)
                        if k >= 2:
                            g.wait_ge(su[k % 2], (k // 2) * 16)
                            g.wait_ge(aq2, k - 1)
                        g.tensor_tensor(lup[k % 2][:], zcu[k % 2][:],
                                        Au[k % 2][:], op=OP.add)
                        g.tensor_tensor(lgp[k % 2][:], zcg[k % 2][:],
                                        Ag[k % 2][:], op=OP.add).then_inc(pq1, 1)
                    k = r - 4
                    if 0 <= k < HN:
                        g.wait_ge(vqa, r)
                        g.tensor_tensor(lfp[k % 3][:], zcf[k % 3][:],
                                        Af[k % 3][:], op=OP.add).then_inc(pq2, 1)

            @blk.sync
            def _(sync):
                for r in range(HN + 7):
                    if r < HN:
                        if r >= 2:
                            sync.wait_ge(aq1, r - 1)
                        mc, co = addr(r)
                        sync.dma_start(out=upb[r % 2][:],
                                       in_=p.upraw[mc, :, co:co + HC]
                                       ).then_inc(ld[r % 2], 16)
                        sync.dma_start(out=gtb[r % 2][:],
                                       in_=p.gateraw[mc, :, co:co + HC]
                                       ).then_inc(ld[r % 2], 16)
                    k = r - 2
                    if 0 <= k < HN:
                        sync.wait_ge(pq1, k + 1)
                        if k >= 2:
                            sync.wait_ge(su[k % 2], (k // 2) * 16)
                        mc, co = addr(k)
                        sync.dma_start(out=p.upq[mc, :, co:co + HC],
                                       in_=lup[k % 2][:]).then_inc(su[k % 2], 16)
                    k = r - 6
                    if 0 <= k < HN:
                        sync.wait_ge(vqa, k + 6)
                        if k >= 2:
                            sync.wait_ge(sm[k % 2], (k // 2) * 16)
                        mc, co = addr(k)
                        sync.dma_start(out=p.g2raw[mc, :, co:co + HC],
                                       in_=mb[k % 2][:]).then_inc(sm[k % 2], 16)
                sync.wait_ge(su[0], HN // 2 * 16)
                sync.wait_ge(su[1], HN // 2 * 16)
                sync.wait_ge(sm[0], HN // 2 * 16)
                sync.wait_ge(sm[1], HN // 2 * 16)
        nc.clear_and_free_semaphores(slist)


def emit_p6(p):
    nc = p.nc
    HN = 4 * MC3
    HC = T // 4
    gp_fn, t_fn, v_fn = _cc_parts(p, 1, dst2=13, prescale=2.0 ** -16)
    with ExitStack() as pc:
        sb = lambda nm, sh, d=F32: pc.enter_context(nc.sbuf_tensor(nm, sh, d))
        g2b = [sb(f"p6_g2{i}", [128, HC]) for i in range(2)]
        uqb = [sb(f"p6_uq{i}", [128, HC]) for i in range(3)]
        y2b = [sb(f"p6_y2{i}", [128, HC]) for i in range(2)]
        zi2 = [sb(f"p6_zi{i}", [128, HC], I32) for i in range(2)]
        upd = [sb(f"p6_upd{i}", [128, HC]) for i in range(2)]
        zc2 = [sb(f"p6_zc{i}", [128, HC]) for i in range(2)]
        A2 = [sb(f"p6_A{i}", [128, HC]) for i in range(2)]
        l2p = [sb(f"p6_l2{i}", [128, HC]) for i in range(2)]
        g2q = [sb(f"p6_g2q{i}", [128, HC]) for i in range(2)]
        ob = [sb(f"p6_o{i}", [128, HC]) for i in range(3)]
        tmx = sb("p6_tmx", [128, 1])
        ps = pc.enter_context(nc.psum_tensor("p6_ps", [128, TT], F32))
        slist = []
        def sem(nm):
            h = nc.alloc_semaphore(name=nm)
            slist.append(h)
            return h
        cc = sem("p6_cc"); ccc = sem("p6_ccc"); bc = sem("p6_bc")
        ld = [sem(f"p6_ld{i}") for i in range(2)]
        aq1 = sem("p6_aq1"); aq2 = sem("p6_aq2")
        vqa = sem("p6_vqa")
        pq1 = sem("p6_pq1"); pq2 = sem("p6_pq2")
        so = [sem(f"p6_so{i}") for i in range(2)]
        def addr(h):
            return h // 4, (h % 4) * HC
        with nc.Block() as blk:
            @blk.tensor
            def _(tensor):
                t_fn(tensor, ps, [10], bc)

            @blk.scalar
            def _(scalar):
                scalar.wait_ge(bc, 5)
                for r in range(HN + 2):
                    if r < HN:
                        scalar.wait_ge(ld[r % 2], (r // 2 + 1) * 32)
                        if r >= 2:
                            scalar.wait_ge(vqa, r - 1)
                        scalar.activation(y2b[r % 2][:], g2b[r % 2][:],
                                          AF.Identity, bias=p.b32k[:, 0:1],
                                          scale=p.sig_sb[:, 8:9])
                        scalar.activation(zi2[r % 2][:], g2b[r % 2][:],
                                          AF.Identity, bias=p.b32k[:, 0:1],
                                          scale=p.sig_sb[:, 8:9]).then_inc(aq1, 1)
                    k = r - 2
                    if 0 <= k < HN:
                        scalar.wait_ge(pq1, k + 1)
                        if k >= 2:
                            scalar.wait_ge(pq2, k - 1)
                        scalar.activation(g2q[k % 2][:], l2p[k % 2][:],
                                          AF.Identity, bias=p.sig_sb[:, 13:14],
                                          scale=p.sig_sb[:, 7:8])
                        scalar.activation(upd[k % 2][:], uqb[k % 3][:],
                                          AF.Identity, bias=p.sig_sb[:, 11:12],
                                          scale=p.sig_sb[:, 3:4]).then_inc(aq2, 1)

            @blk.vector
            def _(vector):
                v_fn(vector, ps, 7, cc, ccc, bc)
                for r in range(HN + 5):
                    g1 = r < HN
                    k3 = r - 5
                    g3 = 0 <= k3 < HN
                    if g1:
                        vector.wait_ge(aq1, r + 1)
                        if r >= 2:
                            vector.wait_ge(pq1, r - 1)
                    if g3:
                        vector.wait_ge(pq2, k3 + 1)
                    if g1:
                        vector.tensor_copy(zc2[r % 2][:], zi2[r % 2][:])
                    if g3:
                        vector.reduce_max(tmx[:], ob[k3 % 3][:], axis=AX,
                                          apply_absolute_value=True)
                    vector.drain()
                    ins = None
                    if g1:
                        ins = vector.scalar_tensor_tensor(
                            A2[r % 2][:], zc2[r % 2][:], 0.0, y2b[r % 2][:],
                            op0=OP.add, op1=OP.is_le)
                    if g3:
                        ins = vector.tensor_max(p.maxacc[:, 11:12],
                                                p.maxacc[:, 11:12], tmx[:])
                    ins.then_inc(vqa, 1)
                    vector.drain()

            @blk.gpsimd
            def _(g):
                gp_fn(g, cc, ccc, bc)
                for r in range(1, HN + 4):
                    k = r - 1
                    if 0 <= k < HN:
                        g.wait_ge(vqa, r)
                        if k >= 2:
                            g.wait_ge(aq2, k - 1)
                        g.tensor_tensor(l2p[k % 2][:], zc2[k % 2][:],
                                        A2[k % 2][:], op=OP.add).then_inc(pq1, 1)
                    k = r - 3
                    if 0 <= k < HN:
                        g.wait_ge(aq2, k + 1)
                        if k >= 3:
                            g.wait_ge(vqa, k + 3)
                            g.wait_ge(so[(k - 3) % 2], ((k - 3) // 2 + 1) * 16)
                        g.tensor_tensor(ob[k % 3][:], g2q[k % 2][:],
                                        upd[k % 2][:], op=OP.mult).then_inc(pq2, 1)

            @blk.sync
            def _(sync):
                for r in range(HN + 5):
                    if r < HN:
                        if r >= 2:
                            sync.wait_ge(aq1, r - 1)
                        if r >= 3:
                            sync.wait_ge(aq2, r - 2)
                        mc, co = addr(r)
                        sync.dma_start(out=g2b[r % 2][:],
                                       in_=p.g2raw[mc, :, co:co + HC]
                                       ).then_inc(ld[r % 2], 16)
                        sync.dma_start(out=uqb[r % 3][:],
                                       in_=p.upq[mc, :, co:co + HC]
                                       ).then_inc(ld[r % 2], 16)
                    k = r - 4
                    if 0 <= k < HN:
                        sync.wait_ge(pq2, k + 1)
                        if k >= 2:
                            sync.wait_ge(so[k % 2], (k // 2) * 16)
                        mc, co = addr(k)
                        sync.dma_start(out=p.oraw[mc, :, co:co + HC],
                                       in_=ob[k % 3][:]).then_inc(so[k % 2], 16)
                sync.wait_ge(so[0], HN // 2 * 16)
                sync.wait_ge(so[1], HN // 2 * 16)
        nc.clear_and_free_semaphores(slist)

def emit_p7(p):
    nc = p.nc
    gp_fn, t_fn, v_fn = _cc_parts(p, 1)
    T7 = 256
    NT7 = T // T7
    NQG = 6
    QW = 4 * T7
    with ExitStack() as pc:
        sb = lambda nm, sh, d=F32: pc.enter_context(nc.sbuf_tensor(nm, sh, d))
        orb = [sb(f"p7_or{i}", [128, MC3, T7]) for i in range(2)]
        oph = [sb(f"p7_oph{i}", [128, MC3, T7], BF16) for i in range(2)]
        opl = [sb(f"p7_opl{i}", [128, MC3, T7], BF16) for i in range(2)]
        z = sb("p7_z", [128, QW]); zi = sb("p7_zi", [128, QW], I32)
        zc = sb("p7_zc", [128, QW]); A = sb("p7_A", [128, QW])
        Bb = sb("p7_B", [128, QW])
        wdc = [[sb(f"p7_wd{i}_{wp}", [128, MC3, 128], BF16) for wp in range(2)]
               for i in range(2)]
        x1c = sb("p7_x1", [128, KC, T7])
        xr2 = sb("p7_xr2", [128, KC, T7])
        sq2 = sb("p7_sq2", [128, KC, T7])
        outst = [sb(f"p7_out{i}", [128, 2, DM]) for i in range(2)]
        ps = [pc.enter_context(nc.psum_tensor(f"p7_ps{i}", [128, T7], F32))
              for i in range(6)]
        m_sb = z[:, 0:T7]
        msq = z[:, T7:2 * T7]
        v_sb = zc[:, 0:T7]
        sd_sb = zc[:, T7:2 * T7]
        slist = []
        def sem(nm):
            h = nc.alloc_semaphore(name=nm)
            slist.append(h)
            return h
        cc = sem("p7_cc"); ccc = sem("p7_ccc"); bc = sem("p7_bc")
        old = [sem(f"p7_old{i}") for i in range(2)]
        x1l = sem("p7_x1l")
        wls = [sem(f"p7_wl{i}") for i in range(2)]
        qg = sem("p7_qg")
        mmg = sem("p7_mmg")
        evg = sem("p7_evg")
        LT1 = sem("p7_LT1"); LT2 = sem("p7_LT2"); LSq = sem("p7_LSq")
        LNv = sem("p7_LNv"); LNs = sem("p7_LNs"); LNx = sem("p7_LNx")
        otr = sem("p7_otr"); oev = sem("p7_oev")
        osd = [sem(f"p7_osd{i}") for i in range(2)]
        with nc.Block() as blk:
            @blk.gpsimd
            def _(g):
                gp_fn(g, cc, ccc, bc)

            @blk.tensor
            def _(tensor):
                t_fn(tensor, ps[2], [11], bc)
                ntr = 0
                for i in range(NT7):
                    tensor.wait_ge(qg, (i + 1) * NQG)
                    for mc in range(KC):
                        n = i * KC + mc
                        tensor.wait_ge(wls[n % 2], (n // 2 + 1) * 32)
                        if n >= 2:
                            tensor.wait_ge(evg, n - 1)
                        k3 = 0
                        for (xs, wp) in ((oph, 0), (opl, 0), (oph, 1)):
                            for kc in range(MC3):
                                ins = tensor.matmul(
                                    ps[n % 2][:],
                                    wdc[n % 2][wp][:, kc, :],
                                    xs[i % 2][:, kc, :],
                                    start=(k3 == 0), stop=(k3 == 3 * MC3 - 1))
                                k3 += 1
                        ins.then_inc(mmg, 1)
                    tensor.wait_ge(evg, (i + 1) * KC)
                    for kc in range(KC):
                        ins = tensor.matmul(ps[2][:], p.ones_sb[:],
                                            xr2[:, kc, :],
                                            start=(kc == 0), stop=(kc == KC - 1))
                    ins.then_inc(LT1, 1)
                    tensor.wait_ge(LSq, i + 1)
                    for kc in range(KC):
                        ins = tensor.matmul(ps[3][:], p.ones_sb[:],
                                            sq2[:, kc, :],
                                            start=(kc == 0), stop=(kc == KC - 1))
                    ins.then_inc(LT2, 1)
                    tensor.wait_ge(LNx, i + 1)
                    for s4 in range(2):
                        for kc in range(KC):
                            if ntr >= 2:
                                tensor.wait_ge(oev, ntr - 1)
                            tensor.transpose(
                                ps[4 + ntr % 2][0:128, 0:128],
                                xr2[:, kc, s4 * 128:(s4 + 1) * 128],
                                p.ident_sb[:]).then_inc(otr, 1)
                            ntr += 1

            @blk.scalar
            def _(scalar):
                for i in range(NT7):
                    scalar.wait_ge(evg, (i + 1) * KC)
                    if i >= 1:
                        scalar.wait_ge(LT2, i)
                        scalar.wait_ge(osd[(i - 1) % 2], ((i - 1) // 2 + 1) * 32)
                    scalar.activation(sq2[:], xr2[:], AF.Square).then_inc(LSq, 1)
                    scalar.drain()
                    scalar.wait_ge(LNv, i + 1)
                    scalar.activation(sd_sb, v_sb, AF.Sqrt, bias=p.eps_sb[:]).then_inc(LNs, 1)
                    scalar.drain()

            @blk.vector
            def _(vector):
                v_fn(vector, ps[2], 9, cc, ccc, bc)
                ntr = 0
                for i in range(NT7):
                    vector.wait_ge(old[i % 2], (i // 2 + 1) * 16)
                    for qv in range(NQG):
                        src = orb[i % 2][:, 4 * qv:4 * qv + 4, :].rearrange(
                            "q c t -> q (c t)")
                        vector.tensor_scalar_mul(z[:], src, p.sig_sb[:, 10:11])
                        vector.drain()
                        _ladder(vector, z, zi, zc, A, Bb, QW)
                        vector.tensor_scalar(zc[:], z[:], -1.0,
                                             p.sig_sb[:, 9:10],
                                             op0=OP.add, op1=OP.mult)
                        vector.drain()
                        zc3 = zc[:].rearrange("q (c t) -> q c t", c=4)
                        dh = oph[i % 2][:, 4 * qv:4 * qv + 4, :]
                        dl = opl[i % 2][:, 4 * qv:4 * qv + 4, :]
                        vector.tensor_copy(dh, zc3)
                        vector.drain()
                        vector.scalar_tensor_tensor(dl, zc3, 1.0, dh,
                                                    op0=OP.mult, op1=OP.subtract)
                        vector.drain()
                        vector.nop().then_inc(qg, 1)
                    vector.wait_ge(x1l, (i + 1) * 16)
                    for mc in range(KC):
                        n = i * KC + mc
                        vector.wait_ge(mmg, n + 1)
                        vector.tensor_add(xr2[:, mc, :], ps[n % 2][:],
                                          x1c[:, mc, :])
                        vector.drain()
                        vector.nop().then_inc(evg, 1)
                    vector.wait_ge(LT1, i + 1)
                    vector.tensor_scalar_mul(m_sb, ps[2][:], 1.0 / DM)
                    vector.drain()
                    vector.tensor_mul(msq, m_sb, m_sb)
                    vector.drain()
                    vector.wait_ge(LT2, i + 1)
                    vector.scalar_tensor_tensor(v_sb, ps[3][:], 1.0 / DM, msq,
                                                op0=OP.mult, op1=OP.subtract)
                    vector.drain()
                    vector.nop().then_inc(LNv, 1)
                    vector.wait_ge(LNs, i + 1)
                    vector.reciprocal(sd_sb, sd_sb)
                    vector.drain()
                    vector.tensor_tensor(
                        xr2[:], xr2[:],
                        m_sb.rearrange("q t -> q () t").broadcast_to([128, KC, T7]),
                        op=OP.subtract)
                    vector.drain()
                    vector.tensor_tensor(
                        xr2[:], xr2[:],
                        sd_sb.rearrange("q t -> q () t").broadcast_to([128, KC, T7]),
                        op=OP.mult)
                    vector.drain()
                    vector.nop().then_inc(LNx, 1)
                    if i >= 2:
                        vector.wait_ge(osd[i % 2], (i // 2) * 32)
                    for s4 in range(2):
                        for kc in range(KC):
                            vector.wait_ge(otr, ntr + 1)
                            vector.tensor_copy(
                                outst[i % 2][:, s4, kc * 128:(kc + 1) * 128],
                                ps[4 + ntr % 2][0:128, 0:128])
                            vector.drain()
                            vector.nop().then_inc(oev, 1)
                            ntr += 1

            @blk.sync
            def _(sync):
                for i in range(NT7):
                    if i >= 2:
                        sync.wait_ge(qg, (i - 1) * NQG)
                    sync.dma_start(
                        out=orb[i % 2][:],
                        in_=p.oraw[:, :, i * T7:(i + 1) * T7].rearrange(
                            "c q t -> q c t")).then_inc(old[i % 2], 16)
                    sync.wait_ge(evg, i * KC)
                    sync.dma_start(
                        out=x1c[:],
                        in_=p.x1[:, :, i * T7:(i + 1) * T7].rearrange(
                            "c q t -> q c t")).then_inc(x1l, 16)
                    for mc in range(KC):
                        n = i * KC + mc
                        if n >= 2:
                            sync.wait_ge(mmg, n - 1)
                        for wp, srcw in ((0, p.wdT_hi), (1, p.wdT_lo)):
                            sync.dma_start(
                                out=wdc[n % 2][wp][:],
                                in_=srcw[:, :, mc * 128:(mc + 1) * 128].rearrange(
                                    "k q m -> q k m")).then_inc(wls[n % 2], 16)
                    if i >= 1:
                        for s4 in range(2):
                            sync.wait_ge(oev, (i - 1) * 16 + (s4 + 1) * KC)
                            r0 = (i - 1) * T7 + s4 * 128
                            sync.dma_start(
                                out=p.out[r0:r0 + 128, :],
                                in_=outst[(i - 1) % 2][:, s4, :]
                                ).then_inc(osd[(i - 1) % 2], 16)
                for s4 in range(2):
                    sync.wait_ge(oev, (NT7 - 1) * 16 + (s4 + 1) * KC)
                    r0 = (NT7 - 1) * T7 + s4 * 128
                    sync.dma_start(out=p.out[r0:r0 + 128, :],
                                   in_=outst[(NT7 - 1) % 2][:, s4, :]
                                   ).then_inc(osd[(NT7 - 1) % 2], 16)
                sync.wait_ge(osd[0], NT7 // 2 * 32)
                sync.wait_ge(osd[1], NT7 // 2 * 32)
        nc.clear_and_free_semaphores(slist)

def emit_debug(p):
    nc = p.nc
    if not p.debug_taps:
        return
    srcs = dict(xf=p.xf, qf=p.qf, kf=p.kf, af=p.af, vt=p.vt, x1=p.x1,
                upraw=p.upraw, gateraw=p.gateraw, upq=p.upq, g2raw=p.g2raw,
                oraw=p.oraw)
    with ExitStack() as pc:
        st = nc.alloc_semaphore(name="dbg_st")
        with nc.Block() as blk:
            @blk.sync
            def _(sync):
                n = 0
                for name, ap in p.debug_taps.items():
                    sync.dma_start(out=ap, in_=srcs[name]).then_inc(st, 16)
                    n += 1
                sync.wait_ge(st, n * 16)


def make_consts():
    ident = np.eye(128, dtype=np.float32)
    ones = np.ones((128, 128), dtype=np.float32)
    cmask = np.zeros((128, 64), dtype=np.float32)
    for u in range(2):
        for t in range(64):
            cmask[u * 64 + t, t + 1:] = NEG
    tmask = np.zeros((128, 64), dtype=np.float32)
    for pr in range(128):
        sub_s, s_ = (pr % 64) // 32, pr % 32
        for col in range(64):
            sub_t, tau = col // 32, col % 32
            if sub_s == sub_t and s_ < tau:
                tmask[pr, col] = 1.0
    return ident, ones, cmask, tmask


def make_in_maps(x, W):
    ident, ones, cmask, tmask = make_consts()
    shared = dict(W)
    shared.update(ident=ident, ones=ones, cmask=cmask, tmask=tmask)
    in_maps = []
    for c in range(NCORES):
        m = dict(shared)
        m["x"] = np.ascontiguousarray(
            np.asarray(x[c * B_PER:(c + 1) * B_PER], dtype=np.float32
                       ).reshape(T, DM))
        in_maps.append(m)
    return in_maps


def kernel(L0_attn_x, L0_attn_Wq, L0_attn_bq, L0_attn_Wk, L0_attn_bk,
           L0_attn_Wv, L0_attn_bv, L0_attn_Wg1, L0_attn_bg1, L0_attn_Wg2,
           L0_attn_bg2, L0_ffn_w_gate, L0_ffn_w_up, L0_ffn_w_down):
    W = dict(
        Wq=np.ascontiguousarray(L0_attn_Wq), Wk=np.ascontiguousarray(L0_attn_Wk),
        Wv=np.ascontiguousarray(L0_attn_Wv),
        bq=np.ascontiguousarray(L0_attn_bq), bk=np.ascontiguousarray(L0_attn_bk),
        bv=np.ascontiguousarray(L0_attn_bv),
        Wg1=np.ascontiguousarray(L0_attn_Wg1), bg1=np.ascontiguousarray(L0_attn_bg1),
        Wg2=np.ascontiguousarray(L0_attn_Wg2), bg2=np.ascontiguousarray(L0_attn_bg2),
        wgate=np.ascontiguousarray(L0_ffn_w_gate), wup=np.ascontiguousarray(L0_ffn_w_up),
        wdown=np.ascontiguousarray(L0_ffn_w_down))
    nc, p = build()
    in_maps = make_in_maps(np.asarray(L0_attn_x), W)
    res = run_bass_kernel_spmd(nc, in_maps, core_ids=list(range(NCORES)))
    outs = [np.asarray(res.results[c]["out"]).reshape(B_PER, L, DM)
            for c in range(NCORES)]
    return np.concatenate(outs, axis=0)
